# version 11
# baseline (speedup 1.0000x reference)
import numpy as np
import ml_dtypes
import concourse.bass as bass
import concourse.mybir as mybir
from concourse.bass_utils import run_bass_kernel_spmd

F32 = mybir.dt.float32
BF16 = mybir.dt.bfloat16
U8 = mybir.dt.uint8
AF = mybir.ActivationFunctionType
ALU = mybir.AluOpType

D, DC, NH, KVH, DH = 2048, 16, 16, 4, 128
S, Q, F, FC = 2048, 1024, 4096, 32
TT = 256
NTILE = S // TT
SCALE = float(DH) ** -0.5
NEG = -1.0e30
EPS_LN, EPS_RMS = 1.0e-5, 1.0e-8
QT_CHUNKS = (8, 16)


class Plan:
    ENGINES = ("sync", "gpsimd", "vector", "scalar", "tensor")

    def __init__(self, nc, es):
        self.nc, self.es = nc, es
        self.q = {e: [] for e in self.ENGINES}
        self.label = ""
        self.ttags = []

    def sem(self, name):
        h = self.es.enter_context(self.nc.semaphore(name))
        return {"h": h, "n": 0, "name": name}

    def op(self, eng, method, *args, wait=(), inc=(), **kw):
        wait = tuple(w for w in wait if w is not None)
        for sm, v in wait:
            assert v <= sm["n"], (f"wait {sm['name']}>={v} but planned {sm['n']} "
                                  f"({eng}.{method})")
        for sm, k in inc:
            sm["n"] += k
        if eng == "tensor":
            is32 = False
            try:
                is32 = (args[1].dtype == mybir.dt.float32 and
                        args[1].shape[0] > 1)
            except Exception:
                pass
            self.ttags.append((self.label, is32))
        vdrain = (eng == "vector" and method != "engine_nop")

        def fn(E, wait=wait, inc=tuple(inc), args=args, kw=kw, method=method,
               vdrain=vdrain):
            for sm, v in wait:
                E.wait_ge(sm["h"], v)
            ins = getattr(E, method)(*args, **kw)
            if vdrain:
                ins = E.drain()
            for sm, k in inc:
                ins.then_inc(sm["h"], k)
        self.q[eng].append(fn)

    def tok(self, sm):
        return (sm, sm["n"])


class Ring:
    def __init__(self, plan, name, nslots, queue):
        self.p, self.queue = plan, queue
        self.sems = [plan.sem(f"{name}{i}") for i in range(nslots)]
        self.nuse = [0] * nslots
        self.free_tok = [None] * nslots
        self.k = 0

    def load(self, out_ap, in_ap, extra_wait=(), **kw):
        s = self.k % len(self.sems)
        w = list(extra_wait)
        if self.free_tok[s] is not None:
            w.append(self.free_tok[s])
        self.p.op(self.queue, "dma_start", wait=w, inc=[(self.sems[s], 16)],
                  out=out_ap, in_=in_ap, **kw)
        self.nuse[s] += 1
        self.k += 1
        return (self.sems[s], 16 * self.nuse[s])

    def set_free(self, use_k, tok):
        self.free_tok[use_k % len(self.sems)] = tok


def build_program(stage="full", expose=False):
    nc = bass.Bass("TRN2", target_bir_lowering=False)
    from contextlib import ExitStack
    es = ExitStack()

    def din(name, shape, dt=F32):
        return nc.dram_tensor(name, shape, dt, kind="ExternalInput").ap()

    xfeat = din("xfeat", [D, S])
    xq = din("xq", [D, Q])
    Wqd, Wkd = din("Wq", [D, D]), din("Wk", [D, 512])
    Wvd, Wod = din("Wv", [D, 512]), din("Wo", [D, D])
    qgd, kgd = din("qg", [128, 1]), din("kg", [128, 1])
    wgr = din("wg", [2, D])
    W0aTd, W0bTd = din("W0aT", [D, F]), din("W0bT", [F, D])
    W1aTd, W1bTd = din("W1aT", [D, F]), din("W1bT", [F, D])
    b0afd, b1afd = din("b0af", [128, FC]), din("b1af", [128, FC])
    b0bfd, b1bfd = din("b0bf", [128, DC]), din("b1bf", [128, DC])
    cfd = din("cf", [128, 16])
    wmodd = din("wmod", [D, 6 * D])
    maskd = din("maskt", [16, 128, 512], BF16)
    idend = din("iden", [128, 128], BF16)

    ydev = nc.dram_tensor("y", [D, Q], F32, kind="ExternalOutput").ap()
    sck = "ExternalOutput" if expose else "Internal"
    modsc = nc.dram_tensor("modsc", [6 * D], F32, kind=sck).ap()
    avhs = nc.dram_tensor("avhs", [D, Q], BF16, kind="Internal").ap()
    avls = nc.dram_tensor("avls", [D, Q], BF16, kind="Internal").ap()
    avsc = nc.dram_tensor("avsc", [D, Q], F32, kind=sck).ap()
    x1sc = nc.dram_tensor("x1sc", [D, Q], F32, kind=sck).ap()
    o0sc = nc.dram_tensor("o0sc", [D, Q], F32, kind=sck).ap()
    wlob = nc.dram_tensor("wlob", [64, 128, 2048], BF16, kind="Internal").ap()
    dmp = {}
    if expose:
        for nm, shp in (("modT", [128, 96]), ("der", [128, 64]),
                        ("K", [128, 4 * S]), ("V", [128, 16 * 512]),
                        ("H", [128, DC * Q]), ("H2", [128, DC * Q])):
            dmp[nm] = nc.dram_tensor("dump_" + nm, shp, F32,
                                     kind="ExternalOutput").ap()

    O_ROWS = 4096
    O_BCAST = 16384
    BIG = 22528
    ARENA = BIG + 180224 + 8192
    XTRA = BIG + 180224
    arena = es.enter_context(nc.sbuf_tensor("arena", [128, ARENA], U8))

    def buf(off, n, dt=F32):
        sz = mybir.dt.size(dt)
        assert off % 4 == 0 and off + n * sz <= ARENA, (off, n)
        return arena[:, off:off + n * sz].bitcast(dt)

    ones = buf(0, 128)
    modT = buf(512, 96)
    der = buf(896, 64)
    siluc = buf(1152, 16)
    cfs = buf(1216, 16)
    qg, kg = buf(1280, 1), buf(1288, 1)
    b0a_s, b1a_s = buf(1296, FC), buf(1424, FC)
    b0b_s, b1b_s = buf(1552, DC), buf(1616, DC)
    wg0, wg1, wgd = buf(1680, 16), buf(1744, 16), buf(1808, 16)
    tmpa, tmpb = buf(1872, 16), buf(1936, 16)
    ones_bf = buf(2192, 64, BF16)
    iden_bf = buf(2320, 128, BF16)

    def row(o, n):
        assert o + n <= 3072
        return buf(O_ROWS + o * 4, n)[0:1, :]
    bc = buf(O_BCAST, 1536)

    Kbuf = buf(BIG + 0, 4 * S)

    def Khl(hl, kvc, c):
        base = BIG + (kvc * 4 + c // 4) * 2048 + hl * 1024 + (c % 4) * 256
        return buf(base, 128, BF16)
    Vhi = buf(BIG + 32768, 16 * 512, BF16)
    Vlo = buf(BIG + 49152, 16 * 512, BF16)

    def Hap(hl, c, qt):
        return buf(BIG + 65536 + c * 4096 + qt * 2048 + hl * 1024, 512, BF16)
    TR = BIG + 131072
    wmod_s = [buf(TR + 16384 + i * 2048, 512) for i in range(4)]
    xh = [buf(TR, DC * TT), buf(TR + 16384, DC * TT)]
    x2s = [buf(TR + 32768 + i * 1024, TT) for i in range(2)]
    wk_s = [buf(TR + 34816 + i * 2048, 512) for i in range(2)]
    wv_s = [buf(TR + 38912 + i * 2048, 512) for i in range(2)]
    k2buf = buf(TR + 43008, 1024)
    t1c = [bc[:, 512 + i * 256:768 + i * 256] for i in range(2)]
    masks = buf(TR, 16 * 512, BF16)
    wq_s = [buf(TR + 16384, DC * 128)]
    wq_hi = buf(XTRA, DC * 128, BF16)
    wq_lo = buf(XTRA + 4096, DC * 128, BF16)
    qh_hi = [buf(TR + 24576 + i * 4096, 1024, BF16) for i in range(2)]
    qh_lo = [buf(TR + 26624 + i * 4096, 1024, BF16) for i in range(2)]
    x2q = [buf(O_ROWS + i * 2048, 512) for i in range(2)]
    pt_s = [buf(TR + 34816 + i * 2048, 512) for i in range(2)]
    pt_hi = [buf(TR + 38912 + i * 1024, 512, BF16) for i in range(3)]
    pt_lo = [buf(TR + 41984 + i * 1024, 512, BF16) for i in range(3)]
    avn_s = [buf(TR + 45056, 512)]
    avn_hi = [buf(TR + 47104, 512, BF16)]
    avn_lo = [buf(TR + 48128, 512, BF16)]
    xq_s = [buf(TR + i * 2048, 512) for i in range(2)]
    x1st_s = [buf(TR + 4096 + i * 2048, 512) for i in range(2)]
    x2w_s = [buf(TR + 8192 + i * 2048, 512) for i in range(2)]
    avqh = buf(TR + 16384, 16 * 512, BF16)
    avql_ = buf(TR + 32768, 16 * 512, BF16)
    wo_s = [buf(BIG + i * 8192, DC * 128) for i in range(2)]
    wo_hi = [buf(BIG + 16384 + i * 8192, DC * 128, BF16) for i in range(2)]
    wo_lo = [buf(BIG + 20480 + i * 8192, DC * 128, BF16) for i in range(2)]

    def H2ap(hl, c, qt):
        return buf(BIG + 65536 + c * 4096 + qt * 2048 + hl * 1024, 512, BF16)
    x1rd3_s = [buf(BIG + 32768 + i * 2048, 512) for i in range(2)]
    hid_hi = buf(BIG + 0, FC * 512, BF16)
    hid_lo = buf(BIG + 32768, FC * 512, BF16)
    t1c2 = [bc[:, 1024:1536], bc[:, 1024:1536]]
    w0a_f = [buf(TR + 0, DC * 128), buf(XTRA, DC * 128)]
    w0a_hi = [buf(TR + 8192 + i * 8192, DC * 128, BF16) for i in range(2)]
    w0a_lo = [buf(TR + 12288 + i * 8192, DC * 128, BF16) for i in range(2)]
    x1rd_s = [buf(TR + i * 2048, 512) for i in range(2)]
    o0st_s = [buf(TR + 4096 + i * 2048, 512) for i in range(2)]
    o0rd_s = [buf(TR + 8192 + i * 2048, 512) for i in range(2)]
    w0b_hi = [buf(TR + 24576 + i * 4096, 16 * 128, BF16) for i in range(2)]
    w0b_lo = [buf(TR + 32768 + i * 4096, 16 * 128, BF16) for i in range(2)]
    wsg_f = buf(BIG + 36864, 16 * 128)
    wsg_hi = buf(BIG + 45056, 16 * 128, BF16)
    wsg_lo = buf(BIG + 49152, 16 * 128, BF16)
    wgd_hi = buf(2000, 16, BF16)
    wgd_lo = buf(2032, 16, BF16)
    nb0b = buf(2064, DC)
    nb1b = buf(2128, DC)

    ps = [es.enter_context(nc.psum_tensor(f"ps{i}", [128, 512], F32))
          for i in range(8)]

    P = Plan(nc, es)
    sT, sV, sC = P.sem("sT"), P.sem("sV"), P.sem("sC")
    sG = P.sem("sG")
    sst = P.sem("sst")
    s_sp = P.sem("ssp")
    r_wmod = Ring(P, "rwm", 4, "sync")
    r_xt = Ring(P, "rxt", 2, "gpsimd")
    r_wk = Ring(P, "rwk", 2, "sync")
    r_wv = Ring(P, "rwv", 2, "sync")
    r_wq = Ring(P, "rwq", 1, "sync")
    r_wqhi = Ring(P, "rwqhi", 1, "gpsimd")
    r_wo = Ring(P, "rwo", 2, "sync")
    r_wohi = Ring(P, "rwohi", 2, "gpsimd")
    avql = [P.sem(f"avq{i}") for i in range(16)]
    r_xq = Ring(P, "rxq", 2, "gpsimd")
    r_x1 = Ring(P, "rx1", 2, "gpsimd")
    r_w0a = Ring(P, "rw0a", 2, "sync")
    r_w0b = Ring(P, "rw0b", 2, "sync")
    r_wahi = Ring(P, "rwahi", 2, "gpsimd")
    r_wbhi = Ring(P, "rwbhi", 2, "gpsimd")
    r_o0 = Ring(P, "ro0", 2, "gpsimd")
    st_avn = [P.sem("stav0"), P.sem("stav1")]
    st_av32 = P.sem("stav32")
    st_x1 = [P.sem("stx10"), P.sem("stx11")]
    st_o0 = [P.sem("sto00"), P.sem("sto01")]
    ws_st = P.sem("wsst")
    r_wblo = Ring(P, "rwblo", 2, "sync")

    def mm(out, lhsT, rhs, start, stop, wait=(), inc=()):
        P.op("tensor", "matmul", out, lhsT, rhs, wait=wait, inc=inc,
             start=start, stop=stop)

    ones_col = ones[:, 0:1]
    ones_row = ones[0:1, :]

    for dst, src in ((cfs, cfd), (qg, qgd), (kg, kgd), (iden_bf, idend),
                     (b0a_s, b0afd),
                     (b1a_s, b1afd), (b0b_s, b0bfd), (b1b_s, b1bfd)):
        P.op("sync", "dma_start", out=dst, in_=src, inc=[(sG, 16)])
    P.op("sync", "dma_start", out=wg0,
         in_=wgr[0:1, :].rearrange("o (a p) -> (o p) a", p=128),
         allow_slow_non_contiguous=True, inc=[(sG, 16)])
    P.op("sync", "dma_start", out=wg1,
         in_=wgr[1:2, :].rearrange("o (a p) -> (o p) a", p=128),
         allow_slow_non_contiguous=True, inc=[(sG, 16)])
    g0 = P.tok(sG)
    P.op("vector", "memset", ones, 1.0)
    P.op("vector", "memset", ones_bf, 1.0, inc=[(sV, 1)])
    P.op("vector", "tensor_sub", wgd, wg0, wg1, wait=[g0], inc=[(sV, 1)])
    P.op("scalar", "activation", siluc, cfs, AF.Silu, wait=[g0], inc=[(sC, 1)])
    silu_tok = P.tok(sC)
    ones_tok = P.tok(sV)

    P.label = "mod"
    n_ft = 6 * D // 512
    mrow = [row(0, 512), row(512, 512)]
    prev_evac = None
    def _wmod_load(ft, c):
        return r_wmod.load(wmod_s[(ft * DC + c) % 4],
                           wmodd[c * 128:(c + 1) * 128, ft * 512:(ft + 1) * 512])
    toks = {c: _wmod_load(0, c) for c in range(4)}
    for ft in range(n_ft):
        for c in range(DC):
            w = [toks.pop(c)]
            if c == 0:
                w += [silu_tok, prev_evac]
            mm(ps[0][0:1, :], siluc[:, c:c + 1], wmod_s[(ft * DC + c) % 4],
               start=(c == 0), stop=(c == DC - 1), wait=w, inc=[(sT, 1)])
            r_wmod.set_free(ft * DC + c, P.tok(sT))
            nc_, nft = c + 4, ft
            if nc_ >= DC:
                nc_, nft = nc_ - DC, ft + 1
            if nft < n_ft:
                toks[nc_] = _wmod_load(nft, nc_)
        slot = mrow[ft % 2]
        w = [P.tok(sT)]
        if ft >= 2:
            w.append((s_sp, 16 * (ft - 1)))
        P.op("scalar", "copy", slot, ps[0][0:1, :], wait=w, inc=[(sC, 1)])
        prev_evac = P.tok(sC)
        P.op("sync", "dma_start", out=modsc[ft * 512:(ft + 1) * 512], in_=slot,
             wait=[prev_evac], inc=[(s_sp, 16)])
    P.op("sync", "dma_start", out=modT,
         in_=modsc.rearrange("(a p) -> p a", p=128),
         allow_slow_non_contiguous=True, wait=[(s_sp, 16 * n_ft)], inc=[(sG, 16)])
    gm = P.tok(sG)
    ca1, gac, cb1, gbc = der[:, 0:16], der[:, 16:32], der[:, 32:48], der[:, 48:64]
    sa_v, ca_v, ga_v = modT[:, 0:16], modT[:, 16:32], modT[:, 32:48]
    sb_v, cb_v, gb_v = modT[:, 48:64], modT[:, 64:80], modT[:, 80:96]
    P.op("vector", "tensor_scalar_add", ca1, ca_v, 1.0, wait=[gm])
    P.op("vector", "tensor_scalar_add", cb1, cb_v, 1.0)
    for g_in, g_out in ((ga_v, gac), (gb_v, gbc)):
        P.op("vector", "tensor_single_scalar", tmpa, g_in, 0.0, ALU.is_ge)
        P.op("vector", "tensor_scalar", tmpa, tmpa, 2.0, 1.0, ALU.mult, ALU.subtract)
        P.op("vector", "tensor_mul", tmpb, g_in, tmpa)
        P.op("vector", "tensor_scalar_max", tmpb, tmpb, 0.25)
        P.op("vector", "tensor_mul", g_out, tmpb, tmpa)
    P.op("vector", "tensor_copy", wgd_hi, wgd)
    P.op("vector", "tensor_tensor", wgd_lo, wgd, wgd_hi, ALU.subtract)
    P.op("vector", "tensor_scalar", nb0b, b0b_s, -1.0, None, ALU.mult)
    P.op("vector", "tensor_scalar", nb1b, b1b_s, -1.0, None, ALU.mult)
    P.op("vector", "engine_nop", inc=[(sV, 1)])
    mod_tok = P.tok(sV)
    wmod_alldone = P.tok(sT)

    if expose:
        P.op("sync", "dma_start", out=dmp["modT"], in_=modT, wait=[mod_tok],
             inc=[(sst, 16)])
        P.op("sync", "dma_start", out=dmp["der"], in_=der, wait=[mod_tok],
             inc=[(sst, 16)])
    if stage == "mod":
        return _finish(P, nc, es, sst)

    st2 = {"x2free": [None, None], "bc_copy": None, "kevac": None,
           "vevac": None, "bc6_free": None}
    P.label = "ph2"
    NT_ALL = NTILE + Q // TT
    r_xt.free_tok[1] = wmod_alldone
    pend_kv = [None]
    ln2_eps = buf(2576, 1)[0:1, :]
    rms_eps = buf(2584, 1)[0:1, :]
    P.op("vector", "memset", ln2_eps, EPS_LN)
    P.op("vector", "memset", rms_eps, EPS_RMS, inc=[(sV, 1)])
    eps_tok = P.tok(sV)

    def emit_kv(t, x_ap, htok, htoks):
        t0 = t * TT
        wk_t = {c: r_wk.load(wk_s[c % 2], Wkd[c * 128:(c + 1) * 128, :])
                for c in range(2)}
        wv_t = {c: r_wv.load(wv_s[c % 2], Wvd[c * 128:(c + 1) * 128, :])
                for c in range(2)}
        for c in range(DC):
            hc = x_ap[:, c * TT:(c + 1) * TT]
            for kvc in range(4):
                w = []
                if kvc == 0:
                    w = [wk_t[c], htoks[c]]
                    if c == 0:
                        w += [st2["kevac"]]
                mm(ps[kvc][:, 0:TT], wk_s[c % 2][:, kvc * 128:(kvc + 1) * 128],
                   hc, start=(c == 0), stop=(c == DC - 1), wait=w,
                   inc=[(sT, 1)] if kvc == 3 else ())
            r_wk.set_free(t * DC + c, P.tok(sT))
            for tc in range(2):
                w = []
                if tc == 0:
                    w = [wv_t[c]]
                    if c == 0:
                        w += [st2["vevac"], st2["bc_copy"]]
                mm(ps[6 + tc][:, :], hc[:, tc * 128:(tc + 1) * 128], wv_s[c % 2],
                   start=(c == 0), stop=(c == DC - 1), wait=w,
                   inc=[(sT, 1)] if tc == 1 else ())
            r_wv.set_free(t * DC + c, P.tok(sT))
            if c + 2 < DC:
                wk_t[c + 2] = r_wk.load(wk_s[c % 2],
                                        Wkd[(c + 2) * 128:(c + 3) * 128, :])
                wv_t[c + 2] = r_wv.load(wv_s[c % 2],
                                        Wvd[(c + 2) * 128:(c + 3) * 128, :])
        kvtok = P.tok(sT)
        for kvc in range(4):
            P.op("scalar", "copy", Kbuf[:, kvc * S + t0:kvc * S + t0 + TT],
                 ps[kvc][:, 0:TT], wait=[kvtok] if kvc == 0 else [],
                 inc=[(sC, 1)])
        st2["kevac"] = P.tok(sC)
        for tc in range(2):
            kc = t0 // 128 + tc
            P.op("scalar", "copy", Vhi[:, kc * 512:(kc + 1) * 512],
                 ps[6 + tc][:, :], inc=[(sC, 1)])
            P.op("vector", "tensor_tensor", Vlo[:, kc * 512:(kc + 1) * 512],
                 ps[6 + tc][:, :], Vhi[:, kc * 512:(kc + 1) * 512],
                 ALU.subtract, wait=[P.tok(sC)], inc=[(sV, 1)])
        st2["vevac"] = P.tok(sV)
        return kvtok

    TILE_ORDER = [0, 1, 2, 3, NTILE, 4, NTILE + 1, 5, NTILE + 2, 6,
                  NTILE + 3, 7]
    for ti, t in enumerate(TILE_ORDER):
        own = t >= NTILE
        t0 = (t - NTILE) * TT if own else t * TT
        x_ap = xh[ti % 2]
        src_d = xq if own else xfeat
        xtok = r_xt.load(
            x_ap.rearrange("p (c t) -> p c t", t=TT),
            src_d.rearrange("(c p) t -> p c t", p=128)[:, :, t0:t0 + TT])
        assert r_xt.k - 1 == ti
        for c in range(DC):
            xc = x_ap[:, c * TT:(c + 1) * TT]
            x2 = x2s[c % 2][:, 0:TT]
            P.op("scalar", "activation", x2, xc, AF.Square,
                 wait=[xtok, st2["x2free"][c % 2]], inc=[(sC, 1)])
            x2tok = P.tok(sC)
            w_mean = [xtok, ones_tok] if c == 0 else []
            mm(ps[4][0:1, 0:TT], ones_col, xc, start=(c == 0),
               stop=(c == DC - 1), wait=w_mean)
            mm(ps[5][0:1, 0:TT], ones_col, x2, start=(c == 0),
               stop=(c == DC - 1), wait=[x2tok], inc=[(sT, 1)])
            st2["x2free"][c % 2] = P.tok(sT)
        stats_tok = P.tok(sT)
        if pend_kv[0] is not None:
            pend_kv[0]()
            pend_kv[0] = None
        r_m, r_t = row(1024, TT), row(1280, TT)
        r_rstd, r_shift = row(1536, TT), row(1536 + TT, TT)
        P.op("vector", "tensor_scalar", r_m, ps[4][0:1, 0:TT], 1.0 / D, None,
             ALU.mult, wait=[stats_tok, eps_tok])
        P.op("vector", "tensor_mul", r_t, r_m, r_m)
        P.op("vector", "scalar_tensor_tensor", r_rstd, ps[5][0:1, 0:TT],
             1.0 / D, r_t, ALU.mult, ALU.subtract, inc=[(sV, 1)])
        P.op("scalar", "activation", r_rstd, r_rstd, AF.Ln,
             bias=ln2_eps, wait=[P.tok(sV)])
        P.op("scalar", "activation", r_rstd, r_rstd, AF.Exp, scale=-0.5,
             inc=[(sC, 1)])
        P.op("vector", "scalar_tensor_tensor", r_shift, r_m, -1.0, r_rstd,
             ALU.mult, ALU.mult, wait=[P.tok(sC)], inc=[(sV, 1)])
        mm(ps[6][:, 0:2 * TT], ones_row, row(1536, 2 * TT), start=True,
           stop=True, wait=[P.tok(sV), st2["bc_copy"], st2["vevac"],
                            st2.get("h_done")], inc=[(sT, 1)])
        P.op("scalar", "copy", bc[:, 0:2 * TT], ps[6][:, 0:2 * TT],
             wait=[P.tok(sT)], inc=[(sC, 1)])
        st2["bc_copy"] = P.tok(sC)
        bctok = st2["bc_copy"]
        RSTD, SHIFT = bc[:, 0:TT], bc[:, TT:2 * TT]
        htoks = []
        for c in range(DC):
            xc = x_ap[:, c * TT:(c + 1) * TT]
            hd = xc
            w = [bctok]
            if t == 0 and c == 0:
                w.append(mod_tok)
            P.op("scalar", "activation", t1c[c % 2], SHIFT, AF.Identity,
                 scale=ca1[:, c:c + 1], bias=sa_v[:, c:c + 1],
                 wait=[bctok] + ([htoks[c - 2]] if c >= 2 else []),
                 inc=[(sC, 1)])
            c2tok = P.tok(sC)
            P.op("vector", "scalar_tensor_tensor", hd, xc, ca1[:, c:c + 1],
                 RSTD, ALU.mult, ALU.mult, wait=w)
            P.op("vector", "tensor_add", hd, hd, t1c[c % 2], wait=[c2tok],
                 inc=[(sV, 1)])
            if own:
                qt_half = (t0 // 512)
                off = t0 % 512
                hhi = buf(BIG + 65536 + c * 4096 + qt_half * 2048, 512, BF16)[:, off:off + TT]
                hlo = buf(BIG + 65536 + c * 4096 + qt_half * 2048 + 1024, 512, BF16)[:, off:off + TT]
                P.op("vector", "tensor_copy", hhi, xc)
                P.op("vector", "tensor_tensor", hlo, xc, hhi, ALU.subtract,
                     inc=[(sV, 1)])
            htoks.append(P.tok(sV))
        st2["h_done"] = P.tok(sV)
        st2["htoks"] = htoks
        htok = P.tok(sV)
        if own:
            r_xt.set_free(ti, htok)
        else:
            def mk(t=t, ti=ti, x_ap=x_ap, htok=htok,
                   htoks=tuple(st2["htoks"])):
                def go():
                    kvtok = emit_kv(t, x_ap, htok, htoks)
                    r_xt.set_free(ti, kvtok)
                return go
            if pend_kv[0] is not None:
                raise AssertionError("kv backlog")
            pend_kv[0] = mk()
    if pend_kv[0] is not None:
        pend_kv[0]()
    hown_tok = P.tok(sV)

    kv_all = st2["vevac"]
    units = [(kvc, half) for kvc in range(4) for half in range(2)]
    prev_stt = hown_tok
    ktmp = wk_s[0]
    for i, (kvc, half) in enumerate(units):
        k2 = k2buf
        P.op("scalar", "activation", k2, Kbuf[:, kvc * S + half * 1024:
             kvc * S + (half + 1) * 1024], AF.Square,
             wait=[kv_all] + ([P.tok(sV)] if i > 0 else [st2["h_done"]]),
             inc=[(sC, 1)])
        k2tok = P.tok(sC)
        for blk in range(2):
            b = half * 2 + blk
            sl = slice(blk * 512, (blk + 1) * 512)
            mm(ps[6][0:1, :], ones_col, k2[:, sl], start=True, stop=True,
               wait=[k2tok, P.tok(sC)], inc=[(sT, 1)])
            r_r = row(1024, 512)
            P.op("scalar", "activation", r_r, ps[6][0:1, :], AF.Ln,
                 scale=1.0 / DH, bias=rms_eps, wait=[P.tok(sT)])
            P.op("scalar", "activation", r_r, r_r, AF.Exp, scale=-0.5,
                 inc=[(sC, 1)])
            mm(ps[7][:, :], ones_row, r_r, start=True, stop=True,
               wait=[P.tok(sC), prev_stt], inc=[(sT, 1)])
            P.op("scalar", "copy", bc[:, 1024:1536], ps[7][:, :],
                 wait=[P.tok(sT)], inc=[(sC, 1)])
            prev_bc = P.tok(sC)
            base = kvc * S + half * 1024 + blk * 512
            kc = Kbuf[:, base:base + 512]
            P.op("vector", "scalar_tensor_tensor", ktmp, kc, kg,
                 bc[:, 1024:1536], ALU.mult, ALU.mult, wait=[prev_bc])
            khi = buf(BIG + (kvc * 4 + b) * 2048, 512, BF16)
            klo = buf(BIG + (kvc * 4 + b) * 2048 + 1024, 512, BF16)
            P.op("vector", "tensor_copy", khi, ktmp)
            P.op("vector", "tensor_tensor", klo, ktmp, khi, ALU.subtract,
                 inc=[(sV, 1)])
            prev_stt = P.tok(sV)
    krms_tok = P.tok(sV)

    if expose:
        P.op("sync", "dma_start", out=dmp["H"],
             in_=buf(BIG + 65536, DC * Q), wait=[hown_tok], inc=[(sst, 16)])
    if stage == "ph2":
        return _finish(P, nc, es, sst)

    P.label = "attn"
    P.op("sync", "dma_start", out=masks.rearrange("p (i n) -> p i n", n=512),
         in_=maskd.rearrange("i p n -> p i n"), wait=[krms_tok], inc=[(sG, 16)])
    mask_tok = P.tok(sG)

    st3 = {
        "qfree": [hown_tok, hown_tok],
        "wq_pair_free": krms_tok,
        "avfree": krms_tok,
        "sbank_free": [krms_tok, krms_tok],
        "ptf_free": [krms_tok, krms_tok],
        "pthl_free": [krms_tok, krms_tok, krms_tok],
        "avslot_free": None,
        "bc_user": [krms_tok, krms_tok],
        "ps6_free": krms_tok,
        "qh_toks": [None, None],
        "x2q_free": [krms_tok, krms_tok],
    }

    def emit_stage_a(h):
        wsrc_q = Wqd.rearrange("(c p) e -> p c e", p=128)[:, :, h * 128:(h + 1) * 128]
        wq_tok = r_wq.load(
            wq_s[0].rearrange("p (c e) -> p c e", e=128), wsrc_q,
            extra_wait=[krms_tok] if h < 1 else ())
        wqhi_tok = r_wqhi.load(
            wq_hi[:].rearrange("p (c e) -> p c e", e=128), wsrc_q,
            extra_wait=[krms_tok] if h < 1 else ())
        P.op("vector", "tensor_tensor", wq_lo, wq_s[0], wq_hi, ALU.subtract,
             wait=[wq_tok, wqhi_tok, st3["wq_pair_free"]], inc=[(sV, 1)])
        wq_split = P.tok(sV)
        r_wq.set_free(h, wq_split)
        qmm_toks = []
        for qt in range(2):
            for c in range(DC):
                w = []
                if c == 0:
                    w = [wq_split, st3["qfree"][qt]]
                    if h == 0:
                        w.append(hown_tok)
                whi = wq_hi[:, c * 128:(c + 1) * 128]
                wlo = wq_lo[:, c * 128:(c + 1) * 128]
                mm(ps[4 + qt][:, :], whi, Hap(0, c, qt),
                   start=(c == 0), stop=False, wait=w)
                mm(ps[4 + qt][:, :], whi, Hap(1, c, qt), start=False,
                   stop=False)
                mm(ps[4 + qt][:, :], wlo, Hap(0, c, qt), start=False,
                   stop=(c == DC - 1),
                   inc=[(sT, 1)] if c == DC - 1 else ())
            qmm_toks.append(P.tok(sT))
        r_wqhi.set_free(h, qmm_toks[1])
        st3["wq_pair_free"] = qmm_toks[1]
        sq_toks = []
        for qt in range(2):
            P.op("scalar", "activation", x2q[qt], ps[4 + qt][:, :], AF.Square,
                 wait=[qmm_toks[qt], st3["x2q_free"][qt]], inc=[(sC, 1)])
            sq_toks.append(P.tok(sC))
        return qmm_toks, sq_toks

    def mk_rms_steps(h, qmm_toks, sq_toks):
        state = {}

        def msq(qt):
            def go():
                w = [sq_toks[qt], st3["ps6_free"]]
                mm(ps[6][0:1, :], ones_col, x2q[qt], start=True, stop=True,
                   wait=w, inc=[(sT, 1)])
                state["msq%d" % qt] = P.tok(sT)
                st3["x2q_free"][qt] = state["msq%d" % qt]
                r_r = row(1024 + qt * 512, 512)
                P.op("scalar", "activation", r_r, ps[6][0:1, :], AF.Ln,
                     scale=1.0 / DH, bias=rms_eps, wait=[state["msq%d" % qt]])
                P.op("scalar", "activation", r_r, r_r, AF.Exp, scale=-0.5,
                     inc=[(sC, 1)])
                state["ln%d" % qt] = P.tok(sC)
            return go

        def bccp(qt):
            def go():
                mm(ps[6][:, :], ones_row, row(1024 + qt * 512, 512),
                   start=True, stop=True, wait=[state["ln%d" % qt]],
                   inc=[(sT, 1)])
                P.op("scalar", "copy", bc[:, qt * 512:qt * 512 + 512],
                     ps[6][:, :], wait=[P.tok(sT), st3["bc_user"][qt]],
                     inc=[(sC, 1)])
                state["cp%d" % qt] = P.tok(sC)
                st3["ps6_free"] = state["cp%d" % qt]
            return go

        def qhat():
            for qt in range(2):
                qsl = slice(qt * 512, qt * 512 + 512)
                P.op("vector", "scalar_tensor_tensor",
                     ps[4 + qt][:, :], ps[4 + qt][:, :], qg,
                     bc[:, qt * 512:qt * 512 + 512], ALU.mult, ALU.mult,
                     wait=[state["cp%d" % qt]])
                P.op("vector", "tensor_copy", qh_hi[h % 2][:, qsl],
                     ps[4 + qt][:, :])
                P.op("vector", "tensor_tensor", qh_lo[h % 2][:, qsl],
                     ps[4 + qt][:, :], qh_hi[h % 2][:, qsl], ALU.subtract,
                     inc=[(sV, 1)])
                st3["qh_next"][qt] = P.tok(sV)
                st3["bc_user"][qt] = st3["qh_next"][qt]
                st3["qfree"][qt] = st3["qh_next"][qt]
        return [msq(0), bccp(0), msq(1), bccp(1)], qhat

    def emit_chunks(h, qt, inject):
        g = h // 4
        ncH = QT_CHUNKS[qt]
        qsl = slice(qt * 512, qt * 512 + 512)
        qhh = qh_hi[h % 2][:, qsl]
        qhl = qh_lo[h % 2][:, qsl]
        qtok = st3["qh_toks"][qt]
        exp_toks = {}
        plo_toks = {}

        def emit_hi(j):
            w_av = [exp_toks[j]]
            if j == 0:
                w_av.append(st3["avfree"])
            vs = slice(j * 512 + g * 128, j * 512 + (g + 1) * 128)
            mm(ps[2][:, :], Vhi[:, vs], pt_hi[j % 3],
               start=(j == 0), stop=False, wait=w_av)
            mm(ps[2][:, :], Vlo[:, vs], pt_hi[j % 3],
               start=False, stop=False)
            mm(ps[3][0:1, :], ones_bf[:, 0:1], pt_hi[j % 3],
               start=(j == 0), stop=False, inc=[(sT, 1)])

        def emit_lo(j):
            vs = slice(j * 512 + g * 128, j * 512 + (g + 1) * 128)
            mm(ps[2][:, :], Vhi[:, vs], pt_lo[j % 3],
               start=False, stop=(j == ncH - 1), wait=[plo_toks[j]])
            mm(ps[3][0:1, :], ones_bf[:, 0:1], pt_lo[j % 3],
               start=False, stop=(j == ncH - 1), inc=[(sT, 1)])
            st3["pthl_free"][j % 3] = P.tok(sT)

        for c in range(ncH):
            if c >= 2:
                emit_hi(c - 2)
            if c >= 3:
                emit_lo(c - 3)
            if inject and c in inject:
                inject[c]()
            sb = ps[c % 2]
            masked = (qt == 0) or (c >= 8)
            w = [qtok, st3["sbank_free"][c % 2]]
            if h == 0 and qt == 0 and c < 2:
                w.append(mask_tok)
            if masked:
                mm(sb[:, :], iden_bf, masks[:, c * 512:(c + 1) * 512],
                   start=True, stop=False,
                   wait=[st3["sbank_free"][c % 2]] +
                   ([mask_tok, ones_tok] if h == 0 and qt == 0 and c < 2 else []))
            mm(sb[:, :], Khl(0, g, c), qhh, start=not masked, stop=False,
               wait=w)
            mm(sb[:, :], Khl(0, g, c), qhl, start=False, stop=False)
            mm(sb[:, :], Khl(1, g, c), qhh, start=False, stop=True,
               inc=[(sT, 1)])
            smm_tok = P.tok(sT)
            P.op("scalar", "activation", pt_s[c % 2], sb[:, :], AF.Exp,
                 scale=SCALE, wait=[smm_tok, st3["ptf_free"][c % 2]])
            P.op("scalar", "activation", pt_hi[c % 3], sb[:, :], AF.Exp,
                 scale=SCALE, wait=[st3["pthl_free"][c % 3]], inc=[(sC, 1)])
            exp_toks[c] = P.tok(sC)
            st3["sbank_free"][c % 2] = exp_toks[c]
            P.op("vector", "tensor_tensor", pt_lo[c % 3], pt_s[c % 2],
                 pt_hi[c % 3], ALU.subtract, wait=[exp_toks[c]],
                 inc=[(sV, 1)])
            plo_toks[c] = P.tok(sV)
            st3["ptf_free"][c % 2] = plo_toks[c]
        emit_hi(ncH - 2)
        emit_lo(ncH - 3)
        emit_hi(ncH - 1)
        emit_lo(ncH - 2)
        emit_lo(ncH - 1)
        den_tok = P.tok(sT)
        r_d = row(1536, 512)
        P.op("vector", "reciprocal", r_d, ps[3][0:1, :], wait=[den_tok],
             inc=[(sV, 1)])
        mm(ps[7][:, :], ones_row, r_d, start=True, stop=True,
           wait=[P.tok(sV), st3["rdb_free"]], inc=[(sT, 1)])
        P.op("scalar", "copy", bc[:, 1024:1536], ps[7][:, :],
             wait=[P.tok(sT)], inc=[(sC, 1)])
        rdb_tok = P.tok(sC)
        P.op("vector", "tensor_tensor", avn_s[0], ps[2][:, :],
             bc[:, 1024:1536], ALU.mult, wait=[rdb_tok] +
             ([(st_av32, st_av32["n"])] if expose else []))
        P.op("vector", "tensor_copy", avn_hi[0], avn_s[0],
             wait=[st3["avslot_free"]])
        P.op("vector", "tensor_tensor", avn_lo[0], avn_s[0], avn_hi[0],
             ALU.subtract, inc=[(sV, 1)])
        st3["avfree"] = P.tok(sV)
        st3["rdb_free"] = st3["avfree"]
        P.op("sync", "dma_start",
             out=avhs[h * 128:(h + 1) * 128, qt * 512:qt * 512 + 512],
             in_=avn_hi[0], wait=[st3["avfree"]], inc=[(st_avn[0], 16)])
        P.op("sync", "dma_start",
             out=avls[h * 128:(h + 1) * 128, qt * 512:qt * 512 + 512],
             in_=avn_lo[0], wait=[st3["avfree"]], inc=[(st_avn[0], 16)])
        if expose:
            P.op("sync", "dma_start",
                 out=avsc[h * 128:(h + 1) * 128, qt * 512:qt * 512 + 512],
                 in_=avn_s[0], wait=[st3["avfree"]], inc=[(st_av32, 16)])
        st3["avslot_free"] = (st_avn[0], st_avn[0]["n"])

    st3["rdb_free"] = krms_tok
    st3["qh_next"] = [None, None]
    qmm0, sq0 = emit_stage_a(0)
    steps0, qhat0 = mk_rms_steps(0, qmm0, sq0)
    for s in steps0:
        s()
    qhat0()
    st3["qh_toks"] = list(st3["qh_next"])
    for h in range(NH):
        nxt = None
        if h + 1 < NH:
            qmms, sqs = emit_stage_a(h + 1)
            steps, qhat_n = mk_rms_steps(h + 1, qmms, sqs)
            nxt = {4: steps[0], 8: steps[1], 11: steps[2], 14: steps[3]}
        emit_chunks(h, 0, None)
        emit_chunks(h, 1, nxt)
        if h + 1 < NH:
            qhat_n()
            st3["qh_toks"] = list(st3["qh_next"])
    avfree = st3["avfree"]
    qfree = st3["qfree"]
    msq_free = [st3["ps6_free"], st3["rdb_free"]]

    if stage == "attn":
        return _finish(P, nc, es, sst)


    P.label = "wo"
    av_total = [(st_avn[0], st_avn[0]["n"]), (st_avn[1], st_avn[1]["n"])]
    wofree = [avfree, avfree]
    wo_split_prev = [avfree, avfree]
    wo_pair_free = [avfree, avfree]
    x1slot_free = [(), ()]
    stats_user = [qfree[0], qfree[1]]
    ps6_user = msq_free[0]
    ps7_user = msq_free[1]
    h2_toks = [None, None]
    ln2rows_tok = [None, None]
    x1q_tokens = [None, None]
    x1_uses = [0]
    h2_last_v = [None]
    h2_cp = [None, None]

    def h2_prologue(qt, bc_gate):
        r_mean = row(1024 + qt * 512, 512)
        r_rstd = row(2048 + qt * 512, 512)
        s0 = row(0, 512)
        P.op("vector", "tensor_scalar", s0, r_mean, -1.0, None, ALU.mult,
             wait=[ln2rows_tok[qt]])
        P.op("vector", "tensor_mul", s0, s0, r_rstd, inc=[(sV, 1)])
        sh_tok = P.tok(sV)
        mm(ps[6][:, :], ones_row, r_rstd, start=True, stop=True,
           wait=[ln2rows_tok[qt], bc_gate[0]], inc=[(sT, 1)])
        mm(ps[7][:, :], ones_row, s0, start=True, stop=True,
           wait=[sh_tok, bc_gate[1]], inc=[(sT, 1)])
        P.op("scalar", "copy", bc[:, 0:512], ps[6][:, :],
             wait=[(sT, P.tok(sT)[1] - 1)])
        P.op("scalar", "copy", bc[:, 512:1024], ps[7][:, :],
             wait=[P.tok(sT)], inc=[(sC, 1)])
        return P.tok(sC)

    def h2_chunk(qt, c, bc2tok):
        x1tok = r_x1.load(
            x1rd3_s[c % 2],
            x1sc[c * 128:(c + 1) * 128, qt * 512:qt * 512 + 512],
            extra_wait=x1q_tokens[qt] if c < 2 else ())
        pt = ps[6 + c % 2]
        P.op("vector", "scalar_tensor_tensor", pt[:, :], x1rd3_s[c % 2],
             cb1[:, c:c + 1], bc[:, 0:512], ALU.mult, ALU.mult,
             wait=[x1tok, bc2tok, h2_cp[c % 2]])
        P.op("scalar", "activation", t1c2[0], bc[:, 512:1024], AF.Identity,
             scale=cb1[:, c:c + 1], bias=sb_v[:, c:c + 1],
             wait=[bc2tok] + ([h2_last_v[0]] if h2_last_v[0] else []),
             inc=[(sC, 1)])
        P.op("vector", "tensor_add", pt[:, :], pt[:, :], t1c2[0],
             wait=[P.tok(sC)], inc=[(sV, 1)])
        P.op("scalar", "copy", H2ap(0, c, qt), pt[:, :],
             wait=[P.tok(sV)], inc=[(sC, 1)])
        h2_cp[c % 2] = P.tok(sC)
        P.op("vector", "tensor_tensor", H2ap(1, c, qt), pt[:, :],
             H2ap(0, c, qt), ALU.subtract, wait=[P.tok(sC)], inc=[(sV, 1)])
        h2_last_v[0] = P.tok(sV)
        x1_uses[0] += 1
        r_x1.set_free(x1_uses[0] - 1, h2_last_v[0])

    ws_state = {"prev_sub": krms_tok, "prev_st": None, "b": 0}

    def wsplit_step():
        b = ws_state["b"]
        if b >= 64:
            return
        ws_state["b"] += 1
        e = b // 32
        do_, fh_ = divmod(b % 32, 2)
        WbT = W0bTd if e == 0 else W1bTd
        wsrc = (WbT.rearrange("(c p) e -> p c e", p=128)
                [:, fh_ * 16:(fh_ + 1) * 16, do_ * 128:(do_ + 1) * 128])
        t1 = rs_f.load(wsg_f[:].rearrange("p (c e) -> p c e", e=128),
                       wsrc, extra_wait=[ws_state["prev_sub"]])
        t2 = rs_h.load(wsg_hi[:].rearrange("p (c e) -> p c e", e=128),
                       wsrc, extra_wait=[ws_state["prev_sub"]])
        P.op("vector", "tensor_tensor", wsg_lo, wsg_f, wsg_hi, ALU.subtract,
             wait=[t1, t2, ws_state["prev_st"]], inc=[(sV, 1)])
        ws_state["prev_sub"] = P.tok(sV)
        P.op("sync", "dma_start", out=wlob[b], in_=wsg_lo,
             wait=[ws_state["prev_sub"]], inc=[(ws_st, 16)])
        ws_state["prev_st"] = (ws_st, ws_st["n"])

    rs_f = Ring(P, "rsf", 1, "sync")
    rs_h = Ring(P, "rsh", 1, "gpsimd")

    h2q0 = [None]
    wo_prev_qt = None
    for qt in range(2):
        avq_toks = []
        for c2 in range(16):
            w = list(av_total) if (qt == 0 and c2 == 0) else []
            if qt == 1 and c2 == 0:
                w = [wo_prev_qt]
            P.op("gpsimd", "dma_start",
                 out=avqh[:, c2 * 512:(c2 + 1) * 512],
                 in_=avhs[c2 * 128:(c2 + 1) * 128, qt * 512:qt * 512 + 512],
                 wait=w, inc=[(avql[c2], 16)])
            P.op("gpsimd", "dma_start",
                 out=avql_[:, c2 * 512:(c2 + 1) * 512],
                 in_=avls[c2 * 128:(c2 + 1) * 128, qt * 512:qt * 512 + 512],
                 inc=[(avql[c2], 16)])
            avq_toks.append((avql[c2], 32 * (qt + 1)))
        last_mm = None
        for do in range(DC):
            wsrc_o = Wod.rearrange("(c p) e -> p c e", p=128)[:, :, do * 128:(do + 1) * 128]
            wo_tok = r_wo.load(
                wo_s[do % 2].rearrange("p (c e) -> p c e", e=128), wsrc_o,
                extra_wait=[wo_split_prev[do % 2]])
            wohi_tok = r_wohi.load(
                wo_hi[do % 2][:].rearrange("p (c e) -> p c e", e=128), wsrc_o,
                extra_wait=[wo_pair_free[do % 2]])
            P.op("vector", "tensor_tensor", wo_lo[do % 2], wo_s[do % 2],
                 wo_hi[do % 2], ALU.subtract, wait=[wo_tok, wohi_tok],
                 inc=[(sV, 1)])
            wo_split_prev[do % 2] = P.tok(sV)
            for c in range(DC):
                w = [avq_toks[c]] if do == 0 else []
                if c == 0:
                    w += [wo_split_prev[do % 2], wofree[do % 2]]
                whi = wo_hi[do % 2][:, c * 128:(c + 1) * 128]
                wlo = wo_lo[do % 2][:, c * 128:(c + 1) * 128]
                ah = avqh[:, c * 512:(c + 1) * 512]
                al = avql_[:, c * 512:(c + 1) * 512]
                mm(ps[do % 2][:, :], whi, ah, start=(c == 0), stop=False,
                   wait=w)
                mm(ps[do % 2][:, :], whi, al, start=False, stop=False)
                mm(ps[do % 2][:, :], wlo, ah, start=False, stop=(c == DC - 1),
                   inc=[(sT, 1)] if c == DC - 1 else ())
            womm_tok = P.tok(sT)
            last_mm = womm_tok
            r_wo.set_free(qt * DC + do, womm_tok)
            r_wohi.set_free(qt * DC + do, womm_tok)
            wo_pair_free[do % 2] = womm_tok
            xq_tok = r_xq.load(
                xq_s[do % 2],
                xq[do * 128:(do + 1) * 128, qt * 512:qt * 512 + 512])
            sl = do % 2
            x1sl = x1st_s[sl]
            P.op("vector", "tensor_tensor", x1sl, ps[do % 2][:, :],
                 Hap(0, do, qt), ALU.add,
                 wait=[womm_tok] + list(x1slot_free[sl]))
            P.op("vector", "tensor_tensor", x1sl, x1sl, Hap(1, do, qt),
                 ALU.add)
            P.op("vector", "scalar_tensor_tensor", x1sl, x1sl,
                 gac[:, do:do + 1], xq_s[sl], ALU.mult, ALU.add,
                 wait=[xq_tok], inc=[(sV, 1)])
            x1tok = P.tok(sV)
            r_xq.set_free(qt * DC + do, x1tok)
            wofree[do % 2] = x1tok
            P.op("scalar", "activation", x2w_s[sl], x1sl, AF.Square,
                 wait=[x1tok], inc=[(sC, 1)])
            x2wtok = P.tok(sC)
            w_m = [x1tok]
            if do == 0:
                w_m += stats_user
            mm(ps[4][0:1, :], ones_col, x1sl, start=(do == 0),
               stop=(do == DC - 1), wait=w_m)
            mm(ps[5][0:1, :], ones_col, x2w_s[sl], start=(do == 0),
               stop=(do == DC - 1), wait=[x2wtok], inc=[(sT, 1)])
            stats_tok = P.tok(sT)
            P.op("scalar", "dma_start",
                 out=x1sc[do * 128:(do + 1) * 128, qt * 512:qt * 512 + 512],
                 in_=x1sl, wait=[x1tok], inc=[(st_x1[sl], 16)])
            x1slot_free[sl] = ((st_x1[sl], st_x1[sl]["n"]), stats_tok)
            if qt == 1 and h2q0[0] is not None:
                h2_chunk(0, do, h2q0[0])
            wsplit_step()
            wsplit_step()
        wo_prev_qt = last_mm
        x1q_tokens[qt] = [(st_x1[0], st_x1[0]["n"]), (st_x1[1], st_x1[1]["n"])]
        r_mean = row(1024 + qt * 512, 512)
        r_rstd = row(2048 + qt * 512, 512)
        s1 = row(512, 512)
        P.op("vector", "tensor_scalar", r_mean, ps[4][0:1, :], 1.0 / D, None,
             ALU.mult, wait=[P.tok(sT)])
        P.op("vector", "tensor_mul", s1, r_mean, r_mean)
        P.op("vector", "scalar_tensor_tensor", r_rstd, ps[5][0:1, :], 1.0 / D,
             s1, ALU.mult, ALU.subtract, inc=[(sV, 1)])
        P.op("scalar", "activation", r_rstd, r_rstd, AF.Ln, bias=ln2_eps,
             wait=[P.tok(sV)])
        P.op("scalar", "activation", r_rstd, r_rstd, AF.Exp, scale=-0.5,
             inc=[(sC, 1)])
        P.op("vector", "engine_nop", wait=[P.tok(sC)], inc=[(sV, 1)])
        ln2rows_tok[qt] = P.tok(sV)
        stats_user = [P.tok(sV)]
        if qt == 0:
            h2q0[0] = h2_prologue(0, (ps6_user, ps7_user))
    t_3b_end = P.tok(sT)
    if stage == "wo":
        return _finish(P, nc, es, sst)

    P.label = "h2c"
    x1_total = [(st_x1[0], st_x1[0]["n"]), (st_x1[1], st_x1[1]["n"])]
    h2_toks[0] = h2_last_v[0]
    bcg = (h2_last_v[0], h2_last_v[0])
    bc2tok = h2_prologue(1, bcg)
    for c in range(DC):
        h2_chunk(1, c, bc2tok)
    h2_toks[1] = h2_last_v[0]
    h2_tok = h2_last_v[0]
    if expose:
        P.op("sync", "dma_start", out=dmp["H2"],
             in_=buf(BIG + 65536, DC * Q), wait=[h2_tok], inc=[(sst, 16)])
    if stage == "h2":
        return _finish(P, nc, es, sst)

    P.label = "ffn"
    gate_user = h2_tok
    hidfree = [h2_tok, h2_tok]
    outfree2 = [h2_tok, h2_tok, h2_tok, h2_tok]
    o0slot_free = [None, None]
    hidden_gate = h2_tok
    regA_gate = x1_total + [t_3b_end]
    regB_gate = [t_3b_end]
    x1_uses = 2 * DC
    wa_split_prev = h2_tok
    wa_hilo_free = [h2_tok, h2_tok]
    ps67_free = [h2_tok, h2_tok]
    wb_split_prev = h2_tok
    wb_hilo_free = [h2_tok, h2_tok]
    for qt in range(2):
        for c in range(DC):
            w = []
            if c == 0:
                w = [h2_toks[qt], gate_user, mod_tok]
            h2h = H2ap(0, c, qt)
            h2l = H2ap(1, c, qt)
            mm(ps[4][0:1, :], wgd_hi[:, c:c + 1], h2h,
               start=(c == 0), stop=False, wait=w)
            mm(ps[4][0:1, :], wgd_hi[:, c:c + 1], h2l, start=False, stop=False)
            mm(ps[4][0:1, :], wgd_lo[:, c:c + 1], h2h, start=False,
               stop=(c == DC - 1), inc=[(sT, 1)] if c == DC - 1 else ())
        r_g = row(0, 512)
        P.op("vector", "tensor_single_scalar", r_g, ps[4][0:1, :], 0.0,
             ALU.is_ge, wait=[P.tok(sT)], inc=[(sV, 1)])
        gate_user = P.tok(sV)
        mm(ps[5][:, :], ones_row, r_g, start=True, stop=True,
           wait=[P.tok(sV)] + (list(outfree2) if qt == 1 else []),
           inc=[(sT, 1)])
        mb_tok = P.tok(sT)
        Mb = ps[5]
        for e in range(2):
            WaT = W0aTd if e == 0 else W1aTd
            WbT = W0bTd if e == 0 else W1bTd
            ba = b0a_s if e == 0 else b1a_s
            bb_ = b0b_s if e == 0 else b1b_s
            actf = AF.Gelu if e == 0 else AF.Relu
            hid_toks = []
            hmm_toks = {}
            split_toks = {}
            for f in range(FC):
                ew = list(regA_gate) if f < 2 else []
                sl2 = f % 2
                wsrc = WaT.rearrange("(c p) f -> p c f", p=128)[:, :, f * 128:(f + 1) * 128]
                wa_tok = r_w0a.load(
                    w0a_f[sl2].rearrange("p (c e) -> p c e", e=128), wsrc,
                    extra_wait=ew)
                hi_tok = r_wahi.load(
                    w0a_hi[sl2].rearrange("p (c e) -> p c e", e=128), wsrc,
                    extra_wait=(ew if f < 2 else []) + [wa_hilo_free[sl2]])
                P.op("vector", "tensor_tensor", w0a_lo[sl2], w0a_f[sl2],
                     w0a_hi[sl2], ALU.subtract,
                     wait=[wa_tok, hi_tok] + (list(regA_gate) if f < 2 else []),
                     inc=[(sV, 1)])
                split_toks[f] = P.tok(sV)
                split_tok = split_toks[f]
                r_w0a.set_free((qt * 2 + e) * FC + f, split_tok)
                for c in range(DC):
                    w = []
                    if c == 0:
                        w = [split_tok, hidfree[sl2]]
                        if f == 0:
                            w.append(hidden_gate)
                    h2h = H2ap(0, c, qt)
                    h2l = H2ap(1, c, qt)
                    whi = w0a_hi[sl2][:, c * 128:(c + 1) * 128]
                    wlo = w0a_lo[sl2][:, c * 128:(c + 1) * 128]
                    mm(ps[sl2][:, :], whi, h2h, start=(c == 0), stop=False,
                       wait=w)
                    mm(ps[sl2][:, :], whi, h2l, start=False, stop=False)
                    mm(ps[sl2][:, :], wlo, h2h, start=False,
                       stop=(c == DC - 1),
                       inc=[(sT, 1)] if c == DC - 1 else ())
                hmm_tok = P.tok(sT)
                hmm_toks[f] = hmm_tok
                wa_hilo_free[sl2] = hmm_tok
                P.op("scalar", "activation", hid_hi[:, f * 512:(f + 1) * 512],
                     ps[sl2][:, :], actf, bias=ba[:, f:f + 1],
                     wait=[hmm_tok, ps67_free[sl2]])
                P.op("scalar", "activation", ps[6 + sl2][:, :], ps[sl2][:, :],
                     actf, bias=ba[:, f:f + 1],
                     wait=[outfree2[2 + sl2]] if f < 2 else (), inc=[(sC, 1)])
                P.op("vector", "tensor_tensor", hid_lo[:, f * 512:(f + 1) * 512],
                     ps[6 + sl2][:, :], hid_hi[:, f * 512:(f + 1) * 512],
                     ALU.subtract, wait=[P.tok(sC)], inc=[(sV, 1)])
                hid_toks.append(P.tok(sV))
                ps67_free[sl2] = hid_toks[-1]
                hidfree[sl2] = hid_toks[-1]
            hid_all = hid_toks[-1]
            outfree2[2] = ps67_free[0]
            outfree2[3] = ps67_free[1]
            gA_f32 = split_toks[FC - 2]
            gA_hilo = hmm_toks[FC - 2]
            for do in range(DC):
                ob = [2, 3, 6, 7][do % 4]
                omm_tok = None
                for fh in range(2):
                    blk = e * 32 + do * 2 + fh
                    ew = list(regB_gate) if (qt == 0 and e == 0 and do == 0) else []
                    if qt == 0 and e == 0 and do == 0:
                        ew.append((ws_st, 16 * 64))
                    wsrc = (WbT.rearrange("(c p) e -> p c e", p=128)
                            [:, fh * 16:(fh + 1) * 16, do * 128:(do + 1) * 128])
                    hi_tok = r_wbhi.load(
                        w0b_hi[fh][:].rearrange("p (c e) -> p c e", e=128), wsrc,
                        extra_wait=ew + [wb_hilo_free[fh]])
                    lo_tok = r_wblo.load(
                        w0b_lo[fh][:].rearrange("p (c e) -> p c e", e=128),
                        wlob[blk].rearrange("p n -> p n"),
                        extra_wait=ew + [wb_hilo_free[fh]])
                    sb_tok = None
                    for c in range(16):
                        fidx = fh * 16 + c
                        w = []
                        if c == 0:
                            w = [hi_tok, lo_tok, hid_all]
                            if fh == 0:
                                w.append(outfree2[do % 4])
                        whi = w0b_hi[fh][:, c * 128:(c + 1) * 128]
                        wlo = w0b_lo[fh][:, c * 128:(c + 1) * 128]
                        hh = hid_hi[:, fidx * 512:(fidx + 1) * 512]
                        hl = hid_lo[:, fidx * 512:(fidx + 1) * 512]
                        mm(ps[ob][:, :], whi, hh,
                           start=(fidx == 0), stop=False, wait=w)
                        mm(ps[ob][:, :], whi, hl, start=False,
                           stop=False)
                        mm(ps[ob][:, :], wlo, hh, start=False,
                           stop=(fidx == FC - 1),
                           inc=[(sT, 1)] if c == 15 else ())
                    wb_hilo_free[fh] = P.tok(sT)
                    r_wbhi.set_free((((qt * 2 + e) * DC) + do) * 2 + fh, P.tok(sT))
                    r_wblo.set_free((((qt * 2 + e) * DC) + do) * 2 + fh, P.tok(sT))
                omm_tok = P.tok(sT)
                sl = do % 2
                if e == 0:
                    P.op("vector", "tensor_scalar", o0st_s[sl], ps[ob][:, :],
                         bb_[:, do:do + 1], None, ALU.add,
                         wait=[omm_tok, o0slot_free[sl], gA_f32],
                         inc=[(sV, 1)])
                    ctok = P.tok(sV)
                    outfree2[do % 4] = ctok
                    P.op("scalar", "dma_start",
                         out=o0sc[do * 128:(do + 1) * 128, qt * 512:qt * 512 + 512],
                         in_=o0st_s[sl], wait=[ctok], inc=[(st_o0[sl], 16)])
                    o0slot_free[sl] = (st_o0[sl], st_o0[sl]["n"])
                else:
                    o0_tok = r_o0.load(
                        o0rd_s[sl],
                        o0sc[do * 128:(do + 1) * 128, qt * 512:qt * 512 + 512],
                        extra_wait=([(st_o0[0], st_o0[0]["n"]),
                                     (st_o0[1], st_o0[1]["n"]), gA_hilo]
                                    if do < 2 else ()))
                    x1tok = r_x1.load(
                        x1rd_s[sl],
                        x1sc[do * 128:(do + 1) * 128, qt * 512:qt * 512 + 512],
                        extra_wait=[gA_f32] if do < 2 else ())
                    x1_uses += 1
                    t = o0rd_s[sl]
                    nb = nb0b if e == 10 else nb1b
                    P.op("scalar", "activation", t, t, AF.Identity,
                         bias=nb[:, do:do + 1],
                         wait=[o0_tok, omm_tok, mb_tok], inc=[(sC, 1)])
                    P.op("vector", "tensor_tensor", t, t, ps[ob][:, :],
                         ALU.subtract, wait=[P.tok(sC)])
                    P.op("vector", "tensor_mul", t, t, Mb[:, :])
                    P.op("vector", "tensor_scalar", t, t, bb_[:, do:do + 1],
                         None, ALU.add)
                    P.op("vector", "tensor_tensor", t, t, ps[ob][:, :],
                         ALU.add)
                    P.op("vector", "scalar_tensor_tensor", o0st_s[sl], t,
                         gbc[:, do:do + 1], x1rd_s[sl], ALU.mult, ALU.add,
                         wait=[x1tok, o0slot_free[sl]] +
                         ([gA_f32] if do < 2 else []), inc=[(sV, 1)])
                    ytok = P.tok(sV)
                    outfree2[do % 4] = ytok
                    r_o0.set_free(qt * DC + do, ytok)
                    r_x1.set_free(x1_uses - 1, ytok)
                    P.op("scalar", "dma_start",
                         out=ydev[do * 128:(do + 1) * 128, qt * 512:qt * 512 + 512],
                         in_=o0st_s[sl], wait=[ytok], inc=[(sst, 16)])
                    o0slot_free[sl] = (sst, sst["n"])
            hidden_gate = P.tok(sT)
            regA_gate = [outfree2[0], outfree2[1],
                         (st_o0[0], st_o0[0]["n"]), (st_o0[1], st_o0[1]["n"]),
                         (sst, sst["n"])]

    return _finish(P, nc, es, sst)




def _finish(P, nc, es, sst):
    final = (sst, sst["n"])
    with nc.Block() as blk:
        @blk.sync
        def _(E):
            for fn in P.q["sync"]:
                fn(E)
            if final[1]:
                E.wait_ge(final[0]["h"], final[1])

        @blk.gpsimd
        def _(E):
            for fn in P.q["gpsimd"]:
                fn(E)

        @blk.vector
        def _(E):
            for fn in P.q["vector"]:
                fn(E)

        @blk.scalar
        def _(E):
            for fn in P.q["scalar"]:
                fn(E)

        @blk.tensor
        def _(E):
            for fn in P.q["tensor"]:
                fn(E)
    es.close()
    return nc



def _own_rows(cls):
    if cls == 0:
        return np.r_[0:512, 1536:2048]
    return np.r_[512:1536]


def _mk_masks(cls):
    q_bases = (0, 1536) if cls == 0 else (512, 1024)
    kp = np.arange(128).reshape(128, 1)
    qf = np.arange(512).reshape(1, 512)
    m = np.zeros((16, 128, 512), np.float32)
    for c in range(16):
        qb = q_bases[0] if c < 8 else q_bases[1]
        m[c] = np.where(128 * c + kp <= qb + qf, 0.0, NEG)
    return np.ascontiguousarray(m.astype(ml_dtypes.bfloat16))


def kernel(L0_attn_x, L0_attn_Wq, L0_attn_Wk, L0_attn_Wv, L0_attn_Wo,
           L0_attn_q_gamma, L0_attn_k_gamma, L0_ffn_Wg, L0_ffn_W0a, L0_ffn_b0a,
           L0_ffn_W0b, L0_ffn_b0b, L0_ffn_W1a, L0_ffn_b1a, L0_ffn_W1b,
           L0_ffn_b1b, L0_c, L0_wmod):
    x = np.ascontiguousarray(np.asarray(L0_attn_x, dtype=np.float32))
    shared = dict(
        Wq=np.ascontiguousarray(L0_attn_Wq, np.float32),
        Wk=np.ascontiguousarray(L0_attn_Wk, np.float32),
        Wv=np.ascontiguousarray(L0_attn_Wv, np.float32),
        Wo=np.ascontiguousarray(L0_attn_Wo, np.float32),
        qg=np.ascontiguousarray(np.asarray(L0_attn_q_gamma, np.float32).reshape(128, 1)),
        kg=np.ascontiguousarray(np.asarray(L0_attn_k_gamma, np.float32).reshape(128, 1)),
        wg=np.ascontiguousarray(L0_ffn_Wg, np.float32),
        W0aT=np.ascontiguousarray(np.asarray(L0_ffn_W0a, np.float32).T),
        W0bT=np.ascontiguousarray(np.asarray(L0_ffn_W0b, np.float32).T),
        W1aT=np.ascontiguousarray(np.asarray(L0_ffn_W1a, np.float32).T),
        W1bT=np.ascontiguousarray(np.asarray(L0_ffn_W1b, np.float32).T),
        b0af=np.ascontiguousarray(np.asarray(L0_ffn_b0a, np.float32).reshape(FC, 128).T),
        b1af=np.ascontiguousarray(np.asarray(L0_ffn_b1a, np.float32).reshape(FC, 128).T),
        b0bf=np.ascontiguousarray(np.asarray(L0_ffn_b0b, np.float32).reshape(DC, 128).T),
        b1bf=np.ascontiguousarray(np.asarray(L0_ffn_b1b, np.float32).reshape(DC, 128).T),
        wmod=np.ascontiguousarray(L0_wmod, np.float32),
        iden=np.ascontiguousarray(np.eye(128).astype(ml_dtypes.bfloat16)),
    )
    cin = np.asarray(L0_c, np.float32)
    masks_by_cls = [_mk_masks(0), _mk_masks(1)]
    in_maps = []
    for core in range(8):
        b, cls = core // 2, core % 2
        xf = np.ascontiguousarray(x[b].T)
        m = dict(shared)
        m["xfeat"] = xf
        m["xq"] = np.ascontiguousarray(xf[:, _own_rows(cls)])
        m["cf"] = np.ascontiguousarray(cin[b].reshape(16, 128).T)
        m["maskt"] = masks_by_cls[cls]
        in_maps.append(m)

    nc = build_program("full")
    res = run_bass_kernel_spmd(nc, in_maps, core_ids=list(range(8)))
    out = np.empty((4, 2048, 2048), np.float32)
    for core in range(8):
        b, cls = core // 2, core % 2
        out[b][_own_rows(cls)] = res.results[core]["y"].T
    return out


# version 12
# speedup vs baseline: 1.0018x; 1.0018x over previous
import numpy as np
import ml_dtypes
import concourse.bass as bass
import concourse.mybir as mybir
from concourse.bass_utils import run_bass_kernel_spmd

F32 = mybir.dt.float32
BF16 = mybir.dt.bfloat16
U8 = mybir.dt.uint8
AF = mybir.ActivationFunctionType
ALU = mybir.AluOpType

D, DC, NH, KVH, DH = 2048, 16, 16, 4, 128
S, Q, F, FC = 2048, 1024, 4096, 32
TT = 256
NTILE = S // TT
SCALE = float(DH) ** -0.5
NEG = -1.0e30
EPS_LN, EPS_RMS = 1.0e-5, 1.0e-8
QT_CHUNKS = (8, 16)


class Plan:
    ENGINES = ("sync", "gpsimd", "vector", "scalar", "tensor")

    def __init__(self, nc, es):
        self.nc, self.es = nc, es
        self.q = {e: [] for e in self.ENGINES}
        self.label = ""
        self.ttags = []

    def sem(self, name):
        h = self.es.enter_context(self.nc.semaphore(name))
        return {"h": h, "n": 0, "name": name}

    def op(self, eng, method, *args, wait=(), inc=(), **kw):
        wait = tuple(w for w in wait if w is not None)
        for sm, v in wait:
            assert v <= sm["n"], (f"wait {sm['name']}>={v} but planned {sm['n']} "
                                  f"({eng}.{method})")
        for sm, k in inc:
            sm["n"] += k
        if eng == "tensor":
            is32 = False
            try:
                is32 = (args[1].dtype == mybir.dt.float32 and
                        args[1].shape[0] > 1)
            except Exception:
                pass
            self.ttags.append((self.label, is32))
        vdrain = (eng == "vector" and method != "engine_nop")

        def fn(E, wait=wait, inc=tuple(inc), args=args, kw=kw, method=method,
               vdrain=vdrain):
            for sm, v in wait:
                E.wait_ge(sm["h"], v)
            ins = getattr(E, method)(*args, **kw)
            if vdrain:
                ins = E.drain()
            for sm, k in inc:
                ins.then_inc(sm["h"], k)
        self.q[eng].append(fn)

    def tok(self, sm):
        return (sm, sm["n"])


class Ring:
    def __init__(self, plan, name, nslots, queue):
        self.p, self.queue = plan, queue
        self.sems = [plan.sem(f"{name}{i}") for i in range(nslots)]
        self.nuse = [0] * nslots
        self.free_tok = [None] * nslots
        self.k = 0

    def load(self, out_ap, in_ap, extra_wait=(), **kw):
        s = self.k % len(self.sems)
        w = list(extra_wait)
        if self.free_tok[s] is not None:
            w.append(self.free_tok[s])
        self.p.op(self.queue, "dma_start", wait=w, inc=[(self.sems[s], 16)],
                  out=out_ap, in_=in_ap, **kw)
        self.nuse[s] += 1
        self.k += 1
        return (self.sems[s], 16 * self.nuse[s])

    def set_free(self, use_k, tok):
        self.free_tok[use_k % len(self.sems)] = tok


def build_program(stage="full", expose=False):
    nc = bass.Bass("TRN2", target_bir_lowering=False)
    from contextlib import ExitStack
    es = ExitStack()

    def din(name, shape, dt=F32):
        return nc.dram_tensor(name, shape, dt, kind="ExternalInput").ap()

    xfeat = din("xfeat", [D, S])
    xq = din("xq", [D, Q])
    Wqd, Wkd = din("Wq", [D, D]), din("Wk", [D, 512])
    Wvd, Wod = din("Wv", [D, 512]), din("Wo", [D, D])
    qgd, kgd = din("qg", [128, 1]), din("kg", [128, 1])
    wgr = din("wg", [2, D])
    W0aTd, W0bTd = din("W0aT", [D, F]), din("W0bT", [F, D])
    W1aTd, W1bTd = din("W1aT", [D, F]), din("W1bT", [F, D])
    b0afd, b1afd = din("b0af", [128, FC]), din("b1af", [128, FC])
    b0bfd, b1bfd = din("b0bf", [128, DC]), din("b1bf", [128, DC])
    cfd = din("cf", [128, 16])
    wmodd = din("wmod", [D, 6 * D])
    maskd = din("maskt", [16, 128, 512], BF16)
    idend = din("iden", [128, 128], BF16)

    ydev = nc.dram_tensor("y", [D, Q], F32, kind="ExternalOutput").ap()
    sck = "ExternalOutput" if expose else "Internal"
    modsc = nc.dram_tensor("modsc", [6 * D], F32, kind=sck).ap()
    avhs = nc.dram_tensor("avhs", [D, Q], BF16, kind="Internal").ap()
    avls = nc.dram_tensor("avls", [D, Q], BF16, kind="Internal").ap()
    avsc = nc.dram_tensor("avsc", [D, Q], F32, kind=sck).ap()
    x1sc = nc.dram_tensor("x1sc", [D, Q], F32, kind=sck).ap()
    o0sc = nc.dram_tensor("o0sc", [D, Q], F32, kind=sck).ap()
    wlob = nc.dram_tensor("wlob", [64, 128, 2048], BF16, kind="Internal").ap()
    dmp = {}
    if expose:
        for nm, shp in (("modT", [128, 96]), ("der", [128, 64]),
                        ("K", [128, 4 * S]), ("V", [128, 16 * 512]),
                        ("H", [128, DC * Q]), ("H2", [128, DC * Q])):
            dmp[nm] = nc.dram_tensor("dump_" + nm, shp, F32,
                                     kind="ExternalOutput").ap()

    O_ROWS = 4096
    O_BCAST = 16384
    BIG = 22528
    ARENA = BIG + 180224 + 8192
    XTRA = BIG + 180224
    arena = es.enter_context(nc.sbuf_tensor("arena", [128, ARENA], U8))

    def buf(off, n, dt=F32):
        sz = mybir.dt.size(dt)
        assert off % 4 == 0 and off + n * sz <= ARENA, (off, n)
        return arena[:, off:off + n * sz].bitcast(dt)

    ones = buf(0, 128)
    modT = buf(512, 96)
    der = buf(896, 64)
    siluc = buf(1152, 16)
    cfs = buf(1216, 16)
    qg, kg = buf(1280, 1), buf(1288, 1)
    b0a_s, b1a_s = buf(1296, FC), buf(1424, FC)
    b0b_s, b1b_s = buf(1552, DC), buf(1616, DC)
    wg0, wg1, wgd = buf(1680, 16), buf(1744, 16), buf(1808, 16)
    tmpa, tmpb = buf(1872, 16), buf(1936, 16)
    ones_bf = buf(2192, 64, BF16)
    iden_bf = buf(2320, 128, BF16)

    def row(o, n):
        assert o + n <= 3072
        return buf(O_ROWS + o * 4, n)[0:1, :]
    bc = buf(O_BCAST, 1536)

    Kbuf = buf(BIG + 0, 4 * S)

    def Khl(hl, kvc, c):
        base = BIG + (kvc * 4 + c // 4) * 2048 + hl * 1024 + (c % 4) * 256
        return buf(base, 128, BF16)
    Vhi = buf(BIG + 32768, 16 * 512, BF16)
    Vlo = buf(BIG + 49152, 16 * 512, BF16)

    def Hap(hl, c, qt):
        return buf(BIG + 65536 + c * 4096 + qt * 2048 + hl * 1024, 512, BF16)
    TR = BIG + 131072
    wmod_s = [buf(TR + 16384 + i * 2048, 512) for i in range(4)]
    xh = [buf(TR, DC * TT), buf(TR + 16384, DC * TT)]
    x2s = [buf(TR + 32768 + i * 1024, TT) for i in range(2)]
    wk_s = [buf(TR + 34816 + i * 2048, 512) for i in range(2)]
    wv_s = [buf(TR + 38912 + i * 2048, 512) for i in range(2)]
    k2buf = buf(TR + 43008, 1024)
    t1c = [bc[:, 512 + i * 256:768 + i * 256] for i in range(2)]
    masks = buf(TR, 16 * 512, BF16)
    wq_s = [buf(TR + 16384, DC * 128)]
    wq_hi = buf(XTRA, DC * 128, BF16)
    wq_lo = buf(XTRA + 4096, DC * 128, BF16)
    qh_hi = [buf(TR + 24576 + i * 4096, 1024, BF16) for i in range(2)]
    qh_lo = [buf(TR + 26624 + i * 4096, 1024, BF16) for i in range(2)]
    x2q = [buf(O_ROWS + i * 2048, 512) for i in range(2)]
    pt_s = [buf(TR + 34816 + i * 2048, 512) for i in range(2)]
    pt_hi = [buf(TR + 38912 + i * 1024, 512, BF16) for i in range(3)]
    pt_lo = [buf(TR + 41984 + i * 1024, 512, BF16) for i in range(3)]
    avn_s = [buf(TR + 45056, 512)]
    avn_hi = [buf(TR + 47104, 512, BF16)]
    avn_lo = [buf(TR + 48128, 512, BF16)]
    xq_s = [buf(TR + i * 2048, 512) for i in range(2)]
    x1st_s = [buf(TR + 4096 + i * 2048, 512) for i in range(2)]
    x2w_s = [buf(TR + 8192 + i * 2048, 512) for i in range(2)]
    avqh = buf(TR + 16384, 16 * 512, BF16)
    avql_ = buf(TR + 32768, 16 * 512, BF16)
    wo_s = [buf(BIG + i * 8192, DC * 128) for i in range(2)]
    wo_hi = [buf(BIG + 16384 + i * 8192, DC * 128, BF16) for i in range(2)]
    wo_lo = [buf(BIG + 20480 + i * 8192, DC * 128, BF16) for i in range(2)]

    def H2ap(hl, c, qt):
        return buf(BIG + 65536 + c * 4096 + qt * 2048 + hl * 1024, 512, BF16)
    x1rd3_s = [buf(BIG + 32768 + i * 2048, 512) for i in range(2)]
    hid_hi = buf(BIG + 0, FC * 512, BF16)
    hid_lo = buf(BIG + 32768, FC * 512, BF16)
    t1c2 = [bc[:, 1024:1536], bc[:, 1024:1536]]
    w0a_f = [buf(TR + 0, DC * 128), buf(XTRA, DC * 128)]
    w0a_hi = [buf(TR + 8192 + i * 8192, DC * 128, BF16) for i in range(2)]
    w0a_lo = [buf(TR + 12288 + i * 8192, DC * 128, BF16) for i in range(2)]
    x1rd_s = [buf(TR + i * 2048, 512) for i in range(2)]
    o0st_s = [buf(TR + 4096 + i * 2048, 512) for i in range(2)]
    o0rd_s = [buf(TR + 8192 + i * 2048, 512) for i in range(2)]
    w0b_hi = [buf(TR + 24576 + i * 4096, 16 * 128, BF16) for i in range(2)]
    w0b_lo = [buf(TR + 32768 + i * 4096, 16 * 128, BF16) for i in range(2)]
    wsg_f = buf(BIG + 36864, 16 * 128)
    wsg_hi = buf(BIG + 45056, 16 * 128, BF16)
    wsg_lo = buf(BIG + 49152, 16 * 128, BF16)
    wgd_hi = buf(2000, 16, BF16)
    wgd_lo = buf(2032, 16, BF16)
    nb0b = buf(2064, DC)
    nb1b = buf(2128, DC)

    ps = [es.enter_context(nc.psum_tensor(f"ps{i}", [128, 512], F32))
          for i in range(8)]

    P = Plan(nc, es)
    sT, sV, sC = P.sem("sT"), P.sem("sV"), P.sem("sC")
    sG = P.sem("sG")
    sst = P.sem("sst")
    s_sp = P.sem("ssp")
    r_wmod = Ring(P, "rwm", 4, "sync")
    r_xt = Ring(P, "rxt", 2, "gpsimd")
    r_wk = Ring(P, "rwk", 2, "sync")
    r_wv = Ring(P, "rwv", 2, "sync")
    r_wq = Ring(P, "rwq", 1, "sync")
    r_wqhi = Ring(P, "rwqhi", 1, "gpsimd")
    r_wo = Ring(P, "rwo", 2, "sync")
    r_wohi = Ring(P, "rwohi", 2, "gpsimd")
    avql = [P.sem(f"avq{i}") for i in range(16)]
    r_xq = Ring(P, "rxq", 2, "gpsimd")
    r_x1 = Ring(P, "rx1", 2, "gpsimd")
    r_w0a = Ring(P, "rw0a", 2, "sync")
    r_w0b = Ring(P, "rw0b", 2, "sync")
    r_wahi = Ring(P, "rwahi", 2, "gpsimd")
    r_wbhi = Ring(P, "rwbhi", 2, "gpsimd")
    r_o0 = Ring(P, "ro0", 2, "gpsimd")
    st_avn = [P.sem("stav0"), P.sem("stav1")]
    st_av32 = P.sem("stav32")
    st_x1 = [P.sem("stx10"), P.sem("stx11")]
    st_o0 = [P.sem("sto00"), P.sem("sto01")]
    ws_st = P.sem("wsst")
    sG2 = P.sem("sG2")
    r_wblo = Ring(P, "rwblo", 2, "sync")

    def mm(out, lhsT, rhs, start, stop, wait=(), inc=()):
        P.op("tensor", "matmul", out, lhsT, rhs, wait=wait, inc=inc,
             start=start, stop=stop)

    ones_col = ones[:, 0:1]
    ones_row = ones[0:1, :]

    for dst, src in ((cfs, cfd), (qg, qgd), (kg, kgd), (iden_bf, idend),
                     (b0a_s, b0afd),
                     (b1a_s, b1afd), (b0b_s, b0bfd), (b1b_s, b1bfd)):
        P.op("sync", "dma_start", out=dst, in_=src, inc=[(sG, 16)])
    P.op("sync", "dma_start", out=wg0,
         in_=wgr[0:1, :].rearrange("o (a p) -> (o p) a", p=128),
         allow_slow_non_contiguous=True, inc=[(sG, 16)])
    P.op("sync", "dma_start", out=wg1,
         in_=wgr[1:2, :].rearrange("o (a p) -> (o p) a", p=128),
         allow_slow_non_contiguous=True, inc=[(sG, 16)])
    g0 = P.tok(sG)
    P.op("vector", "memset", ones, 1.0)
    P.op("vector", "memset", ones_bf, 1.0, inc=[(sV, 1)])
    P.op("vector", "tensor_sub", wgd, wg0, wg1, wait=[g0], inc=[(sV, 1)])
    P.op("scalar", "activation", siluc, cfs, AF.Silu, wait=[g0], inc=[(sC, 1)])
    silu_tok = P.tok(sC)
    ones_tok = P.tok(sV)

    P.label = "mod"
    n_ft = 6 * D // 512
    mrow = [row(0, 512), row(512, 512)]
    prev_evac = None
    def _wmod_load(ft, c):
        return r_wmod.load(wmod_s[(ft * DC + c) % 4],
                           wmodd[c * 128:(c + 1) * 128, ft * 512:(ft + 1) * 512])
    toks = {c: _wmod_load(0, c) for c in range(4)}
    for ft in range(n_ft):
        for c in range(DC):
            w = [toks.pop(c)]
            if c == 0:
                w += [silu_tok, prev_evac]
            mm(ps[0][0:1, :], siluc[:, c:c + 1], wmod_s[(ft * DC + c) % 4],
               start=(c == 0), stop=(c == DC - 1), wait=w, inc=[(sT, 1)])
            r_wmod.set_free(ft * DC + c, P.tok(sT))
            nc_, nft = c + 4, ft
            if nc_ >= DC:
                nc_, nft = nc_ - DC, ft + 1
            if nft < n_ft:
                toks[nc_] = _wmod_load(nft, nc_)
        slot = mrow[ft % 2]
        w = [P.tok(sT)]
        if ft >= 2:
            w.append((s_sp, 16 * (ft - 1)))
        P.op("scalar", "copy", slot, ps[0][0:1, :], wait=w, inc=[(sC, 1)])
        prev_evac = P.tok(sC)
        P.op("sync", "dma_start", out=modsc[ft * 512:(ft + 1) * 512], in_=slot,
             wait=[prev_evac], inc=[(s_sp, 16)])
    P.op("sync", "dma_start", out=modT,
         in_=modsc.rearrange("(a p) -> p a", p=128),
         allow_slow_non_contiguous=True, wait=[(s_sp, 16 * n_ft)], inc=[(sG, 16)])
    gm = P.tok(sG)
    ca1, gac, cb1, gbc = der[:, 0:16], der[:, 16:32], der[:, 32:48], der[:, 48:64]
    sa_v, ca_v, ga_v = modT[:, 0:16], modT[:, 16:32], modT[:, 32:48]
    sb_v, cb_v, gb_v = modT[:, 48:64], modT[:, 64:80], modT[:, 80:96]
    P.op("vector", "tensor_scalar_add", ca1, ca_v, 1.0, wait=[gm])
    P.op("vector", "tensor_scalar_add", cb1, cb_v, 1.0)
    for g_in, g_out in ((ga_v, gac), (gb_v, gbc)):
        P.op("vector", "tensor_single_scalar", tmpa, g_in, 0.0, ALU.is_ge)
        P.op("vector", "tensor_scalar", tmpa, tmpa, 2.0, 1.0, ALU.mult, ALU.subtract)
        P.op("vector", "tensor_mul", tmpb, g_in, tmpa)
        P.op("vector", "tensor_scalar_max", tmpb, tmpb, 0.25)
        P.op("vector", "tensor_mul", g_out, tmpb, tmpa)
    P.op("vector", "tensor_copy", wgd_hi, wgd)
    P.op("vector", "tensor_tensor", wgd_lo, wgd, wgd_hi, ALU.subtract)
    P.op("vector", "tensor_scalar", nb0b, b0b_s, -1.0, None, ALU.mult)
    P.op("vector", "tensor_scalar", nb1b, b1b_s, -1.0, None, ALU.mult)
    P.op("vector", "engine_nop", inc=[(sV, 1)])
    mod_tok = P.tok(sV)
    wmod_alldone = P.tok(sT)

    if expose:
        P.op("sync", "dma_start", out=dmp["modT"], in_=modT, wait=[mod_tok],
             inc=[(sst, 16)])
        P.op("sync", "dma_start", out=dmp["der"], in_=der, wait=[mod_tok],
             inc=[(sst, 16)])
    if stage == "mod":
        return _finish(P, nc, es, sst)

    st2 = {"x2free": [None, None], "bc_copy": None, "kevac": None,
           "vevac": None, "bc6_free": None}
    P.label = "ph2"
    NT_ALL = NTILE + Q // TT
    r_xt.free_tok[1] = wmod_alldone
    pend_kv = [None]
    ln2_eps = buf(2576, 1)[0:1, :]
    rms_eps = buf(2584, 1)[0:1, :]
    P.op("vector", "memset", ln2_eps, EPS_LN)
    P.op("vector", "memset", rms_eps, EPS_RMS, inc=[(sV, 1)])
    eps_tok = P.tok(sV)

    def emit_kv(t, x_ap, htok, htoks):
        t0 = t * TT
        wk_t = {c: r_wk.load(wk_s[c % 2], Wkd[c * 128:(c + 1) * 128, :])
                for c in range(2)}
        wv_t = {c: r_wv.load(wv_s[c % 2], Wvd[c * 128:(c + 1) * 128, :])
                for c in range(2)}
        for c in range(DC):
            hc = x_ap[:, c * TT:(c + 1) * TT]
            for kvc in range(4):
                w = []
                if kvc == 0:
                    w = [wk_t[c], htoks[c]]
                    if c == 0:
                        w += [st2["kevac"]]
                mm(ps[kvc][:, 0:TT], wk_s[c % 2][:, kvc * 128:(kvc + 1) * 128],
                   hc, start=(c == 0), stop=(c == DC - 1), wait=w,
                   inc=[(sT, 1)] if kvc == 3 else ())
            r_wk.set_free(t * DC + c, P.tok(sT))
            for tc in range(2):
                w = []
                if tc == 0:
                    w = [wv_t[c]]
                    if c == 0:
                        w += [st2["vevac"], st2["bc_copy"]]
                mm(ps[6 + tc][:, :], hc[:, tc * 128:(tc + 1) * 128], wv_s[c % 2],
                   start=(c == 0), stop=(c == DC - 1), wait=w,
                   inc=[(sT, 1)] if tc == 1 else ())
            r_wv.set_free(t * DC + c, P.tok(sT))
            if c + 2 < DC:
                wk_t[c + 2] = r_wk.load(wk_s[c % 2],
                                        Wkd[(c + 2) * 128:(c + 3) * 128, :])
                wv_t[c + 2] = r_wv.load(wv_s[c % 2],
                                        Wvd[(c + 2) * 128:(c + 3) * 128, :])
        kvtok = P.tok(sT)
        for kvc in range(4):
            P.op("scalar", "copy", Kbuf[:, kvc * S + t0:kvc * S + t0 + TT],
                 ps[kvc][:, 0:TT], wait=[kvtok] if kvc == 0 else [],
                 inc=[(sC, 1)])
        st2["kevac"] = P.tok(sC)
        for tc in range(2):
            kc = t0 // 128 + tc
            P.op("scalar", "copy", Vhi[:, kc * 512:(kc + 1) * 512],
                 ps[6 + tc][:, :], inc=[(sC, 1)])
            P.op("vector", "tensor_tensor", Vlo[:, kc * 512:(kc + 1) * 512],
                 ps[6 + tc][:, :], Vhi[:, kc * 512:(kc + 1) * 512],
                 ALU.subtract, wait=[P.tok(sC)], inc=[(sV, 1)])
        st2["vevac"] = P.tok(sV)
        return kvtok

    TILE_ORDER = [0, 1, 2, 3, NTILE, 4, NTILE + 1, 5, NTILE + 2, 6,
                  NTILE + 3, 7]
    for ti, t in enumerate(TILE_ORDER):
        own = t >= NTILE
        t0 = (t - NTILE) * TT if own else t * TT
        x_ap = xh[ti % 2]
        src_d = xq if own else xfeat
        xtok = r_xt.load(
            x_ap.rearrange("p (c t) -> p c t", t=TT),
            src_d.rearrange("(c p) t -> p c t", p=128)[:, :, t0:t0 + TT])
        assert r_xt.k - 1 == ti
        for c in range(DC):
            xc = x_ap[:, c * TT:(c + 1) * TT]
            x2 = x2s[c % 2][:, 0:TT]
            P.op("scalar", "activation", x2, xc, AF.Square,
                 wait=[xtok, st2["x2free"][c % 2]], inc=[(sC, 1)])
            x2tok = P.tok(sC)
            w_mean = [xtok, ones_tok] if c == 0 else []
            mm(ps[4][0:1, 0:TT], ones_col, xc, start=(c == 0),
               stop=(c == DC - 1), wait=w_mean)
            mm(ps[5][0:1, 0:TT], ones_col, x2, start=(c == 0),
               stop=(c == DC - 1), wait=[x2tok], inc=[(sT, 1)])
            st2["x2free"][c % 2] = P.tok(sT)
        stats_tok = P.tok(sT)
        if pend_kv[0] is not None:
            pend_kv[0]()
            pend_kv[0] = None
        r_m, r_t = row(1024, TT), row(1280, TT)
        r_rstd, r_shift = row(1536, TT), row(1536 + TT, TT)
        P.op("vector", "tensor_scalar", r_m, ps[4][0:1, 0:TT], 1.0 / D, None,
             ALU.mult, wait=[stats_tok, eps_tok])
        P.op("vector", "tensor_mul", r_t, r_m, r_m)
        P.op("vector", "scalar_tensor_tensor", r_rstd, ps[5][0:1, 0:TT],
             1.0 / D, r_t, ALU.mult, ALU.subtract, inc=[(sV, 1)])
        P.op("scalar", "activation", r_rstd, r_rstd, AF.Ln,
             bias=ln2_eps, wait=[P.tok(sV)])
        P.op("scalar", "activation", r_rstd, r_rstd, AF.Exp, scale=-0.5,
             inc=[(sC, 1)])
        P.op("vector", "scalar_tensor_tensor", r_shift, r_m, -1.0, r_rstd,
             ALU.mult, ALU.mult, wait=[P.tok(sC)], inc=[(sV, 1)])
        mm(ps[6][:, 0:2 * TT], ones_row, row(1536, 2 * TT), start=True,
           stop=True, wait=[P.tok(sV), st2["bc_copy"], st2["vevac"],
                            st2.get("h_done")], inc=[(sT, 1)])
        P.op("scalar", "copy", bc[:, 0:2 * TT], ps[6][:, 0:2 * TT],
             wait=[P.tok(sT)], inc=[(sC, 1)])
        st2["bc_copy"] = P.tok(sC)
        bctok = st2["bc_copy"]
        RSTD, SHIFT = bc[:, 0:TT], bc[:, TT:2 * TT]
        htoks = []
        for c in range(DC):
            xc = x_ap[:, c * TT:(c + 1) * TT]
            hd = xc
            w = [bctok]
            if t == 0 and c == 0:
                w.append(mod_tok)
            P.op("scalar", "activation", t1c[c % 2], SHIFT, AF.Identity,
                 scale=ca1[:, c:c + 1], bias=sa_v[:, c:c + 1],
                 wait=[bctok] + ([htoks[c - 2]] if c >= 2 else []),
                 inc=[(sC, 1)])
            c2tok = P.tok(sC)
            P.op("vector", "scalar_tensor_tensor", hd, xc, ca1[:, c:c + 1],
                 RSTD, ALU.mult, ALU.mult, wait=w)
            P.op("vector", "tensor_add", hd, hd, t1c[c % 2], wait=[c2tok],
                 inc=[(sV, 1)])
            if own:
                qt_half = (t0 // 512)
                off = t0 % 512
                hhi = buf(BIG + 65536 + c * 4096 + qt_half * 2048, 512, BF16)[:, off:off + TT]
                hlo = buf(BIG + 65536 + c * 4096 + qt_half * 2048 + 1024, 512, BF16)[:, off:off + TT]
                P.op("vector", "tensor_copy", hhi, xc)
                P.op("vector", "tensor_tensor", hlo, xc, hhi, ALU.subtract,
                     inc=[(sV, 1)])
            htoks.append(P.tok(sV))
        st2["h_done"] = P.tok(sV)
        st2["htoks"] = htoks
        htok = P.tok(sV)
        if own:
            r_xt.set_free(ti, htok)
        else:
            def mk(t=t, ti=ti, x_ap=x_ap, htok=htok,
                   htoks=tuple(st2["htoks"])):
                def go():
                    kvtok = emit_kv(t, x_ap, htok, htoks)
                    r_xt.set_free(ti, kvtok)
                return go
            if pend_kv[0] is not None:
                raise AssertionError("kv backlog")
            pend_kv[0] = mk()
    if pend_kv[0] is not None:
        pend_kv[0]()
    hown_tok = P.tok(sV)

    kv_all = st2["vevac"]
    units = [(kvc, half) for kvc in range(4) for half in range(2)]
    prev_stt = hown_tok
    ktmp = wk_s[0]
    for i, (kvc, half) in enumerate(units):
        k2 = k2buf
        P.op("scalar", "activation", k2, Kbuf[:, kvc * S + half * 1024:
             kvc * S + (half + 1) * 1024], AF.Square,
             wait=[kv_all] + ([P.tok(sV)] if i > 0 else [st2["h_done"]]),
             inc=[(sC, 1)])
        k2tok = P.tok(sC)
        for blk in range(2):
            b = half * 2 + blk
            sl = slice(blk * 512, (blk + 1) * 512)
            mm(ps[6][0:1, :], ones_col, k2[:, sl], start=True, stop=True,
               wait=[k2tok, P.tok(sC)], inc=[(sT, 1)])
            r_r = row(1024, 512)
            P.op("scalar", "activation", r_r, ps[6][0:1, :], AF.Ln,
                 scale=1.0 / DH, bias=rms_eps, wait=[P.tok(sT)])
            P.op("scalar", "activation", r_r, r_r, AF.Exp, scale=-0.5,
                 inc=[(sC, 1)])
            mm(ps[7][:, :], ones_row, r_r, start=True, stop=True,
               wait=[P.tok(sC), prev_stt], inc=[(sT, 1)])
            P.op("scalar", "copy", bc[:, 1024:1536], ps[7][:, :],
                 wait=[P.tok(sT)], inc=[(sC, 1)])
            prev_bc = P.tok(sC)
            base = kvc * S + half * 1024 + blk * 512
            kc = Kbuf[:, base:base + 512]
            P.op("vector", "scalar_tensor_tensor", ktmp, kc, kg,
                 bc[:, 1024:1536], ALU.mult, ALU.mult, wait=[prev_bc])
            khi = buf(BIG + (kvc * 4 + b) * 2048, 512, BF16)
            klo = buf(BIG + (kvc * 4 + b) * 2048 + 1024, 512, BF16)
            P.op("vector", "tensor_copy", khi, ktmp)
            P.op("vector", "tensor_tensor", klo, ktmp, khi, ALU.subtract,
                 inc=[(sV, 1)])
            prev_stt = P.tok(sV)
    krms_tok = P.tok(sV)

    if expose:
        P.op("sync", "dma_start", out=dmp["H"],
             in_=buf(BIG + 65536, DC * Q), wait=[hown_tok], inc=[(sst, 16)])
    if stage == "ph2":
        return _finish(P, nc, es, sst)

    P.label = "attn"
    P.op("sync", "dma_start", out=masks.rearrange("p (i n) -> p i n", n=512),
         in_=maskd.rearrange("i p n -> p i n"), wait=[krms_tok], inc=[(sG, 16)])
    mask_tok = P.tok(sG)

    st3 = {
        "qfree": [hown_tok, hown_tok],
        "wq_pair_free": krms_tok,
        "avfree": krms_tok,
        "sbank_free": [krms_tok, krms_tok],
        "ptf_free": [krms_tok, krms_tok],
        "pthl_free": [krms_tok, krms_tok, krms_tok],
        "avslot_free": None,
        "bc_user": [krms_tok, krms_tok],
        "ps6_free": krms_tok,
        "qh_toks": [None, None],
        "x2q_free": [krms_tok, krms_tok],
    }

    def emit_stage_a(h):
        wsrc_q = Wqd.rearrange("(c p) e -> p c e", p=128)[:, :, h * 128:(h + 1) * 128]
        wq_tok = r_wq.load(
            wq_s[0].rearrange("p (c e) -> p c e", e=128), wsrc_q,
            extra_wait=[krms_tok] if h < 1 else ())
        wqhi_tok = r_wqhi.load(
            wq_hi[:].rearrange("p (c e) -> p c e", e=128), wsrc_q,
            extra_wait=[krms_tok] if h < 1 else ())
        P.op("vector", "tensor_tensor", wq_lo, wq_s[0], wq_hi, ALU.subtract,
             wait=[wq_tok, wqhi_tok, st3["wq_pair_free"]], inc=[(sV, 1)])
        wq_split = P.tok(sV)
        r_wq.set_free(h, wq_split)
        qmm_toks = []
        for qt in range(2):
            for c in range(DC):
                w = []
                if c == 0:
                    w = [wq_split, st3["qfree"][qt]]
                    if h == 0:
                        w.append(hown_tok)
                whi = wq_hi[:, c * 128:(c + 1) * 128]
                wlo = wq_lo[:, c * 128:(c + 1) * 128]
                mm(ps[4 + qt][:, :], whi, Hap(0, c, qt),
                   start=(c == 0), stop=False, wait=w)
                mm(ps[4 + qt][:, :], whi, Hap(1, c, qt), start=False,
                   stop=False)
                mm(ps[4 + qt][:, :], wlo, Hap(0, c, qt), start=False,
                   stop=(c == DC - 1),
                   inc=[(sT, 1)] if c == DC - 1 else ())
            qmm_toks.append(P.tok(sT))
        r_wqhi.set_free(h, qmm_toks[1])
        st3["wq_pair_free"] = qmm_toks[1]
        sq_toks = []
        for qt in range(2):
            P.op("scalar", "activation", x2q[qt], ps[4 + qt][:, :], AF.Square,
                 wait=[qmm_toks[qt], st3["x2q_free"][qt]], inc=[(sC, 1)])
            sq_toks.append(P.tok(sC))
        return qmm_toks, sq_toks

    def mk_rms_steps(h, qmm_toks, sq_toks):
        state = {}

        def msq(qt):
            def go():
                w = [sq_toks[qt], st3["ps6_free"]]
                mm(ps[6][0:1, :], ones_col, x2q[qt], start=True, stop=True,
                   wait=w, inc=[(sT, 1)])
                state["msq%d" % qt] = P.tok(sT)
                st3["x2q_free"][qt] = state["msq%d" % qt]
                r_r = row(1024 + qt * 512, 512)
                P.op("scalar", "activation", r_r, ps[6][0:1, :], AF.Ln,
                     scale=1.0 / DH, bias=rms_eps, wait=[state["msq%d" % qt]])
                P.op("scalar", "activation", r_r, r_r, AF.Exp, scale=-0.5,
                     inc=[(sC, 1)])
                state["ln%d" % qt] = P.tok(sC)
            return go

        def bccp(qt):
            def go():
                mm(ps[6][:, :], ones_row, row(1024 + qt * 512, 512),
                   start=True, stop=True, wait=[state["ln%d" % qt]],
                   inc=[(sT, 1)])
                P.op("scalar", "copy", bc[:, qt * 512:qt * 512 + 512],
                     ps[6][:, :], wait=[P.tok(sT), st3["bc_user"][qt]],
                     inc=[(sC, 1)])
                state["cp%d" % qt] = P.tok(sC)
                st3["ps6_free"] = state["cp%d" % qt]
            return go

        def qhat():
            for qt in range(2):
                qsl = slice(qt * 512, qt * 512 + 512)
                P.op("vector", "scalar_tensor_tensor",
                     ps[4 + qt][:, :], ps[4 + qt][:, :], qg,
                     bc[:, qt * 512:qt * 512 + 512], ALU.mult, ALU.mult,
                     wait=[state["cp%d" % qt]])
                P.op("vector", "tensor_copy", qh_hi[h % 2][:, qsl],
                     ps[4 + qt][:, :])
                P.op("vector", "tensor_tensor", qh_lo[h % 2][:, qsl],
                     ps[4 + qt][:, :], qh_hi[h % 2][:, qsl], ALU.subtract,
                     inc=[(sV, 1)])
                st3["qh_next"][qt] = P.tok(sV)
                st3["bc_user"][qt] = st3["qh_next"][qt]
                st3["qfree"][qt] = st3["qh_next"][qt]
        return [msq(0), bccp(0), msq(1), bccp(1)], qhat

    def emit_chunks(h, qt, inject):
        g = h // 4
        ncH = QT_CHUNKS[qt]
        qsl = slice(qt * 512, qt * 512 + 512)
        qhh = qh_hi[h % 2][:, qsl]
        qhl = qh_lo[h % 2][:, qsl]
        qtok = st3["qh_toks"][qt]
        exp_toks = {}
        plo_toks = {}

        def emit_hi(j):
            w_av = [exp_toks[j]]
            if j == 0:
                w_av.append(st3["avfree"])
            vs = slice(j * 512 + g * 128, j * 512 + (g + 1) * 128)
            mm(ps[2][:, :], Vhi[:, vs], pt_hi[j % 3],
               start=(j == 0), stop=False, wait=w_av)
            mm(ps[2][:, :], Vlo[:, vs], pt_hi[j % 3],
               start=False, stop=False)
            mm(ps[3][0:1, :], ones_bf[:, 0:1], pt_hi[j % 3],
               start=(j == 0), stop=False, inc=[(sT, 1)])

        def emit_lo(j):
            vs = slice(j * 512 + g * 128, j * 512 + (g + 1) * 128)
            mm(ps[2][:, :], Vhi[:, vs], pt_lo[j % 3],
               start=False, stop=(j == ncH - 1), wait=[plo_toks[j]])
            mm(ps[3][0:1, :], ones_bf[:, 0:1], pt_lo[j % 3],
               start=False, stop=(j == ncH - 1), inc=[(sT, 1)])
            st3["pthl_free"][j % 3] = P.tok(sT)

        for c in range(ncH):
            if c >= 2:
                emit_hi(c - 2)
            if c >= 3:
                emit_lo(c - 3)
            if inject and c in inject:
                inject[c]()
            sb = ps[c % 2]
            masked = (qt == 0) or (c >= 8)
            w = [qtok, st3["sbank_free"][c % 2]]
            if h == 0 and qt == 0 and c < 2:
                w.append(mask_tok)
            if masked:
                mm(sb[:, :], iden_bf, masks[:, c * 512:(c + 1) * 512],
                   start=True, stop=False,
                   wait=[st3["sbank_free"][c % 2]] +
                   ([mask_tok, ones_tok] if h == 0 and qt == 0 and c < 2 else []))
            mm(sb[:, :], Khl(0, g, c), qhh, start=not masked, stop=False,
               wait=w)
            mm(sb[:, :], Khl(0, g, c), qhl, start=False, stop=False)
            mm(sb[:, :], Khl(1, g, c), qhh, start=False, stop=True,
               inc=[(sT, 1)])
            smm_tok = P.tok(sT)
            P.op("scalar", "activation", pt_s[c % 2], sb[:, :], AF.Exp,
                 scale=SCALE, wait=[smm_tok, st3["ptf_free"][c % 2]])
            P.op("scalar", "activation", pt_hi[c % 3], sb[:, :], AF.Exp,
                 scale=SCALE, wait=[st3["pthl_free"][c % 3]], inc=[(sC, 1)])
            exp_toks[c] = P.tok(sC)
            st3["sbank_free"][c % 2] = exp_toks[c]
            P.op("vector", "tensor_tensor", pt_lo[c % 3], pt_s[c % 2],
                 pt_hi[c % 3], ALU.subtract, wait=[exp_toks[c]],
                 inc=[(sV, 1)])
            plo_toks[c] = P.tok(sV)
            st3["ptf_free"][c % 2] = plo_toks[c]
        emit_hi(ncH - 2)
        emit_lo(ncH - 3)
        emit_hi(ncH - 1)
        emit_lo(ncH - 2)
        emit_lo(ncH - 1)
        den_tok = P.tok(sT)
        r_d = row(1536, 512)
        P.op("vector", "reciprocal", r_d, ps[3][0:1, :], wait=[den_tok],
             inc=[(sV, 1)])
        mm(ps[7][:, :], ones_row, r_d, start=True, stop=True,
           wait=[P.tok(sV), st3["rdb_free"]], inc=[(sT, 1)])
        P.op("scalar", "copy", bc[:, 1024:1536], ps[7][:, :],
             wait=[P.tok(sT)], inc=[(sC, 1)])
        rdb_tok = P.tok(sC)
        P.op("vector", "tensor_tensor", avn_s[0], ps[2][:, :],
             bc[:, 1024:1536], ALU.mult, wait=[rdb_tok] +
             ([(st_av32, st_av32["n"])] if expose else []))
        P.op("vector", "tensor_copy", avn_hi[0], avn_s[0],
             wait=[st3["avslot_free"]])
        P.op("vector", "tensor_tensor", avn_lo[0], avn_s[0], avn_hi[0],
             ALU.subtract, inc=[(sV, 1)])
        st3["avfree"] = P.tok(sV)
        st3["rdb_free"] = st3["avfree"]
        P.op("sync", "dma_start",
             out=avhs[h * 128:(h + 1) * 128, qt * 512:qt * 512 + 512],
             in_=avn_hi[0], wait=[st3["avfree"]], inc=[(st_avn[0], 16)])
        P.op("sync", "dma_start",
             out=avls[h * 128:(h + 1) * 128, qt * 512:qt * 512 + 512],
             in_=avn_lo[0], wait=[st3["avfree"]], inc=[(st_avn[0], 16)])
        if expose:
            P.op("sync", "dma_start",
                 out=avsc[h * 128:(h + 1) * 128, qt * 512:qt * 512 + 512],
                 in_=avn_s[0], wait=[st3["avfree"]], inc=[(st_av32, 16)])
        st3["avslot_free"] = (st_avn[0], st_avn[0]["n"])

    st3["rdb_free"] = krms_tok
    st3["qh_next"] = [None, None]
    qmm0, sq0 = emit_stage_a(0)
    steps0, qhat0 = mk_rms_steps(0, qmm0, sq0)
    for s in steps0:
        s()
    qhat0()
    st3["qh_toks"] = list(st3["qh_next"])
    for h in range(NH):
        nxt = None
        if h + 1 < NH:
            qmms, sqs = emit_stage_a(h + 1)
            steps, qhat_n = mk_rms_steps(h + 1, qmms, sqs)
            nxt = {4: steps[0], 8: steps[1], 11: steps[2], 14: steps[3]}
        emit_chunks(h, 0, None)
        emit_chunks(h, 1, nxt)
        if h + 1 < NH:
            qhat_n()
            st3["qh_toks"] = list(st3["qh_next"])
    avfree = st3["avfree"]
    qfree = st3["qfree"]
    msq_free = [st3["ps6_free"], st3["rdb_free"]]

    if stage == "attn":
        return _finish(P, nc, es, sst)


    P.label = "wo"
    av_total = [(st_avn[0], st_avn[0]["n"]), (st_avn[1], st_avn[1]["n"])]
    wofree = [avfree, avfree]
    wo_split_prev = [avfree, avfree]
    wo_pair_free = [avfree, avfree]
    x1slot_free = [(), ()]
    stats_user = [qfree[0], qfree[1]]
    ps6_user = msq_free[0]
    ps7_user = msq_free[1]
    h2_toks = [None, None]
    ln2rows_tok = [None, None]
    x1q_tokens = [None, None]
    x1_uses = [0]
    h2_last_v = [None]
    h2_cp = [None, None]

    def h2_prologue(qt, bc_gate):
        r_mean = row(1024 + qt * 512, 512)
        r_rstd = row(2048 + qt * 512, 512)
        s0 = row(0, 512)
        P.op("vector", "tensor_scalar", s0, r_mean, -1.0, None, ALU.mult,
             wait=[ln2rows_tok[qt]])
        P.op("vector", "tensor_mul", s0, s0, r_rstd, inc=[(sV, 1)])
        sh_tok = P.tok(sV)
        mm(ps[6][:, :], ones_row, r_rstd, start=True, stop=True,
           wait=[ln2rows_tok[qt], bc_gate[0]], inc=[(sT, 1)])
        mm(ps[7][:, :], ones_row, s0, start=True, stop=True,
           wait=[sh_tok, bc_gate[1]], inc=[(sT, 1)])
        P.op("scalar", "copy", bc[:, 0:512], ps[6][:, :],
             wait=[(sT, P.tok(sT)[1] - 1)])
        P.op("scalar", "copy", bc[:, 512:1024], ps[7][:, :],
             wait=[P.tok(sT)], inc=[(sC, 1)])
        return P.tok(sC)

    def h2_chunk(qt, c, bc2tok):
        x1tok = r_x1.load(
            x1rd3_s[c % 2],
            x1sc[c * 128:(c + 1) * 128, qt * 512:qt * 512 + 512],
            extra_wait=x1q_tokens[qt] if c < 2 else ())
        pt = ps[6 + c % 2]
        P.op("vector", "scalar_tensor_tensor", pt[:, :], x1rd3_s[c % 2],
             cb1[:, c:c + 1], bc[:, 0:512], ALU.mult, ALU.mult,
             wait=[x1tok, bc2tok, h2_cp[c % 2]])
        P.op("scalar", "activation", t1c2[0], bc[:, 512:1024], AF.Identity,
             scale=cb1[:, c:c + 1], bias=sb_v[:, c:c + 1],
             wait=[bc2tok] + ([h2_last_v[0]] if h2_last_v[0] else []),
             inc=[(sC, 1)])
        P.op("vector", "tensor_add", pt[:, :], pt[:, :], t1c2[0],
             wait=[P.tok(sC)], inc=[(sV, 1)])
        P.op("scalar", "copy", H2ap(0, c, qt), pt[:, :],
             wait=[P.tok(sV)], inc=[(sC, 1)])
        h2_cp[c % 2] = P.tok(sC)
        P.op("vector", "tensor_tensor", H2ap(1, c, qt), pt[:, :],
             H2ap(0, c, qt), ALU.subtract, wait=[P.tok(sC)], inc=[(sV, 1)])
        h2_last_v[0] = P.tok(sV)
        x1_uses[0] += 1
        r_x1.set_free(x1_uses[0] - 1, h2_last_v[0])

    ws_state = {"prev_sub": krms_tok, "prev_st": None, "b": 0}

    def wsplit_step():
        b = ws_state["b"]
        if b >= 64:
            return
        ws_state["b"] += 1
        e = b // 32
        do_, fh_ = divmod(b % 32, 2)
        WbT = W0bTd if e == 0 else W1bTd
        wsrc = (WbT.rearrange("(c p) e -> p c e", p=128)
                [:, fh_ * 16:(fh_ + 1) * 16, do_ * 128:(do_ + 1) * 128])
        t1 = rs_f.load(wsg_f[:].rearrange("p (c e) -> p c e", e=128),
                       wsrc, extra_wait=[ws_state["prev_sub"]])
        t2 = rs_h.load(wsg_hi[:].rearrange("p (c e) -> p c e", e=128),
                       wsrc, extra_wait=[ws_state["prev_sub"]])
        P.op("gpsimd", "tensor_tensor", wsg_lo, wsg_f, wsg_hi, ALU.subtract,
             wait=[t1, t2, ws_state["prev_st"]], inc=[(sG2, 1)])
        ws_state["prev_sub"] = P.tok(sG2)
        P.op("sync", "dma_start", out=wlob[b], in_=wsg_lo,
             wait=[ws_state["prev_sub"]], inc=[(ws_st, 16)])
        ws_state["prev_st"] = (ws_st, ws_st["n"])

    rs_f = Ring(P, "rsf", 1, "sync")
    rs_h = Ring(P, "rsh", 1, "gpsimd")

    h2q0 = [None]
    wo_prev_qt = None
    for qt in range(2):
        avq_toks = []
        for c2 in range(16):
            w = list(av_total) if (qt == 0 and c2 == 0) else []
            if qt == 1 and c2 == 0:
                w = [wo_prev_qt]
            P.op("gpsimd", "dma_start",
                 out=avqh[:, c2 * 512:(c2 + 1) * 512],
                 in_=avhs[c2 * 128:(c2 + 1) * 128, qt * 512:qt * 512 + 512],
                 wait=w, inc=[(avql[c2], 16)])
            P.op("gpsimd", "dma_start",
                 out=avql_[:, c2 * 512:(c2 + 1) * 512],
                 in_=avls[c2 * 128:(c2 + 1) * 128, qt * 512:qt * 512 + 512],
                 inc=[(avql[c2], 16)])
            avq_toks.append((avql[c2], 32 * (qt + 1)))
        last_mm = None
        for do in range(DC):
            wsrc_o = Wod.rearrange("(c p) e -> p c e", p=128)[:, :, do * 128:(do + 1) * 128]
            wo_tok = r_wo.load(
                wo_s[do % 2].rearrange("p (c e) -> p c e", e=128), wsrc_o,
                extra_wait=[wo_split_prev[do % 2]])
            wohi_tok = r_wohi.load(
                wo_hi[do % 2][:].rearrange("p (c e) -> p c e", e=128), wsrc_o,
                extra_wait=[wo_pair_free[do % 2]])
            P.op("vector", "tensor_tensor", wo_lo[do % 2], wo_s[do % 2],
                 wo_hi[do % 2], ALU.subtract, wait=[wo_tok, wohi_tok],
                 inc=[(sV, 1)])
            wo_split_prev[do % 2] = P.tok(sV)
            for c in range(DC):
                w = [avq_toks[c]] if do == 0 else []
                if c == 0:
                    w += [wo_split_prev[do % 2], wofree[do % 2]]
                whi = wo_hi[do % 2][:, c * 128:(c + 1) * 128]
                wlo = wo_lo[do % 2][:, c * 128:(c + 1) * 128]
                ah = avqh[:, c * 512:(c + 1) * 512]
                al = avql_[:, c * 512:(c + 1) * 512]
                mm(ps[do % 2][:, :], whi, ah, start=(c == 0), stop=False,
                   wait=w)
                mm(ps[do % 2][:, :], whi, al, start=False, stop=False)
                mm(ps[do % 2][:, :], wlo, ah, start=False, stop=(c == DC - 1),
                   inc=[(sT, 1)] if c == DC - 1 else ())
            womm_tok = P.tok(sT)
            last_mm = womm_tok
            r_wo.set_free(qt * DC + do, womm_tok)
            r_wohi.set_free(qt * DC + do, womm_tok)
            wo_pair_free[do % 2] = womm_tok
            xq_tok = r_xq.load(
                xq_s[do % 2],
                xq[do * 128:(do + 1) * 128, qt * 512:qt * 512 + 512])
            sl = do % 2
            x1sl = x1st_s[sl]
            P.op("vector", "tensor_tensor", x1sl, ps[do % 2][:, :],
                 Hap(0, do, qt), ALU.add,
                 wait=[womm_tok] + list(x1slot_free[sl]))
            P.op("vector", "tensor_tensor", x1sl, x1sl, Hap(1, do, qt),
                 ALU.add)
            P.op("vector", "scalar_tensor_tensor", x1sl, x1sl,
                 gac[:, do:do + 1], xq_s[sl], ALU.mult, ALU.add,
                 wait=[xq_tok], inc=[(sV, 1)])
            x1tok = P.tok(sV)
            r_xq.set_free(qt * DC + do, x1tok)
            wofree[do % 2] = x1tok
            P.op("scalar", "activation", x2w_s[sl], x1sl, AF.Square,
                 wait=[x1tok], inc=[(sC, 1)])
            x2wtok = P.tok(sC)
            w_m = [x1tok]
            if do == 0:
                w_m += stats_user
            mm(ps[4][0:1, :], ones_col, x1sl, start=(do == 0),
               stop=(do == DC - 1), wait=w_m)
            mm(ps[5][0:1, :], ones_col, x2w_s[sl], start=(do == 0),
               stop=(do == DC - 1), wait=[x2wtok], inc=[(sT, 1)])
            stats_tok = P.tok(sT)
            P.op("scalar", "dma_start",
                 out=x1sc[do * 128:(do + 1) * 128, qt * 512:qt * 512 + 512],
                 in_=x1sl, wait=[x1tok], inc=[(st_x1[sl], 16)])
            x1slot_free[sl] = ((st_x1[sl], st_x1[sl]["n"]), stats_tok)
            if qt == 1 and h2q0[0] is not None:
                h2_chunk(0, do, h2q0[0])
            if qt == 0:
                wsplit_step()
                wsplit_step()
        wo_prev_qt = last_mm
        x1q_tokens[qt] = [(st_x1[0], st_x1[0]["n"]), (st_x1[1], st_x1[1]["n"])]
        r_mean = row(1024 + qt * 512, 512)
        r_rstd = row(2048 + qt * 512, 512)
        s1 = row(512, 512)
        P.op("vector", "tensor_scalar", r_mean, ps[4][0:1, :], 1.0 / D, None,
             ALU.mult, wait=[P.tok(sT)])
        P.op("vector", "tensor_mul", s1, r_mean, r_mean)
        P.op("vector", "scalar_tensor_tensor", r_rstd, ps[5][0:1, :], 1.0 / D,
             s1, ALU.mult, ALU.subtract, inc=[(sV, 1)])
        P.op("scalar", "activation", r_rstd, r_rstd, AF.Ln, bias=ln2_eps,
             wait=[P.tok(sV)])
        P.op("scalar", "activation", r_rstd, r_rstd, AF.Exp, scale=-0.5,
             inc=[(sC, 1)])
        P.op("vector", "engine_nop", wait=[P.tok(sC)], inc=[(sV, 1)])
        ln2rows_tok[qt] = P.tok(sV)
        stats_user = [P.tok(sV)]
        if qt == 0:
            h2q0[0] = h2_prologue(0, (ps6_user, ps7_user))
    t_3b_end = P.tok(sT)
    if stage == "wo":
        return _finish(P, nc, es, sst)

    P.label = "h2c"
    x1_total = [(st_x1[0], st_x1[0]["n"]), (st_x1[1], st_x1[1]["n"])]
    h2_toks[0] = h2_last_v[0]
    bcg = (h2_last_v[0], h2_last_v[0])
    bc2tok = h2_prologue(1, bcg)
    for c in range(DC):
        h2_chunk(1, c, bc2tok)
        wsplit_step()
        wsplit_step()
    h2_toks[1] = h2_last_v[0]
    h2_tok = h2_last_v[0]
    if expose:
        P.op("sync", "dma_start", out=dmp["H2"],
             in_=buf(BIG + 65536, DC * Q), wait=[h2_tok], inc=[(sst, 16)])
    if stage == "h2":
        return _finish(P, nc, es, sst)

    P.label = "ffn"
    gate_user = h2_tok
    hidfree = [h2_tok, h2_tok]
    outfree2 = [h2_tok, h2_tok, h2_tok, h2_tok]
    o0slot_free = [None, None]
    hidden_gate = h2_tok
    regA_gate = x1_total + [t_3b_end]
    regB_gate = [t_3b_end]
    x1_uses = 2 * DC
    wa_split_prev = h2_tok
    wa_hilo_free = [h2_tok, h2_tok]
    ps67_free = [h2_tok, h2_tok]
    wb_split_prev = h2_tok
    wb_hilo_free = [h2_tok, h2_tok]
    for qt in range(2):
        for c in range(DC):
            w = []
            if c == 0:
                w = [h2_toks[qt], gate_user, mod_tok]
            h2h = H2ap(0, c, qt)
            h2l = H2ap(1, c, qt)
            mm(ps[4][0:1, :], wgd_hi[:, c:c + 1], h2h,
               start=(c == 0), stop=False, wait=w)
            mm(ps[4][0:1, :], wgd_hi[:, c:c + 1], h2l, start=False, stop=False)
            mm(ps[4][0:1, :], wgd_lo[:, c:c + 1], h2h, start=False,
               stop=(c == DC - 1), inc=[(sT, 1)] if c == DC - 1 else ())
        r_g = row(0, 512)
        P.op("vector", "tensor_single_scalar", r_g, ps[4][0:1, :], 0.0,
             ALU.is_ge, wait=[P.tok(sT)], inc=[(sV, 1)])
        gate_user = P.tok(sV)
        mm(ps[5][:, :], ones_row, r_g, start=True, stop=True,
           wait=[P.tok(sV)] + (list(outfree2) if qt == 1 else []),
           inc=[(sT, 1)])
        mb_tok = P.tok(sT)
        Mb = ps[5]
        for e in range(2):
            WaT = W0aTd if e == 0 else W1aTd
            WbT = W0bTd if e == 0 else W1bTd
            ba = b0a_s if e == 0 else b1a_s
            bb_ = b0b_s if e == 0 else b1b_s
            actf = AF.Gelu if e == 0 else AF.Relu
            hid_toks = []
            hmm_toks = {}
            split_toks = {}
            for f in range(FC):
                ew = list(regA_gate) if f < 2 else []
                sl2 = f % 2
                wsrc = WaT.rearrange("(c p) f -> p c f", p=128)[:, :, f * 128:(f + 1) * 128]
                wa_tok = r_w0a.load(
                    w0a_f[sl2].rearrange("p (c e) -> p c e", e=128), wsrc,
                    extra_wait=ew)
                hi_tok = r_wahi.load(
                    w0a_hi[sl2].rearrange("p (c e) -> p c e", e=128), wsrc,
                    extra_wait=(ew if f < 2 else []) + [wa_hilo_free[sl2]])
                P.op("vector", "tensor_tensor", w0a_lo[sl2], w0a_f[sl2],
                     w0a_hi[sl2], ALU.subtract,
                     wait=[wa_tok, hi_tok] + (list(regA_gate) if f < 2 else []),
                     inc=[(sV, 1)])
                split_toks[f] = P.tok(sV)
                split_tok = split_toks[f]
                r_w0a.set_free((qt * 2 + e) * FC + f, split_tok)
                for c in range(DC):
                    w = []
                    if c == 0:
                        w = [split_tok, hidfree[sl2]]
                        if f == 0:
                            w.append(hidden_gate)
                    h2h = H2ap(0, c, qt)
                    h2l = H2ap(1, c, qt)
                    whi = w0a_hi[sl2][:, c * 128:(c + 1) * 128]
                    wlo = w0a_lo[sl2][:, c * 128:(c + 1) * 128]
                    mm(ps[sl2][:, :], whi, h2h, start=(c == 0), stop=False,
                       wait=w)
                    mm(ps[sl2][:, :], whi, h2l, start=False, stop=False)
                    mm(ps[sl2][:, :], wlo, h2h, start=False,
                       stop=(c == DC - 1),
                       inc=[(sT, 1)] if c == DC - 1 else ())
                hmm_tok = P.tok(sT)
                hmm_toks[f] = hmm_tok
                wa_hilo_free[sl2] = hmm_tok
                P.op("scalar", "activation", hid_hi[:, f * 512:(f + 1) * 512],
                     ps[sl2][:, :], actf, bias=ba[:, f:f + 1],
                     wait=[hmm_tok, ps67_free[sl2]])
                P.op("scalar", "activation", ps[6 + sl2][:, :], ps[sl2][:, :],
                     actf, bias=ba[:, f:f + 1],
                     wait=[outfree2[2 + sl2]] if f < 2 else (), inc=[(sC, 1)])
                P.op("vector", "tensor_tensor", hid_lo[:, f * 512:(f + 1) * 512],
                     ps[6 + sl2][:, :], hid_hi[:, f * 512:(f + 1) * 512],
                     ALU.subtract, wait=[P.tok(sC)], inc=[(sV, 1)])
                hid_toks.append(P.tok(sV))
                ps67_free[sl2] = hid_toks[-1]
                hidfree[sl2] = hid_toks[-1]
            hid_all = hid_toks[-1]
            outfree2[2] = ps67_free[0]
            outfree2[3] = ps67_free[1]
            gA_f32 = split_toks[FC - 2]
            gA_hilo = hmm_toks[FC - 2]
            for do in range(DC):
                ob = [2, 3, 6, 7][do % 4]
                omm_tok = None
                for fh in range(2):
                    blk = e * 32 + do * 2 + fh
                    ew = list(regB_gate) if (qt == 0 and e == 0 and do == 0) else []
                    if e == 0 and qt == 0 and do == 0:
                        ew.append((ws_st, 16 * 32))
                    if e == 1 and qt == 0 and do == 0:
                        ew = ew + [(ws_st, 16 * 64)]
                    wsrc = (WbT.rearrange("(c p) e -> p c e", p=128)
                            [:, fh * 16:(fh + 1) * 16, do * 128:(do + 1) * 128])
                    hi_tok = r_wbhi.load(
                        w0b_hi[fh][:].rearrange("p (c e) -> p c e", e=128), wsrc,
                        extra_wait=ew + [wb_hilo_free[fh]])
                    lo_tok = r_wblo.load(
                        w0b_lo[fh][:].rearrange("p (c e) -> p c e", e=128),
                        wlob[blk].rearrange("p n -> p n"),
                        extra_wait=ew + [wb_hilo_free[fh]])
                    sb_tok = None
                    for c in range(16):
                        fidx = fh * 16 + c
                        w = []
                        if c == 0:
                            w = [hi_tok, lo_tok, hid_all]
                            if fh == 0:
                                w.append(outfree2[do % 4])
                        whi = w0b_hi[fh][:, c * 128:(c + 1) * 128]
                        wlo = w0b_lo[fh][:, c * 128:(c + 1) * 128]
                        hh = hid_hi[:, fidx * 512:(fidx + 1) * 512]
                        hl = hid_lo[:, fidx * 512:(fidx + 1) * 512]
                        mm(ps[ob][:, :], whi, hh,
                           start=(fidx == 0), stop=False, wait=w)
                        mm(ps[ob][:, :], whi, hl, start=False,
                           stop=False)
                        mm(ps[ob][:, :], wlo, hh, start=False,
                           stop=(fidx == FC - 1),
                           inc=[(sT, 1)] if c == 15 else ())
                    wb_hilo_free[fh] = P.tok(sT)
                    r_wbhi.set_free((((qt * 2 + e) * DC) + do) * 2 + fh, P.tok(sT))
                    r_wblo.set_free((((qt * 2 + e) * DC) + do) * 2 + fh, P.tok(sT))
                omm_tok = P.tok(sT)
                sl = do % 2
                if e == 0:
                    P.op("vector", "tensor_scalar", o0st_s[sl], ps[ob][:, :],
                         bb_[:, do:do + 1], None, ALU.add,
                         wait=[omm_tok, o0slot_free[sl], gA_f32],
                         inc=[(sV, 1)])
                    ctok = P.tok(sV)
                    outfree2[do % 4] = ctok
                    P.op("scalar", "dma_start",
                         out=o0sc[do * 128:(do + 1) * 128, qt * 512:qt * 512 + 512],
                         in_=o0st_s[sl], wait=[ctok], inc=[(st_o0[sl], 16)])
                    o0slot_free[sl] = (st_o0[sl], st_o0[sl]["n"])
                else:
                    o0_tok = r_o0.load(
                        o0rd_s[sl],
                        o0sc[do * 128:(do + 1) * 128, qt * 512:qt * 512 + 512],
                        extra_wait=([(st_o0[0], st_o0[0]["n"]),
                                     (st_o0[1], st_o0[1]["n"]), gA_hilo]
                                    if do < 2 else ()))
                    x1tok = r_x1.load(
                        x1rd_s[sl],
                        x1sc[do * 128:(do + 1) * 128, qt * 512:qt * 512 + 512],
                        extra_wait=[gA_f32] if do < 2 else ())
                    x1_uses += 1
                    t = o0rd_s[sl]
                    nb = nb0b if e == 10 else nb1b
                    P.op("scalar", "activation", t, t, AF.Identity,
                         bias=nb[:, do:do + 1],
                         wait=[o0_tok, omm_tok, mb_tok], inc=[(sC, 1)])
                    P.op("vector", "tensor_tensor", t, t, ps[ob][:, :],
                         ALU.subtract, wait=[P.tok(sC)])
                    P.op("vector", "tensor_mul", t, t, Mb[:, :])
                    P.op("vector", "tensor_scalar", t, t, bb_[:, do:do + 1],
                         None, ALU.add)
                    P.op("vector", "tensor_tensor", t, t, ps[ob][:, :],
                         ALU.add)
                    P.op("vector", "scalar_tensor_tensor", o0st_s[sl], t,
                         gbc[:, do:do + 1], x1rd_s[sl], ALU.mult, ALU.add,
                         wait=[x1tok, o0slot_free[sl]] +
                         ([gA_f32] if do < 2 else []), inc=[(sV, 1)])
                    ytok = P.tok(sV)
                    outfree2[do % 4] = ytok
                    r_o0.set_free(qt * DC + do, ytok)
                    r_x1.set_free(x1_uses - 1, ytok)
                    P.op("scalar", "dma_start",
                         out=ydev[do * 128:(do + 1) * 128, qt * 512:qt * 512 + 512],
                         in_=o0st_s[sl], wait=[ytok], inc=[(sst, 16)])
                    o0slot_free[sl] = (sst, sst["n"])
            hidden_gate = P.tok(sT)
            regA_gate = [outfree2[0], outfree2[1],
                         (st_o0[0], st_o0[0]["n"]), (st_o0[1], st_o0[1]["n"]),
                         (sst, sst["n"])]

    return _finish(P, nc, es, sst)




def _finish(P, nc, es, sst):
    final = (sst, sst["n"])
    with nc.Block() as blk:
        @blk.sync
        def _(E):
            for fn in P.q["sync"]:
                fn(E)
            if final[1]:
                E.wait_ge(final[0]["h"], final[1])

        @blk.gpsimd
        def _(E):
            for fn in P.q["gpsimd"]:
                fn(E)

        @blk.vector
        def _(E):
            for fn in P.q["vector"]:
                fn(E)

        @blk.scalar
        def _(E):
            for fn in P.q["scalar"]:
                fn(E)

        @blk.tensor
        def _(E):
            for fn in P.q["tensor"]:
                fn(E)
    es.close()
    return nc



def _own_rows(cls):
    if cls == 0:
        return np.r_[0:512, 1536:2048]
    return np.r_[512:1536]


def _mk_masks(cls):
    q_bases = (0, 1536) if cls == 0 else (512, 1024)
    kp = np.arange(128).reshape(128, 1)
    qf = np.arange(512).reshape(1, 512)
    m = np.zeros((16, 128, 512), np.float32)
    for c in range(16):
        qb = q_bases[0] if c < 8 else q_bases[1]
        m[c] = np.where(128 * c + kp <= qb + qf, 0.0, NEG)
    return np.ascontiguousarray(m.astype(ml_dtypes.bfloat16))


def kernel(L0_attn_x, L0_attn_Wq, L0_attn_Wk, L0_attn_Wv, L0_attn_Wo,
           L0_attn_q_gamma, L0_attn_k_gamma, L0_ffn_Wg, L0_ffn_W0a, L0_ffn_b0a,
           L0_ffn_W0b, L0_ffn_b0b, L0_ffn_W1a, L0_ffn_b1a, L0_ffn_W1b,
           L0_ffn_b1b, L0_c, L0_wmod):
    x = np.ascontiguousarray(np.asarray(L0_attn_x, dtype=np.float32))
    shared = dict(
        Wq=np.ascontiguousarray(L0_attn_Wq, np.float32),
        Wk=np.ascontiguousarray(L0_attn_Wk, np.float32),
        Wv=np.ascontiguousarray(L0_attn_Wv, np.float32),
        Wo=np.ascontiguousarray(L0_attn_Wo, np.float32),
        qg=np.ascontiguousarray(np.asarray(L0_attn_q_gamma, np.float32).reshape(128, 1)),
        kg=np.ascontiguousarray(np.asarray(L0_attn_k_gamma, np.float32).reshape(128, 1)),
        wg=np.ascontiguousarray(L0_ffn_Wg, np.float32),
        W0aT=np.ascontiguousarray(np.asarray(L0_ffn_W0a, np.float32).T),
        W0bT=np.ascontiguousarray(np.asarray(L0_ffn_W0b, np.float32).T),
        W1aT=np.ascontiguousarray(np.asarray(L0_ffn_W1a, np.float32).T),
        W1bT=np.ascontiguousarray(np.asarray(L0_ffn_W1b, np.float32).T),
        b0af=np.ascontiguousarray(np.asarray(L0_ffn_b0a, np.float32).reshape(FC, 128).T),
        b1af=np.ascontiguousarray(np.asarray(L0_ffn_b1a, np.float32).reshape(FC, 128).T),
        b0bf=np.ascontiguousarray(np.asarray(L0_ffn_b0b, np.float32).reshape(DC, 128).T),
        b1bf=np.ascontiguousarray(np.asarray(L0_ffn_b1b, np.float32).reshape(DC, 128).T),
        wmod=np.ascontiguousarray(L0_wmod, np.float32),
        iden=np.ascontiguousarray(np.eye(128).astype(ml_dtypes.bfloat16)),
    )
    cin = np.asarray(L0_c, np.float32)
    masks_by_cls = [_mk_masks(0), _mk_masks(1)]
    in_maps = []
    for core in range(8):
        b, cls = core // 2, core % 2
        xf = np.ascontiguousarray(x[b].T)
        m = dict(shared)
        m["xfeat"] = xf
        m["xq"] = np.ascontiguousarray(xf[:, _own_rows(cls)])
        m["cf"] = np.ascontiguousarray(cin[b].reshape(16, 128).T)
        m["maskt"] = masks_by_cls[cls]
        in_maps.append(m)

    nc = build_program("full")
    res = run_bass_kernel_spmd(nc, in_maps, core_ids=list(range(8)))
    out = np.empty((4, 2048, 2048), np.float32)
    for core in range(8):
        b, cls = core // 2, core % 2
        out[b][_own_rows(cls)] = res.results[core]["y"].T
    return out


# version 13
# speedup vs baseline: 1.0414x; 1.0395x over previous
import numpy as np
import ml_dtypes
import concourse.bass as bass
import concourse.mybir as mybir
from concourse.bass_utils import run_bass_kernel_spmd

F32 = mybir.dt.float32
BF16 = mybir.dt.bfloat16
U8 = mybir.dt.uint8
AF = mybir.ActivationFunctionType
ALU = mybir.AluOpType

D, DC, NH, KVH, DH = 2048, 16, 16, 4, 128
S, Q, F, FC = 2048, 1024, 4096, 32
TT = 256
NTILE = S // TT
SCALE = float(DH) ** -0.5
NEG = -1.0e30
EPS_LN, EPS_RMS = 1.0e-5, 1.0e-8
QT_CHUNKS = (8, 16)


class Plan:
    ENGINES = ("sync", "gpsimd", "vector", "scalar", "tensor")

    def __init__(self, nc, es):
        self.nc, self.es = nc, es
        self.q = {e: [] for e in self.ENGINES}
        self.label = ""
        self.ttags = []

    def sem(self, name):
        h = self.es.enter_context(self.nc.semaphore(name))
        return {"h": h, "n": 0, "name": name}

    def op(self, eng, method, *args, wait=(), inc=(), **kw):
        wait = tuple(w for w in wait if w is not None)
        for sm, v in wait:
            assert v <= sm["n"], (f"wait {sm['name']}>={v} but planned {sm['n']} "
                                  f"({eng}.{method})")
        for sm, k in inc:
            sm["n"] += k
        if eng == "tensor":
            is32 = False
            try:
                is32 = (args[1].dtype == mybir.dt.float32 and
                        args[1].shape[0] > 1)
            except Exception:
                pass
            self.ttags.append((self.label, is32))
        vdrain = (eng == "vector" and method != "engine_nop")

        def fn(E, wait=wait, inc=tuple(inc), args=args, kw=kw, method=method,
               vdrain=vdrain):
            for sm, v in wait:
                E.wait_ge(sm["h"], v)
            ins = getattr(E, method)(*args, **kw)
            if vdrain:
                ins = E.drain()
            for sm, k in inc:
                ins.then_inc(sm["h"], k)
        self.q[eng].append(fn)

    def tok(self, sm):
        return (sm, sm["n"])


class Ring:
    def __init__(self, plan, name, nslots, queue):
        self.p, self.queue = plan, queue
        self.sems = [plan.sem(f"{name}{i}") for i in range(nslots)]
        self.nuse = [0] * nslots
        self.free_tok = [None] * nslots
        self.k = 0

    def load(self, out_ap, in_ap, extra_wait=(), **kw):
        s = self.k % len(self.sems)
        w = list(extra_wait)
        if self.free_tok[s] is not None:
            w.append(self.free_tok[s])
        self.p.op(self.queue, "dma_start", wait=w, inc=[(self.sems[s], 16)],
                  out=out_ap, in_=in_ap, **kw)
        self.nuse[s] += 1
        self.k += 1
        return (self.sems[s], 16 * self.nuse[s])

    def set_free(self, use_k, tok):
        self.free_tok[use_k % len(self.sems)] = tok


def build_program(stage="full", expose=False):
    nc = bass.Bass("TRN2", target_bir_lowering=False)
    from contextlib import ExitStack
    es = ExitStack()

    def din(name, shape, dt=F32):
        return nc.dram_tensor(name, shape, dt, kind="ExternalInput").ap()

    xfeat = din("xfeat", [D, S])
    xq = din("xq", [D, Q])
    Wqd, Wkd = din("Wq", [D, D]), din("Wk", [D, 512])
    Wvd, Wod = din("Wv", [D, 512]), din("Wo", [D, D])
    qgd, kgd = din("qg", [128, 1]), din("kg", [128, 1])
    wgr = din("wg", [2, D])
    W0aTd, W0bTd = din("W0aT", [D, F]), din("W0bT", [F, D])
    W1aTd, W1bTd = din("W1aT", [D, F]), din("W1bT", [F, D])
    b0afd, b1afd = din("b0af", [128, FC]), din("b1af", [128, FC])
    b0bfd, b1bfd = din("b0bf", [128, DC]), din("b1bf", [128, DC])
    cfd = din("cf", [128, 16])
    wmodd = din("wmod", [D, 6 * D])
    maskd = din("maskt", [16, 128, 512], BF16)
    idend = din("iden", [128, 128], BF16)

    ydev = nc.dram_tensor("y", [D, Q], F32, kind="ExternalOutput").ap()
    sck = "ExternalOutput" if expose else "Internal"
    modsc = nc.dram_tensor("modsc", [6 * D], F32, kind=sck).ap()
    avhs = nc.dram_tensor("avhs", [D, Q], BF16, kind="Internal").ap()
    avls = nc.dram_tensor("avls", [D, Q], BF16, kind="Internal").ap()
    avsc = nc.dram_tensor("avsc", [D, Q], F32, kind=sck).ap()
    x1sc = nc.dram_tensor("x1sc", [D, Q], F32, kind=sck).ap()
    o0sc = nc.dram_tensor("o0sc", [D, Q], F32, kind=sck).ap()
    dmp = {}
    if expose:
        for nm, shp in (("modT", [128, 96]), ("der", [128, 64]),
                        ("K", [128, 4 * S]), ("V", [128, 16 * 512]),
                        ("H", [128, DC * Q]), ("H2", [128, DC * Q])):
            dmp[nm] = nc.dram_tensor("dump_" + nm, shp, F32,
                                     kind="ExternalOutput").ap()

    O_ROWS = 4096
    O_BCAST = 16384
    BIG = 22528
    ARENA = BIG + 180224 + 8192
    XTRA = BIG + 180224
    arena = es.enter_context(nc.sbuf_tensor("arena", [128, ARENA], U8))

    def buf(off, n, dt=F32):
        sz = mybir.dt.size(dt)
        assert off % 4 == 0 and off + n * sz <= ARENA, (off, n)
        return arena[:, off:off + n * sz].bitcast(dt)

    ones = buf(0, 128)
    modT = buf(512, 96)
    der = buf(896, 64)
    siluc = buf(1152, 16)
    cfs = buf(1216, 16)
    qg, kg = buf(1280, 1), buf(1288, 1)
    b0a_s, b1a_s = buf(1296, FC), buf(1424, FC)
    b0b_s, b1b_s = buf(1552, DC), buf(1616, DC)
    wg0, wg1, wgd = buf(1680, 16), buf(1744, 16), buf(1808, 16)
    tmpa, tmpb = buf(1872, 16), buf(1936, 16)
    ones_bf = buf(2192, 64, BF16)
    iden_bf = buf(2320, 128, BF16)

    def row(o, n):
        assert o + n <= 3072
        return buf(O_ROWS + o * 4, n)[0:1, :]
    bc = buf(O_BCAST, 1536)

    Kbuf = buf(BIG + 0, 4 * S)

    def Khl(hl, kvc, c):
        base = BIG + (kvc * 4 + c // 4) * 2048 + hl * 1024 + (c % 4) * 256
        return buf(base, 128, BF16)
    Vhi = buf(BIG + 32768, 16 * 512, BF16)
    Vlo = buf(BIG + 49152, 16 * 512, BF16)

    def Hap(hl, c, qt):
        return buf(BIG + 65536 + c * 4096 + qt * 2048 + hl * 1024, 512, BF16)
    TR = BIG + 131072
    wmod_s = [buf(TR + 16384 + i * 2048, 512) for i in range(4)]
    xh = [buf(TR, DC * TT), buf(TR + 16384, DC * TT)]
    x2s = [buf(TR + 32768 + i * 1024, TT) for i in range(2)]
    wk_s = [buf(TR + 34816 + i * 2048, 512) for i in range(2)]
    wv_s = [buf(TR + 38912 + i * 2048, 512) for i in range(2)]
    k2buf = buf(TR + 43008, 1024)
    t1c = [bc[:, 512 + i * 256:768 + i * 256] for i in range(2)]
    masks = buf(TR, 16 * 512, BF16)
    wq_s = [buf(TR + 16384, DC * 128)]
    wq_hi = buf(XTRA, DC * 128, BF16)
    wq_lo = buf(XTRA + 4096, DC * 128, BF16)
    qh_hi = [buf(TR + 24576 + i * 4096, 1024, BF16) for i in range(2)]
    qh_lo = [buf(TR + 26624 + i * 4096, 1024, BF16) for i in range(2)]
    x2q = [buf(O_ROWS + i * 2048, 512) for i in range(2)]
    pt_s = [buf(TR + 34816 + i * 2048, 512) for i in range(2)]
    pt_hi = [buf(TR + 38912 + i * 1024, 512, BF16) for i in range(3)]
    pt_lo = [buf(TR + 41984 + i * 1024, 512, BF16) for i in range(3)]
    avn_s = [buf(TR + 45056, 512)]
    avn_hi = [buf(TR + 47104, 512, BF16)]
    avn_lo = [buf(TR + 48128, 512, BF16)]
    xq_s = [buf(TR + i * 2048, 512) for i in range(2)]
    x1st_s = [buf(TR + 4096 + i * 2048, 512) for i in range(2)]
    x2w_s = [buf(TR + 8192 + i * 2048, 512) for i in range(2)]
    avqh = buf(TR + 16384, 16 * 512, BF16)
    avql_ = buf(TR + 32768, 16 * 512, BF16)
    wo_s = [buf(BIG + i * 8192, DC * 128) for i in range(2)]
    wo_hi = [buf(BIG + 16384 + i * 8192, DC * 128, BF16) for i in range(2)]
    wo_lo = [buf(BIG + 20480 + i * 8192, DC * 128, BF16) for i in range(2)]

    def H2ap(hl, c, qt):
        return buf(BIG + 65536 + c * 4096 + qt * 2048 + hl * 1024, 512, BF16)
    x1rd3_s = [buf(BIG + 32768 + i * 2048, 512) for i in range(2)]
    hid_hi = buf(BIG + 0, FC * 512, BF16)
    hid_lo = buf(BIG + 32768, FC * 512, BF16)
    t1c2 = [bc[:, 1024:1536], bc[:, 1024:1536]]
    w0a_f = [buf(TR + 0, DC * 128), buf(XTRA, DC * 128)]
    w0a_hi = [buf(TR + 8192 + i * 8192, DC * 128, BF16) for i in range(2)]
    w0a_lo = [buf(TR + 12288 + i * 8192, DC * 128, BF16) for i in range(2)]
    x1rd_s = [buf(TR + i * 2048, 512) for i in range(2)]
    o0st_s = [buf(TR + 4096 + i * 2048, 512) for i in range(2)]
    o0rd_s = [buf(TR + 8192 + i * 2048, 512) for i in range(2)]
    w0b_f = buf(TR + 24576, 16 * 128)
    w0b_hi = [buf(TR + 32768 + i * 8192, 16 * 128, BF16) for i in range(2)]
    w0b_lo = [buf(TR + 36864 + i * 8192, 16 * 128, BF16) for i in range(2)]
    wgd_hi = buf(2000, 16, BF16)
    wgd_lo = buf(2032, 16, BF16)
    nb0b = buf(2064, DC)
    nb1b = buf(2128, DC)

    ps = [es.enter_context(nc.psum_tensor(f"ps{i}", [128, 512], F32))
          for i in range(8)]

    P = Plan(nc, es)
    sT, sV, sC = P.sem("sT"), P.sem("sV"), P.sem("sC")
    sG = P.sem("sG")
    sst = P.sem("sst")
    s_sp = P.sem("ssp")
    r_wmod = Ring(P, "rwm", 4, "sync")
    r_xt = Ring(P, "rxt", 2, "gpsimd")
    r_wk = Ring(P, "rwk", 2, "sync")
    r_wv = Ring(P, "rwv", 2, "sync")
    r_wq = Ring(P, "rwq", 1, "sync")
    r_wqhi = Ring(P, "rwqhi", 1, "gpsimd")
    r_wo = Ring(P, "rwo", 2, "sync")
    r_wohi = Ring(P, "rwohi", 2, "gpsimd")
    avql = [P.sem(f"avq{i}") for i in range(16)]
    r_xq = Ring(P, "rxq", 2, "gpsimd")
    r_x1 = Ring(P, "rx1", 2, "gpsimd")
    r_w0a = Ring(P, "rw0a", 2, "sync")
    r_w0b = Ring(P, "rw0b", 2, "sync")
    r_wahi = Ring(P, "rwahi", 2, "gpsimd")
    r_wbhi = Ring(P, "rwbhi", 2, "gpsimd")
    r_o0 = Ring(P, "ro0", 2, "gpsimd")
    st_avn = [P.sem("stav0"), P.sem("stav1")]
    st_av32 = P.sem("stav32")
    st_x1 = [P.sem("stx10"), P.sem("stx11")]
    st_o0 = [P.sem("sto00"), P.sem("sto01")]

    def mm(out, lhsT, rhs, start, stop, wait=(), inc=()):
        P.op("tensor", "matmul", out, lhsT, rhs, wait=wait, inc=inc,
             start=start, stop=stop)

    ones_col = ones[:, 0:1]
    ones_row = ones[0:1, :]

    for dst, src in ((cfs, cfd), (qg, qgd), (kg, kgd), (iden_bf, idend),
                     (b0a_s, b0afd),
                     (b1a_s, b1afd), (b0b_s, b0bfd), (b1b_s, b1bfd)):
        P.op("sync", "dma_start", out=dst, in_=src, inc=[(sG, 16)])
    P.op("sync", "dma_start", out=wg0,
         in_=wgr[0:1, :].rearrange("o (a p) -> (o p) a", p=128),
         allow_slow_non_contiguous=True, inc=[(sG, 16)])
    P.op("sync", "dma_start", out=wg1,
         in_=wgr[1:2, :].rearrange("o (a p) -> (o p) a", p=128),
         allow_slow_non_contiguous=True, inc=[(sG, 16)])
    g0 = P.tok(sG)
    P.op("vector", "memset", ones, 1.0)
    P.op("vector", "memset", ones_bf, 1.0, inc=[(sV, 1)])
    P.op("vector", "tensor_sub", wgd, wg0, wg1, wait=[g0], inc=[(sV, 1)])
    P.op("scalar", "activation", siluc, cfs, AF.Silu, wait=[g0], inc=[(sC, 1)])
    silu_tok = P.tok(sC)
    ones_tok = P.tok(sV)

    P.label = "mod"
    n_ft = 6 * D // 512
    mrow = [row(0, 512), row(512, 512)]
    prev_evac = None
    def _wmod_load(ft, c):
        return r_wmod.load(wmod_s[(ft * DC + c) % 4],
                           wmodd[c * 128:(c + 1) * 128, ft * 512:(ft + 1) * 512])
    toks = {c: _wmod_load(0, c) for c in range(4)}
    for ft in range(n_ft):
        for c in range(DC):
            w = [toks.pop(c)]
            if c == 0:
                w += [silu_tok, prev_evac]
            mm(ps[0][0:1, :], siluc[:, c:c + 1], wmod_s[(ft * DC + c) % 4],
               start=(c == 0), stop=(c == DC - 1), wait=w, inc=[(sT, 1)])
            r_wmod.set_free(ft * DC + c, P.tok(sT))
            nc_, nft = c + 4, ft
            if nc_ >= DC:
                nc_, nft = nc_ - DC, ft + 1
            if nft < n_ft:
                toks[nc_] = _wmod_load(nft, nc_)
        slot = mrow[ft % 2]
        w = [P.tok(sT)]
        if ft >= 2:
            w.append((s_sp, 16 * (ft - 1)))
        P.op("scalar", "copy", slot, ps[0][0:1, :], wait=w, inc=[(sC, 1)])
        prev_evac = P.tok(sC)
        P.op("sync", "dma_start", out=modsc[ft * 512:(ft + 1) * 512], in_=slot,
             wait=[prev_evac], inc=[(s_sp, 16)])
    P.op("sync", "dma_start", out=modT,
         in_=modsc.rearrange("(a p) -> p a", p=128),
         allow_slow_non_contiguous=True, wait=[(s_sp, 16 * n_ft)], inc=[(sG, 16)])
    gm = P.tok(sG)
    ca1, gac, cb1, gbc = der[:, 0:16], der[:, 16:32], der[:, 32:48], der[:, 48:64]
    sa_v, ca_v, ga_v = modT[:, 0:16], modT[:, 16:32], modT[:, 32:48]
    sb_v, cb_v, gb_v = modT[:, 48:64], modT[:, 64:80], modT[:, 80:96]
    P.op("vector", "tensor_scalar_add", ca1, ca_v, 1.0, wait=[gm])
    P.op("vector", "tensor_scalar_add", cb1, cb_v, 1.0)
    for g_in, g_out in ((ga_v, gac), (gb_v, gbc)):
        P.op("vector", "tensor_single_scalar", tmpa, g_in, 0.0, ALU.is_ge)
        P.op("vector", "tensor_scalar", tmpa, tmpa, 2.0, 1.0, ALU.mult, ALU.subtract)
        P.op("vector", "tensor_mul", tmpb, g_in, tmpa)
        P.op("vector", "tensor_scalar_max", tmpb, tmpb, 0.25)
        P.op("vector", "tensor_mul", g_out, tmpb, tmpa)
    P.op("vector", "tensor_copy", wgd_hi, wgd)
    P.op("vector", "tensor_tensor", wgd_lo, wgd, wgd_hi, ALU.subtract)
    P.op("vector", "tensor_scalar", nb0b, b0b_s, -1.0, None, ALU.mult)
    P.op("vector", "tensor_scalar", nb1b, b1b_s, -1.0, None, ALU.mult)
    P.op("vector", "engine_nop", inc=[(sV, 1)])
    mod_tok = P.tok(sV)
    wmod_alldone = P.tok(sT)

    if expose:
        P.op("sync", "dma_start", out=dmp["modT"], in_=modT, wait=[mod_tok],
             inc=[(sst, 16)])
        P.op("sync", "dma_start", out=dmp["der"], in_=der, wait=[mod_tok],
             inc=[(sst, 16)])
    if stage == "mod":
        return _finish(P, nc, es, sst)

    st2 = {"x2free": [None, None], "bc_copy": None, "kevac": None,
           "vevac": None, "bc6_free": None}
    P.label = "ph2"
    NT_ALL = NTILE + Q // TT
    r_xt.free_tok[1] = wmod_alldone
    pend_kv = [None]
    ln2_eps = buf(2576, 1)[0:1, :]
    rms_eps = buf(2584, 1)[0:1, :]
    P.op("vector", "memset", ln2_eps, EPS_LN)
    P.op("vector", "memset", rms_eps, EPS_RMS, inc=[(sV, 1)])
    eps_tok = P.tok(sV)

    def emit_kv(t, x_ap, htok, htoks):
        t0 = t * TT
        wk_t = {c: r_wk.load(wk_s[c % 2], Wkd[c * 128:(c + 1) * 128, :])
                for c in range(2)}
        wv_t = {c: r_wv.load(wv_s[c % 2], Wvd[c * 128:(c + 1) * 128, :])
                for c in range(2)}
        for c in range(DC):
            hc = x_ap[:, c * TT:(c + 1) * TT]
            for kvc in range(4):
                w = []
                if kvc == 0:
                    w = [wk_t[c], htoks[c]]
                    if c == 0:
                        w += [st2["kevac"]]
                mm(ps[kvc][:, 0:TT], wk_s[c % 2][:, kvc * 128:(kvc + 1) * 128],
                   hc, start=(c == 0), stop=(c == DC - 1), wait=w,
                   inc=[(sT, 1)] if kvc == 3 else ())
            r_wk.set_free(t * DC + c, P.tok(sT))
            for tc in range(2):
                w = []
                if tc == 0:
                    w = [wv_t[c]]
                    if c == 0:
                        w += [st2["vevac"], st2["bc_copy"]]
                mm(ps[6 + tc][:, :], hc[:, tc * 128:(tc + 1) * 128], wv_s[c % 2],
                   start=(c == 0), stop=(c == DC - 1), wait=w,
                   inc=[(sT, 1)] if tc == 1 else ())
            r_wv.set_free(t * DC + c, P.tok(sT))
            if c + 2 < DC:
                wk_t[c + 2] = r_wk.load(wk_s[c % 2],
                                        Wkd[(c + 2) * 128:(c + 3) * 128, :])
                wv_t[c + 2] = r_wv.load(wv_s[c % 2],
                                        Wvd[(c + 2) * 128:(c + 3) * 128, :])
        kvtok = P.tok(sT)
        for kvc in range(4):
            P.op("scalar", "copy", Kbuf[:, kvc * S + t0:kvc * S + t0 + TT],
                 ps[kvc][:, 0:TT], wait=[kvtok] if kvc == 0 else [],
                 inc=[(sC, 1)])
        st2["kevac"] = P.tok(sC)
        for tc in range(2):
            kc = t0 // 128 + tc
            P.op("scalar", "copy", Vhi[:, kc * 512:(kc + 1) * 512],
                 ps[6 + tc][:, :], inc=[(sC, 1)])
            P.op("vector", "tensor_tensor", Vlo[:, kc * 512:(kc + 1) * 512],
                 ps[6 + tc][:, :], Vhi[:, kc * 512:(kc + 1) * 512],
                 ALU.subtract, wait=[P.tok(sC)], inc=[(sV, 1)])
        st2["vevac"] = P.tok(sV)
        return kvtok

    TILE_ORDER = [0, 1, 2, 3, NTILE, 4, NTILE + 1, 5, NTILE + 2, 6,
                  NTILE + 3, 7]
    for ti, t in enumerate(TILE_ORDER):
        own = t >= NTILE
        t0 = (t - NTILE) * TT if own else t * TT
        x_ap = xh[ti % 2]
        src_d = xq if own else xfeat
        xtok = r_xt.load(
            x_ap.rearrange("p (c t) -> p c t", t=TT),
            src_d.rearrange("(c p) t -> p c t", p=128)[:, :, t0:t0 + TT])
        for c in range(DC):
            xc = x_ap[:, c * TT:(c + 1) * TT]
            x2 = x2s[c % 2][:, 0:TT]
            P.op("scalar", "activation", x2, xc, AF.Square,
                 wait=[xtok, st2["x2free"][c % 2]], inc=[(sC, 1)])
            x2tok = P.tok(sC)
            w_mean = [xtok, ones_tok] if c == 0 else []
            mm(ps[4][0:1, 0:TT], ones_col, xc, start=(c == 0),
               stop=(c == DC - 1), wait=w_mean)
            mm(ps[5][0:1, 0:TT], ones_col, x2, start=(c == 0),
               stop=(c == DC - 1), wait=[x2tok], inc=[(sT, 1)])
            st2["x2free"][c % 2] = P.tok(sT)
        stats_tok = P.tok(sT)
        if pend_kv[0] is not None:
            pend_kv[0]()
            pend_kv[0] = None
        r_m, r_t = row(1024, TT), row(1280, TT)
        r_rstd, r_shift = row(1536, TT), row(1536 + TT, TT)
        P.op("vector", "tensor_scalar", r_m, ps[4][0:1, 0:TT], 1.0 / D, None,
             ALU.mult, wait=[stats_tok, eps_tok])
        P.op("vector", "tensor_mul", r_t, r_m, r_m)
        P.op("vector", "scalar_tensor_tensor", r_rstd, ps[5][0:1, 0:TT],
             1.0 / D, r_t, ALU.mult, ALU.subtract, inc=[(sV, 1)])
        P.op("scalar", "activation", r_rstd, r_rstd, AF.Ln,
             bias=ln2_eps, wait=[P.tok(sV)])
        P.op("scalar", "activation", r_rstd, r_rstd, AF.Exp, scale=-0.5,
             inc=[(sC, 1)])
        P.op("vector", "scalar_tensor_tensor", r_shift, r_m, -1.0, r_rstd,
             ALU.mult, ALU.mult, wait=[P.tok(sC)], inc=[(sV, 1)])
        mm(ps[6][:, 0:2 * TT], ones_row, row(1536, 2 * TT), start=True,
           stop=True, wait=[P.tok(sV), st2["bc_copy"], st2["vevac"],
                            st2.get("h_done")], inc=[(sT, 1)])
        P.op("scalar", "copy", bc[:, 0:2 * TT], ps[6][:, 0:2 * TT],
             wait=[P.tok(sT)], inc=[(sC, 1)])
        st2["bc_copy"] = P.tok(sC)
        bctok = st2["bc_copy"]
        RSTD, SHIFT = bc[:, 0:TT], bc[:, TT:2 * TT]
        htoks = []
        for c in range(DC):
            xc = x_ap[:, c * TT:(c + 1) * TT]
            hd = xc
            w = [bctok]
            if t == 0 and c == 0:
                w.append(mod_tok)
            P.op("scalar", "activation", t1c[c % 2], SHIFT, AF.Identity,
                 scale=ca1[:, c:c + 1], bias=sa_v[:, c:c + 1],
                 wait=[bctok] + ([htoks[c - 2]] if c >= 2 else []),
                 inc=[(sC, 1)])
            c2tok = P.tok(sC)
            P.op("vector", "scalar_tensor_tensor", hd, xc, ca1[:, c:c + 1],
                 RSTD, ALU.mult, ALU.mult, wait=w)
            P.op("vector", "tensor_add", hd, hd, t1c[c % 2], wait=[c2tok],
                 inc=[(sV, 1)])
            if own:
                qt_half = (t0 // 512)
                off = t0 % 512
                hhi = buf(BIG + 65536 + c * 4096 + qt_half * 2048, 512, BF16)[:, off:off + TT]
                hlo = buf(BIG + 65536 + c * 4096 + qt_half * 2048 + 1024, 512, BF16)[:, off:off + TT]
                P.op("vector", "tensor_copy", hhi, xc)
                P.op("vector", "tensor_tensor", hlo, xc, hhi, ALU.subtract,
                     inc=[(sV, 1)])
            htoks.append(P.tok(sV))
        st2["h_done"] = P.tok(sV)
        st2["htoks"] = htoks
        htok = P.tok(sV)
        if own:
            r_xt.set_free(ti, htok)
        else:
            def mk(t=t, ti=ti, x_ap=x_ap, htok=htok,
                   htoks=tuple(st2["htoks"])):
                def go():
                    kvtok = emit_kv(t, x_ap, htok, htoks)
                    r_xt.set_free(ti, kvtok)
                return go
            pend_kv[0] = mk()
    if pend_kv[0] is not None:
        pend_kv[0]()
    hown_tok = P.tok(sV)

    kv_all = st2["vevac"]
    units = [(kvc, half) for kvc in range(4) for half in range(2)]
    prev_stt = hown_tok
    ktmp = wk_s[0]
    for i, (kvc, half) in enumerate(units):
        k2 = k2buf
        P.op("scalar", "activation", k2, Kbuf[:, kvc * S + half * 1024:
             kvc * S + (half + 1) * 1024], AF.Square,
             wait=[kv_all] + ([P.tok(sV)] if i > 0 else [st2["h_done"]]),
             inc=[(sC, 1)])
        k2tok = P.tok(sC)
        for blk in range(2):
            b = half * 2 + blk
            sl = slice(blk * 512, (blk + 1) * 512)
            mm(ps[6][0:1, :], ones_col, k2[:, sl], start=True, stop=True,
               wait=[k2tok, P.tok(sC)], inc=[(sT, 1)])
            r_r = row(1024, 512)
            P.op("scalar", "activation", r_r, ps[6][0:1, :], AF.Ln,
                 scale=1.0 / DH, bias=rms_eps, wait=[P.tok(sT)])
            P.op("scalar", "activation", r_r, r_r, AF.Exp, scale=-0.5,
                 inc=[(sC, 1)])
            mm(ps[7][:, :], ones_row, r_r, start=True, stop=True,
               wait=[P.tok(sC), prev_stt], inc=[(sT, 1)])
            P.op("scalar", "copy", bc[:, 1024:1536], ps[7][:, :],
                 wait=[P.tok(sT)], inc=[(sC, 1)])
            prev_bc = P.tok(sC)
            base = kvc * S + half * 1024 + blk * 512
            kc = Kbuf[:, base:base + 512]
            P.op("vector", "scalar_tensor_tensor", ktmp, kc, kg,
                 bc[:, 1024:1536], ALU.mult, ALU.mult, wait=[prev_bc])
            khi = buf(BIG + (kvc * 4 + b) * 2048, 512, BF16)
            klo = buf(BIG + (kvc * 4 + b) * 2048 + 1024, 512, BF16)
            P.op("vector", "tensor_copy", khi, ktmp)
            P.op("vector", "tensor_tensor", klo, ktmp, khi, ALU.subtract,
                 inc=[(sV, 1)])
            prev_stt = P.tok(sV)
    krms_tok = P.tok(sV)

    if expose:
        P.op("sync", "dma_start", out=dmp["H"],
             in_=buf(BIG + 65536, DC * Q), wait=[hown_tok], inc=[(sst, 16)])
    if stage == "ph2":
        return _finish(P, nc, es, sst)

    P.label = "attn"
    P.op("sync", "dma_start", out=masks.rearrange("p (i n) -> p i n", n=512),
         in_=maskd.rearrange("i p n -> p i n"), wait=[krms_tok], inc=[(sG, 16)])
    mask_tok = P.tok(sG)

    st3 = {
        "qfree": [hown_tok, hown_tok],
        "wq_pair_free": krms_tok,
        "avfree": krms_tok,
        "sbank_free": [krms_tok, krms_tok],
        "ptf_free": [krms_tok, krms_tok],
        "pthl_free": [krms_tok, krms_tok, krms_tok],
        "avslot_free": None,
        "bc_user": [krms_tok, krms_tok],
        "ps6_free": krms_tok,
        "qh_toks": [None, None],
        "x2q_free": [krms_tok, krms_tok],
    }

    def emit_stage_a(h):
        wsrc_q = Wqd.rearrange("(c p) e -> p c e", p=128)[:, :, h * 128:(h + 1) * 128]
        wq_tok = r_wq.load(
            wq_s[0].rearrange("p (c e) -> p c e", e=128), wsrc_q,
            extra_wait=[krms_tok] if h < 1 else ())
        wqhi_tok = r_wqhi.load(
            wq_hi[:].rearrange("p (c e) -> p c e", e=128), wsrc_q,
            extra_wait=[krms_tok] if h < 1 else ())
        P.op("vector", "tensor_tensor", wq_lo, wq_s[0], wq_hi, ALU.subtract,
             wait=[wq_tok, wqhi_tok, st3["wq_pair_free"]], inc=[(sV, 1)])
        wq_split = P.tok(sV)
        r_wq.set_free(h, wq_split)
        qmm_toks = []
        for qt in range(2):
            for c in range(DC):
                w = []
                if c == 0:
                    w = [wq_split, st3["qfree"][qt]]
                    if h == 0:
                        w.append(hown_tok)
                whi = wq_hi[:, c * 128:(c + 1) * 128]
                wlo = wq_lo[:, c * 128:(c + 1) * 128]
                mm(ps[4 + qt][:, :], whi, Hap(0, c, qt),
                   start=(c == 0), stop=False, wait=w)
                mm(ps[4 + qt][:, :], whi, Hap(1, c, qt), start=False,
                   stop=False)
                mm(ps[4 + qt][:, :], wlo, Hap(0, c, qt), start=False,
                   stop=(c == DC - 1),
                   inc=[(sT, 1)] if c == DC - 1 else ())
            qmm_toks.append(P.tok(sT))
        r_wqhi.set_free(h, qmm_toks[1])
        st3["wq_pair_free"] = qmm_toks[1]
        sq_toks = []
        for qt in range(2):
            P.op("scalar", "activation", x2q[qt], ps[4 + qt][:, :], AF.Square,
                 wait=[qmm_toks[qt], st3["x2q_free"][qt]], inc=[(sC, 1)])
            sq_toks.append(P.tok(sC))
        return qmm_toks, sq_toks

    def mk_rms_steps(h, qmm_toks, sq_toks):
        state = {}

        def msq(qt):
            def go():
                w = [sq_toks[qt], st3["ps6_free"]]
                mm(ps[6][0:1, :], ones_col, x2q[qt], start=True, stop=True,
                   wait=w, inc=[(sT, 1)])
                state["msq%d" % qt] = P.tok(sT)
                st3["x2q_free"][qt] = state["msq%d" % qt]
                r_r = row(1024 + qt * 512, 512)
                P.op("scalar", "activation", r_r, ps[6][0:1, :], AF.Ln,
                     scale=1.0 / DH, bias=rms_eps, wait=[state["msq%d" % qt]])
                P.op("scalar", "activation", r_r, r_r, AF.Exp, scale=-0.5,
                     inc=[(sC, 1)])
                state["ln%d" % qt] = P.tok(sC)
            return go

        def bccp(qt):
            def go():
                mm(ps[6][:, :], ones_row, row(1024 + qt * 512, 512),
                   start=True, stop=True, wait=[state["ln%d" % qt]],
                   inc=[(sT, 1)])
                P.op("scalar", "copy", bc[:, qt * 512:qt * 512 + 512],
                     ps[6][:, :], wait=[P.tok(sT), st3["bc_user"][qt]],
                     inc=[(sC, 1)])
                state["cp%d" % qt] = P.tok(sC)
                st3["ps6_free"] = state["cp%d" % qt]
            return go

        def qhat():
            for qt in range(2):
                qsl = slice(qt * 512, qt * 512 + 512)
                P.op("vector", "scalar_tensor_tensor",
                     ps[4 + qt][:, :], ps[4 + qt][:, :], qg,
                     bc[:, qt * 512:qt * 512 + 512], ALU.mult, ALU.mult,
                     wait=[state["cp%d" % qt]])
                P.op("vector", "tensor_copy", qh_hi[h % 2][:, qsl],
                     ps[4 + qt][:, :])
                P.op("vector", "tensor_tensor", qh_lo[h % 2][:, qsl],
                     ps[4 + qt][:, :], qh_hi[h % 2][:, qsl], ALU.subtract,
                     inc=[(sV, 1)])
                st3["qh_next"][qt] = P.tok(sV)
                st3["bc_user"][qt] = st3["qh_next"][qt]
                st3["qfree"][qt] = st3["qh_next"][qt]
        return [msq(0), bccp(0), msq(1), bccp(1)], qhat

    def emit_chunks(h, qt, inject):
        g = h // 4
        ncH = QT_CHUNKS[qt]
        qsl = slice(qt * 512, qt * 512 + 512)
        qhh = qh_hi[h % 2][:, qsl]
        qhl = qh_lo[h % 2][:, qsl]
        qtok = st3["qh_toks"][qt]
        exp_toks = {}
        plo_toks = {}

        def emit_hi(j):
            w_av = [exp_toks[j]]
            if j == 0:
                w_av.append(st3["avfree"])
            vs = slice(j * 512 + g * 128, j * 512 + (g + 1) * 128)
            mm(ps[2][:, :], Vhi[:, vs], pt_hi[j % 3],
               start=(j == 0), stop=False, wait=w_av)
            mm(ps[2][:, :], Vlo[:, vs], pt_hi[j % 3],
               start=False, stop=False)
            mm(ps[3][0:1, :], ones_bf[:, 0:1], pt_hi[j % 3],
               start=(j == 0), stop=False, inc=[(sT, 1)])

        def emit_lo(j):
            vs = slice(j * 512 + g * 128, j * 512 + (g + 1) * 128)
            mm(ps[2][:, :], Vhi[:, vs], pt_lo[j % 3],
               start=False, stop=(j == ncH - 1), wait=[plo_toks[j]])
            mm(ps[3][0:1, :], ones_bf[:, 0:1], pt_lo[j % 3],
               start=False, stop=(j == ncH - 1), inc=[(sT, 1)])
            st3["pthl_free"][j % 3] = P.tok(sT)

        for c in range(ncH):
            if c >= 2:
                emit_hi(c - 2)
            if c >= 3:
                emit_lo(c - 3)
            if inject and c in inject:
                inject[c]()
            sb = ps[c % 2]
            masked = (qt == 0) or (c >= 8)
            w = [qtok, st3["sbank_free"][c % 2]]
            if h == 0 and qt == 0 and c < 2:
                w.append(mask_tok)
            if masked:
                mm(sb[:, :], iden_bf, masks[:, c * 512:(c + 1) * 512],
                   start=True, stop=False,
                   wait=[st3["sbank_free"][c % 2]] +
                   ([mask_tok, ones_tok] if h == 0 and qt == 0 and c < 2 else []))
            mm(sb[:, :], Khl(0, g, c), qhh, start=not masked, stop=False,
               wait=w)
            mm(sb[:, :], Khl(0, g, c), qhl, start=False, stop=False)
            mm(sb[:, :], Khl(1, g, c), qhh, start=False, stop=True,
               inc=[(sT, 1)])
            smm_tok = P.tok(sT)
            P.op("scalar", "activation", pt_s[c % 2], sb[:, :], AF.Exp,
                 scale=SCALE, wait=[smm_tok, st3["ptf_free"][c % 2]])
            P.op("scalar", "activation", pt_hi[c % 3], sb[:, :], AF.Exp,
                 scale=SCALE, wait=[st3["pthl_free"][c % 3]], inc=[(sC, 1)])
            exp_toks[c] = P.tok(sC)
            st3["sbank_free"][c % 2] = exp_toks[c]
            P.op("vector", "tensor_tensor", pt_lo[c % 3], pt_s[c % 2],
                 pt_hi[c % 3], ALU.subtract, wait=[exp_toks[c]],
                 inc=[(sV, 1)])
            plo_toks[c] = P.tok(sV)
            st3["ptf_free"][c % 2] = plo_toks[c]
        emit_hi(ncH - 2)
        emit_lo(ncH - 3)
        emit_hi(ncH - 1)
        emit_lo(ncH - 2)
        emit_lo(ncH - 1)
        den_tok = P.tok(sT)
        r_d = row(1536, 512)
        P.op("vector", "reciprocal", r_d, ps[3][0:1, :], wait=[den_tok],
             inc=[(sV, 1)])
        mm(ps[7][:, :], ones_row, r_d, start=True, stop=True,
           wait=[P.tok(sV), st3["rdb_free"]], inc=[(sT, 1)])
        P.op("scalar", "copy", bc[:, 1024:1536], ps[7][:, :],
             wait=[P.tok(sT)], inc=[(sC, 1)])
        rdb_tok = P.tok(sC)
        P.op("vector", "tensor_tensor", avn_s[0], ps[2][:, :],
             bc[:, 1024:1536], ALU.mult, wait=[rdb_tok] +
             ([(st_av32, st_av32["n"])] if expose else []))
        P.op("vector", "tensor_copy", avn_hi[0], avn_s[0],
             wait=[st3["avslot_free"]])
        P.op("vector", "tensor_tensor", avn_lo[0], avn_s[0], avn_hi[0],
             ALU.subtract, inc=[(sV, 1)])
        st3["avfree"] = P.tok(sV)
        st3["rdb_free"] = st3["avfree"]
        P.op("sync", "dma_start",
             out=avhs[h * 128:(h + 1) * 128, qt * 512:qt * 512 + 512],
             in_=avn_hi[0], wait=[st3["avfree"]], inc=[(st_avn[0], 16)])
        P.op("sync", "dma_start",
             out=avls[h * 128:(h + 1) * 128, qt * 512:qt * 512 + 512],
             in_=avn_lo[0], wait=[st3["avfree"]], inc=[(st_avn[0], 16)])
        if expose:
            P.op("sync", "dma_start",
                 out=avsc[h * 128:(h + 1) * 128, qt * 512:qt * 512 + 512],
                 in_=avn_s[0], wait=[st3["avfree"]], inc=[(st_av32, 16)])
        st3["avslot_free"] = (st_avn[0], st_avn[0]["n"])

    st3["rdb_free"] = krms_tok
    st3["qh_next"] = [None, None]
    qmm0, sq0 = emit_stage_a(0)
    steps0, qhat0 = mk_rms_steps(0, qmm0, sq0)
    for s in steps0:
        s()
    qhat0()
    st3["qh_toks"] = list(st3["qh_next"])
    for h in range(NH):
        nxt = None
        if h + 1 < NH:
            qmms, sqs = emit_stage_a(h + 1)
            steps, qhat_n = mk_rms_steps(h + 1, qmms, sqs)
            nxt = {4: steps[0], 8: steps[1], 11: steps[2], 14: steps[3]}
        emit_chunks(h, 0, None)
        emit_chunks(h, 1, nxt)
        if h + 1 < NH:
            qhat_n()
            st3["qh_toks"] = list(st3["qh_next"])
    avfree = st3["avfree"]
    qfree = st3["qfree"]
    msq_free = [st3["ps6_free"], st3["rdb_free"]]

    if stage == "attn":
        return _finish(P, nc, es, sst)


    P.label = "wo"
    av_total = [(st_avn[0], st_avn[0]["n"]), (st_avn[1], st_avn[1]["n"])]
    wofree = [avfree, avfree]
    wo_split_prev = [avfree, avfree]
    wo_pair_free = [avfree, avfree]
    x1slot_free = [(), ()]
    stats_user = [qfree[0], qfree[1]]
    ps6_user = msq_free[0]
    ps7_user = msq_free[1]
    h2_toks = [None, None]
    ln2rows_tok = [None, None]
    x1q_tokens = [None, None]
    x1_uses = [0]
    h2_last_v = [None]
    h2_cp = [None, None]

    def h2_prologue(qt, bc_gate):
        r_mean = row(1024 + qt * 512, 512)
        r_rstd = row(2048 + qt * 512, 512)
        s0 = row(0, 512)
        P.op("vector", "tensor_scalar", s0, r_mean, -1.0, None, ALU.mult,
             wait=[ln2rows_tok[qt]])
        P.op("vector", "tensor_mul", s0, s0, r_rstd, inc=[(sV, 1)])
        sh_tok = P.tok(sV)
        mm(ps[6][:, :], ones_row, r_rstd, start=True, stop=True,
           wait=[ln2rows_tok[qt], bc_gate[0]], inc=[(sT, 1)])
        mm(ps[7][:, :], ones_row, s0, start=True, stop=True,
           wait=[sh_tok, bc_gate[1]], inc=[(sT, 1)])
        P.op("scalar", "copy", bc[:, 0:512], ps[6][:, :],
             wait=[(sT, P.tok(sT)[1] - 1)])
        P.op("scalar", "copy", bc[:, 512:1024], ps[7][:, :],
             wait=[P.tok(sT)], inc=[(sC, 1)])
        return P.tok(sC)

    def h2_chunk(qt, c, bc2tok):
        x1tok = r_x1.load(
            x1rd3_s[c % 2],
            x1sc[c * 128:(c + 1) * 128, qt * 512:qt * 512 + 512],
            extra_wait=x1q_tokens[qt] if c < 2 else ())
        pt = ps[6 + c % 2]
        P.op("vector", "scalar_tensor_tensor", pt[:, :], x1rd3_s[c % 2],
             cb1[:, c:c + 1], bc[:, 0:512], ALU.mult, ALU.mult,
             wait=[x1tok, bc2tok, h2_cp[c % 2]])
        P.op("scalar", "activation", t1c2[0], bc[:, 512:1024], AF.Identity,
             scale=cb1[:, c:c + 1], bias=sb_v[:, c:c + 1],
             wait=[bc2tok] + ([h2_last_v[0]] if h2_last_v[0] else []),
             inc=[(sC, 1)])
        P.op("vector", "tensor_add", pt[:, :], pt[:, :], t1c2[0],
             wait=[P.tok(sC)], inc=[(sV, 1)])
        P.op("scalar", "copy", H2ap(0, c, qt), pt[:, :],
             wait=[P.tok(sV)], inc=[(sC, 1)])
        h2_cp[c % 2] = P.tok(sC)
        P.op("vector", "tensor_tensor", H2ap(1, c, qt), pt[:, :],
             H2ap(0, c, qt), ALU.subtract, wait=[P.tok(sC)], inc=[(sV, 1)])
        h2_last_v[0] = P.tok(sV)
        x1_uses[0] += 1
        r_x1.set_free(x1_uses[0] - 1, h2_last_v[0])

    h2q0 = [None]
    wo_prev_qt = None
    for qt in range(2):
        avq_toks = []
        for c2 in range(16):
            w = list(av_total) if (qt == 0 and c2 == 0) else []
            if qt == 1 and c2 == 0:
                w = [wo_prev_qt]
            P.op("gpsimd", "dma_start",
                 out=avqh[:, c2 * 512:(c2 + 1) * 512],
                 in_=avhs[c2 * 128:(c2 + 1) * 128, qt * 512:qt * 512 + 512],
                 wait=w, inc=[(avql[c2], 16)])
            P.op("gpsimd", "dma_start",
                 out=avql_[:, c2 * 512:(c2 + 1) * 512],
                 in_=avls[c2 * 128:(c2 + 1) * 128, qt * 512:qt * 512 + 512],
                 inc=[(avql[c2], 16)])
            avq_toks.append((avql[c2], 32 * (qt + 1)))
        last_mm = None
        for do in range(DC):
            wsrc_o = Wod.rearrange("(c p) e -> p c e", p=128)[:, :, do * 128:(do + 1) * 128]
            wo_tok = r_wo.load(
                wo_s[do % 2].rearrange("p (c e) -> p c e", e=128), wsrc_o,
                extra_wait=[wo_split_prev[do % 2]])
            wohi_tok = r_wohi.load(
                wo_hi[do % 2][:].rearrange("p (c e) -> p c e", e=128), wsrc_o,
                extra_wait=[wo_pair_free[do % 2]])
            P.op("vector", "tensor_tensor", wo_lo[do % 2], wo_s[do % 2],
                 wo_hi[do % 2], ALU.subtract, wait=[wo_tok, wohi_tok],
                 inc=[(sV, 1)])
            wo_split_prev[do % 2] = P.tok(sV)
            for c in range(DC):
                w = [avq_toks[c]] if do == 0 else []
                if c == 0:
                    w += [wo_split_prev[do % 2], wofree[do % 2]]
                whi = wo_hi[do % 2][:, c * 128:(c + 1) * 128]
                wlo = wo_lo[do % 2][:, c * 128:(c + 1) * 128]
                ah = avqh[:, c * 512:(c + 1) * 512]
                al = avql_[:, c * 512:(c + 1) * 512]
                mm(ps[do % 2][:, :], whi, ah, start=(c == 0), stop=False,
                   wait=w)
                mm(ps[do % 2][:, :], whi, al, start=False, stop=False)
                mm(ps[do % 2][:, :], wlo, ah, start=False, stop=(c == DC - 1),
                   inc=[(sT, 1)] if c == DC - 1 else ())
            womm_tok = P.tok(sT)
            last_mm = womm_tok
            r_wo.set_free(qt * DC + do, womm_tok)
            r_wohi.set_free(qt * DC + do, womm_tok)
            wo_pair_free[do % 2] = womm_tok
            xq_tok = r_xq.load(
                xq_s[do % 2],
                xq[do * 128:(do + 1) * 128, qt * 512:qt * 512 + 512])
            sl = do % 2
            x1sl = x1st_s[sl]
            P.op("vector", "tensor_tensor", x1sl, ps[do % 2][:, :],
                 Hap(0, do, qt), ALU.add,
                 wait=[womm_tok] + list(x1slot_free[sl]))
            P.op("vector", "tensor_tensor", x1sl, x1sl, Hap(1, do, qt),
                 ALU.add)
            P.op("vector", "scalar_tensor_tensor", x1sl, x1sl,
                 gac[:, do:do + 1], xq_s[sl], ALU.mult, ALU.add,
                 wait=[xq_tok], inc=[(sV, 1)])
            x1tok = P.tok(sV)
            r_xq.set_free(qt * DC + do, x1tok)
            wofree[do % 2] = x1tok
            P.op("scalar", "activation", x2w_s[sl], x1sl, AF.Square,
                 wait=[x1tok], inc=[(sC, 1)])
            x2wtok = P.tok(sC)
            w_m = [x1tok]
            if do == 0:
                w_m += stats_user
            mm(ps[4][0:1, :], ones_col, x1sl, start=(do == 0),
               stop=(do == DC - 1), wait=w_m)
            mm(ps[5][0:1, :], ones_col, x2w_s[sl], start=(do == 0),
               stop=(do == DC - 1), wait=[x2wtok], inc=[(sT, 1)])
            stats_tok = P.tok(sT)
            P.op("scalar", "dma_start",
                 out=x1sc[do * 128:(do + 1) * 128, qt * 512:qt * 512 + 512],
                 in_=x1sl, wait=[x1tok], inc=[(st_x1[sl], 16)])
            x1slot_free[sl] = ((st_x1[sl], st_x1[sl]["n"]), stats_tok)
            if qt == 1 and h2q0[0] is not None:
                h2_chunk(0, do, h2q0[0])
        wo_prev_qt = last_mm
        x1q_tokens[qt] = [(st_x1[0], st_x1[0]["n"]), (st_x1[1], st_x1[1]["n"])]
        r_mean = row(1024 + qt * 512, 512)
        r_rstd = row(2048 + qt * 512, 512)
        s1 = row(512, 512)
        P.op("vector", "tensor_scalar", r_mean, ps[4][0:1, :], 1.0 / D, None,
             ALU.mult, wait=[P.tok(sT)])
        P.op("vector", "tensor_mul", s1, r_mean, r_mean)
        P.op("vector", "scalar_tensor_tensor", r_rstd, ps[5][0:1, :], 1.0 / D,
             s1, ALU.mult, ALU.subtract, inc=[(sV, 1)])
        P.op("scalar", "activation", r_rstd, r_rstd, AF.Ln, bias=ln2_eps,
             wait=[P.tok(sV)])
        P.op("scalar", "activation", r_rstd, r_rstd, AF.Exp, scale=-0.5,
             inc=[(sC, 1)])
        P.op("vector", "engine_nop", wait=[P.tok(sC)], inc=[(sV, 1)])
        ln2rows_tok[qt] = P.tok(sV)
        stats_user = [P.tok(sV)]
        if qt == 0:
            h2q0[0] = h2_prologue(0, (ps6_user, ps7_user))
    t_3b_end = P.tok(sT)
    if stage == "wo":
        return _finish(P, nc, es, sst)

    P.label = "h2c"
    x1_total = [(st_x1[0], st_x1[0]["n"]), (st_x1[1], st_x1[1]["n"])]
    h2_toks[0] = h2_last_v[0]
    bcg = (h2_last_v[0], h2_last_v[0])
    bc2tok = h2_prologue(1, bcg)
    for c in range(DC):
        h2_chunk(1, c, bc2tok)
    h2_toks[1] = h2_last_v[0]
    h2_tok = h2_last_v[0]
    if expose:
        P.op("sync", "dma_start", out=dmp["H2"],
             in_=buf(BIG + 65536, DC * Q), wait=[h2_tok], inc=[(sst, 16)])
    if stage == "h2":
        return _finish(P, nc, es, sst)

    P.label = "ffn"
    gate_user = h2_tok
    hidfree = [h2_tok, h2_tok]
    outfree2 = [h2_tok, h2_tok, h2_tok, h2_tok]
    o0slot_free = [None, None]
    hidden_gate = h2_tok
    regA_gate = x1_total + [t_3b_end]
    regB_gate = [t_3b_end]
    x1_uses = 2 * DC
    wa_split_prev = h2_tok
    wa_hilo_free = [h2_tok, h2_tok]
    ps67_free = [h2_tok, h2_tok]
    wb_split_prev = h2_tok
    wb_hilo_free = [h2_tok, h2_tok]
    for qt in range(2):
        for c in range(DC):
            w = []
            if c == 0:
                w = [h2_toks[qt], gate_user, mod_tok]
            h2h = H2ap(0, c, qt)
            h2l = H2ap(1, c, qt)
            mm(ps[4][0:1, :], wgd_hi[:, c:c + 1], h2h,
               start=(c == 0), stop=False, wait=w)
            mm(ps[4][0:1, :], wgd_hi[:, c:c + 1], h2l, start=False, stop=False)
            mm(ps[4][0:1, :], wgd_lo[:, c:c + 1], h2h, start=False,
               stop=(c == DC - 1), inc=[(sT, 1)] if c == DC - 1 else ())
        r_g = row(0, 512)
        P.op("vector", "tensor_single_scalar", r_g, ps[4][0:1, :], 0.0,
             ALU.is_ge, wait=[P.tok(sT)], inc=[(sV, 1)])
        gate_user = P.tok(sV)
        mm(ps[5][:, :], ones_row, r_g, start=True, stop=True,
           wait=[P.tok(sV)] + (list(outfree2) if qt == 1 else []),
           inc=[(sT, 1)])
        mb_tok = P.tok(sT)
        Mb = ps[5]
        for e in range(2):
            WaT = W0aTd if e == 0 else W1aTd
            WbT = W0bTd if e == 0 else W1bTd
            ba = b0a_s if e == 0 else b1a_s
            bb_ = b0b_s if e == 0 else b1b_s
            actf = AF.Gelu if e == 0 else AF.Relu
            hid_toks = []
            hmm_toks = {}
            split_toks = {}
            for f in range(FC):
                ew = list(regA_gate) if f < 2 else []
                sl2 = f % 2
                wsrc = WaT.rearrange("(c p) f -> p c f", p=128)[:, :, f * 128:(f + 1) * 128]
                wa_tok = r_w0a.load(
                    w0a_f[sl2].rearrange("p (c e) -> p c e", e=128), wsrc,
                    extra_wait=ew)
                hi_tok = r_wahi.load(
                    w0a_hi[sl2].rearrange("p (c e) -> p c e", e=128), wsrc,
                    extra_wait=(ew if f < 2 else []) + [wa_hilo_free[sl2]])
                P.op("vector", "tensor_tensor", w0a_lo[sl2], w0a_f[sl2],
                     w0a_hi[sl2], ALU.subtract,
                     wait=[wa_tok, hi_tok] + (list(regA_gate) if f < 2 else []),
                     inc=[(sV, 1)])
                split_toks[f] = P.tok(sV)
                split_tok = split_toks[f]
                r_w0a.set_free((qt * 2 + e) * FC + f, split_tok)
                for c in range(DC):
                    w = []
                    if c == 0:
                        w = [split_tok, hidfree[sl2]]
                        if f == 0:
                            w.append(hidden_gate)
                    h2h = H2ap(0, c, qt)
                    h2l = H2ap(1, c, qt)
                    whi = w0a_hi[sl2][:, c * 128:(c + 1) * 128]
                    wlo = w0a_lo[sl2][:, c * 128:(c + 1) * 128]
                    mm(ps[sl2][:, :], whi, h2h, start=(c == 0), stop=False,
                       wait=w)
                    mm(ps[sl2][:, :], whi, h2l, start=False, stop=False)
                    mm(ps[sl2][:, :], wlo, h2h, start=False,
                       stop=(c == DC - 1),
                       inc=[(sT, 1)] if c == DC - 1 else ())
                hmm_tok = P.tok(sT)
                hmm_toks[f] = hmm_tok
                wa_hilo_free[sl2] = hmm_tok
                P.op("scalar", "activation", hid_hi[:, f * 512:(f + 1) * 512],
                     ps[sl2][:, :], actf, bias=ba[:, f:f + 1],
                     wait=[hmm_tok, ps67_free[sl2]])
                P.op("scalar", "activation", ps[6 + sl2][:, :], ps[sl2][:, :],
                     actf, bias=ba[:, f:f + 1],
                     wait=[outfree2[2 + sl2]] if f < 2 else (), inc=[(sC, 1)])
                P.op("vector", "tensor_tensor", hid_lo[:, f * 512:(f + 1) * 512],
                     ps[6 + sl2][:, :], hid_hi[:, f * 512:(f + 1) * 512],
                     ALU.subtract, wait=[P.tok(sC)], inc=[(sV, 1)])
                hid_toks.append(P.tok(sV))
                ps67_free[sl2] = hid_toks[-1]
                hidfree[sl2] = hid_toks[-1]
            hid_all = hid_toks[-1]
            outfree2[2] = ps67_free[0]
            outfree2[3] = ps67_free[1]
            gA_f32 = split_toks[FC - 2]
            gA_hilo = hmm_toks[FC - 2]
            for do in range(DC):
                ob = [2, 3, 6, 7][do % 4]
                omm_tok = None
                for fh in range(2):
                    ew = list(regB_gate) if (qt == 0 and e == 0 and do == 0) else []
                    wsrc = (WbT.rearrange("(c p) e -> p c e", p=128)
                            [:, fh * 16:(fh + 1) * 16, do * 128:(do + 1) * 128])
                    wb_tok = r_w0b.load(
                        w0b_f.rearrange("p (c e) -> p c e", e=128), wsrc,
                        extra_wait=ew + [wb_split_prev])
                    hi_tok = r_wbhi.load(
                        w0b_hi[fh][:].rearrange("p (c e) -> p c e", e=128), wsrc,
                        extra_wait=ew + [wb_hilo_free[fh]])
                    P.op("vector", "tensor_tensor", w0b_lo[fh], w0b_f,
                         w0b_hi[fh], ALU.subtract, wait=[wb_tok, hi_tok] + ew,
                         inc=[(sV, 1)])
                    wb_split_prev = P.tok(sV)
                    sb_tok = wb_split_prev
                    for c in range(16):
                        fidx = fh * 16 + c
                        w = []
                        if c == 0:
                            w = [sb_tok, hid_all]
                            if fh == 0:
                                w.append(outfree2[do % 4])
                        whi = w0b_hi[fh][:, c * 128:(c + 1) * 128]
                        wlo = w0b_lo[fh][:, c * 128:(c + 1) * 128]
                        hh = hid_hi[:, fidx * 512:(fidx + 1) * 512]
                        hl = hid_lo[:, fidx * 512:(fidx + 1) * 512]
                        mm(ps[ob][:, :], whi, hh,
                           start=(fidx == 0), stop=False, wait=w)
                        mm(ps[ob][:, :], whi, hl, start=False,
                           stop=False)
                        mm(ps[ob][:, :], wlo, hh, start=False,
                           stop=(fidx == FC - 1),
                           inc=[(sT, 1)] if c == 15 else ())
                    wb_hilo_free[fh] = P.tok(sT)
                    r_w0b.set_free((((qt * 2 + e) * DC) + do) * 2 + fh, P.tok(sT))
                omm_tok = P.tok(sT)
                sl = do % 2
                if e == 0:
                    P.op("vector", "tensor_scalar", o0st_s[sl], ps[ob][:, :],
                         bb_[:, do:do + 1], None, ALU.add,
                         wait=[omm_tok, o0slot_free[sl], gA_f32],
                         inc=[(sV, 1)])
                    ctok = P.tok(sV)
                    outfree2[do % 4] = ctok
                    P.op("scalar", "dma_start",
                         out=o0sc[do * 128:(do + 1) * 128, qt * 512:qt * 512 + 512],
                         in_=o0st_s[sl], wait=[ctok], inc=[(st_o0[sl], 16)])
                    o0slot_free[sl] = (st_o0[sl], st_o0[sl]["n"])
                else:
                    o0_tok = r_o0.load(
                        o0rd_s[sl],
                        o0sc[do * 128:(do + 1) * 128, qt * 512:qt * 512 + 512],
                        extra_wait=([(st_o0[0], st_o0[0]["n"]),
                                     (st_o0[1], st_o0[1]["n"]), gA_hilo]
                                    if do < 2 else ()))
                    x1tok = r_x1.load(
                        x1rd_s[sl],
                        x1sc[do * 128:(do + 1) * 128, qt * 512:qt * 512 + 512],
                        extra_wait=[gA_f32] if do < 2 else ())
                    x1_uses += 1
                    t = o0rd_s[sl]
                    nb = nb0b if e == 10 else nb1b
                    P.op("scalar", "activation", t, t, AF.Identity,
                         bias=nb[:, do:do + 1],
                         wait=[o0_tok, omm_tok, mb_tok], inc=[(sC, 1)])
                    P.op("vector", "tensor_tensor", t, t, ps[ob][:, :],
                         ALU.subtract, wait=[P.tok(sC)])
                    P.op("vector", "tensor_mul", t, t, Mb[:, :])
                    P.op("vector", "tensor_scalar", t, t, bb_[:, do:do + 1],
                         None, ALU.add)
                    P.op("vector", "tensor_tensor", t, t, ps[ob][:, :],
                         ALU.add)
                    P.op("vector", "scalar_tensor_tensor", o0st_s[sl], t,
                         gbc[:, do:do + 1], x1rd_s[sl], ALU.mult, ALU.add,
                         wait=[x1tok, o0slot_free[sl]] +
                         ([gA_f32] if do < 2 else []), inc=[(sV, 1)])
                    ytok = P.tok(sV)
                    outfree2[do % 4] = ytok
                    r_o0.set_free(qt * DC + do, ytok)
                    r_x1.set_free(x1_uses - 1, ytok)
                    P.op("scalar", "dma_start",
                         out=ydev[do * 128:(do + 1) * 128, qt * 512:qt * 512 + 512],
                         in_=o0st_s[sl], wait=[ytok], inc=[(sst, 16)])
                    o0slot_free[sl] = (sst, sst["n"])
            hidden_gate = P.tok(sT)
            regA_gate = [outfree2[0], outfree2[1],
                         (st_o0[0], st_o0[0]["n"]), (st_o0[1], st_o0[1]["n"]),
                         (sst, sst["n"])]

    return _finish(P, nc, es, sst)




def _finish(P, nc, es, sst):
    final = (sst, sst["n"])
    with nc.Block() as blk:
        @blk.sync
        def _(E):
            for fn in P.q["sync"]:
                fn(E)
            if final[1]:
                E.wait_ge(final[0]["h"], final[1])

        @blk.gpsimd
        def _(E):
            for fn in P.q["gpsimd"]:
                fn(E)

        @blk.vector
        def _(E):
            for fn in P.q["vector"]:
                fn(E)

        @blk.scalar
        def _(E):
            for fn in P.q["scalar"]:
                fn(E)

        @blk.tensor
        def _(E):
            for fn in P.q["tensor"]:
                fn(E)
    es.close()
    return nc



def _own_rows(cls):
    if cls == 0:
        return np.r_[0:512, 1536:2048]
    return np.r_[512:1536]


def _mk_masks(cls):
    q_bases = (0, 1536) if cls == 0 else (512, 1024)
    kp = np.arange(128).reshape(128, 1)
    qf = np.arange(512).reshape(1, 512)
    m = np.zeros((16, 128, 512), np.float32)
    for c in range(16):
        qb = q_bases[0] if c < 8 else q_bases[1]
        m[c] = np.where(128 * c + kp <= qb + qf, 0.0, NEG)
    return np.ascontiguousarray(m.astype(ml_dtypes.bfloat16))


def kernel(L0_attn_x, L0_attn_Wq, L0_attn_Wk, L0_attn_Wv, L0_attn_Wo,
           L0_attn_q_gamma, L0_attn_k_gamma, L0_ffn_Wg, L0_ffn_W0a, L0_ffn_b0a,
           L0_ffn_W0b, L0_ffn_b0b, L0_ffn_W1a, L0_ffn_b1a, L0_ffn_W1b,
           L0_ffn_b1b, L0_c, L0_wmod):
    x = np.ascontiguousarray(np.asarray(L0_attn_x, dtype=np.float32))
    shared = dict(
        Wq=np.ascontiguousarray(L0_attn_Wq, np.float32),
        Wk=np.ascontiguousarray(L0_attn_Wk, np.float32),
        Wv=np.ascontiguousarray(L0_attn_Wv, np.float32),
        Wo=np.ascontiguousarray(L0_attn_Wo, np.float32),
        qg=np.ascontiguousarray(np.asarray(L0_attn_q_gamma, np.float32).reshape(128, 1)),
        kg=np.ascontiguousarray(np.asarray(L0_attn_k_gamma, np.float32).reshape(128, 1)),
        wg=np.ascontiguousarray(L0_ffn_Wg, np.float32),
        W0aT=np.ascontiguousarray(np.asarray(L0_ffn_W0a, np.float32).T),
        W0bT=np.ascontiguousarray(np.asarray(L0_ffn_W0b, np.float32).T),
        W1aT=np.ascontiguousarray(np.asarray(L0_ffn_W1a, np.float32).T),
        W1bT=np.ascontiguousarray(np.asarray(L0_ffn_W1b, np.float32).T),
        b0af=np.ascontiguousarray(np.asarray(L0_ffn_b0a, np.float32).reshape(FC, 128).T),
        b1af=np.ascontiguousarray(np.asarray(L0_ffn_b1a, np.float32).reshape(FC, 128).T),
        b0bf=np.ascontiguousarray(np.asarray(L0_ffn_b0b, np.float32).reshape(DC, 128).T),
        b1bf=np.ascontiguousarray(np.asarray(L0_ffn_b1b, np.float32).reshape(DC, 128).T),
        wmod=np.ascontiguousarray(L0_wmod, np.float32),
        iden=np.ascontiguousarray(np.eye(128).astype(ml_dtypes.bfloat16)),
    )
    cin = np.asarray(L0_c, np.float32)
    masks_by_cls = [_mk_masks(0), _mk_masks(1)]
    in_maps = []
    for core in range(8):
        b, cls = core // 2, core % 2
        xf = np.ascontiguousarray(x[b].T)
        m = dict(shared)
        m["xfeat"] = xf
        m["xq"] = np.ascontiguousarray(xf[:, _own_rows(cls)])
        m["cf"] = np.ascontiguousarray(cin[b].reshape(16, 128).T)
        m["maskt"] = masks_by_cls[cls]
        in_maps.append(m)

    nc = build_program("full")
    res = run_bass_kernel_spmd(nc, in_maps, core_ids=list(range(8)))
    out = np.empty((4, 2048, 2048), np.float32)
    for core in range(8):
        b, cls = core // 2, core % 2
        out[b][_own_rows(cls)] = res.results[core]["y"].T
    return out


# version 14
# speedup vs baseline: 1.1008x; 1.0571x over previous
import numpy as np
import ml_dtypes
import concourse.bass as bass
import concourse.mybir as mybir
from concourse.bass_utils import run_bass_kernel_spmd

F32 = mybir.dt.float32
BF16 = mybir.dt.bfloat16
U8 = mybir.dt.uint8
AF = mybir.ActivationFunctionType
ALU = mybir.AluOpType

D, DC, NH, KVH, DH = 2048, 16, 16, 4, 128
S, Q, F, FC = 2048, 1024, 4096, 32
TT = 256
NTILE = S // TT
SCALE = float(DH) ** -0.5
NEG = -1.0e30
EPS_LN, EPS_RMS = 1.0e-5, 1.0e-8
QT_CHUNKS = (8, 16)


class Plan:
    ENGINES = ("sync", "gpsimd", "vector", "scalar", "tensor")

    def __init__(self, nc, es):
        self.nc, self.es = nc, es
        self.q = {e: [] for e in self.ENGINES}
        self.label = ""
        self.ttags = []

    def sem(self, name):
        h = self.es.enter_context(self.nc.semaphore(name))
        return {"h": h, "n": 0, "name": name}

    def op(self, eng, method, *args, wait=(), inc=(), **kw):
        wait = tuple(w for w in wait if w is not None)
        for sm, v in wait:
            assert v <= sm["n"], (f"wait {sm['name']}>={v} but planned {sm['n']} "
                                  f"({eng}.{method})")
        for sm, k in inc:
            sm["n"] += k
        if eng == "tensor":
            is32 = False
            try:
                is32 = (args[1].dtype == mybir.dt.float32 and
                        args[1].shape[0] > 1)
            except Exception:
                pass
            self.ttags.append((self.label, is32))
        vdrain = (eng == "vector" and method != "engine_nop")

        def fn(E, wait=wait, inc=tuple(inc), args=args, kw=kw, method=method,
               vdrain=vdrain):
            for sm, v in wait:
                E.wait_ge(sm["h"], v)
            ins = getattr(E, method)(*args, **kw)
            if vdrain:
                ins = E.drain()
            for sm, k in inc:
                ins.then_inc(sm["h"], k)
        self.q[eng].append(fn)

    def tok(self, sm):
        return (sm, sm["n"])


class Ring:
    def __init__(self, plan, name, nslots, queue):
        self.p, self.queue = plan, queue
        self.sems = [plan.sem(f"{name}{i}") for i in range(nslots)]
        self.nuse = [0] * nslots
        self.free_tok = [None] * nslots
        self.k = 0

    def load(self, out_ap, in_ap, extra_wait=(), **kw):
        s = self.k % len(self.sems)
        w = list(extra_wait)
        if self.free_tok[s] is not None:
            w.append(self.free_tok[s])
        self.p.op(self.queue, "dma_start", wait=w, inc=[(self.sems[s], 16)],
                  out=out_ap, in_=in_ap, **kw)
        self.nuse[s] += 1
        self.k += 1
        return (self.sems[s], 16 * self.nuse[s])

    def set_free(self, use_k, tok):
        self.free_tok[use_k % len(self.sems)] = tok


def build_program(stage="full", expose=False):
    nc = bass.Bass("TRN2", target_bir_lowering=False)
    from contextlib import ExitStack
    es = ExitStack()

    def din(name, shape, dt=F32):
        return nc.dram_tensor(name, shape, dt, kind="ExternalInput").ap()

    xfeat = din("xfeat", [D, S])
    xq = din("xq", [D, Q])
    Wqd, Wkd = din("Wq", [D, D]), din("Wk", [D, 512])
    Wvd, Wod = din("Wv", [D, 512]), din("Wo", [D, D])
    qgd, kgd = din("qg", [128, 1]), din("kg", [128, 1])
    wgr = din("wg", [2, D])
    W0aTd, W0bTd = din("W0aT", [D, F]), din("W0bT", [F, D])
    W1aTd, W1bTd = din("W1aT", [D, F]), din("W1bT", [F, D])
    b0afd, b1afd = din("b0af", [128, FC]), din("b1af", [128, FC])
    b0bfd, b1bfd = din("b0bf", [128, DC]), din("b1bf", [128, DC])
    cfd = din("cf", [128, 16])
    wmodd = din("wmod", [D, 6 * D])
    maskd = din("maskt", [16, 128, 512], BF16)
    idend = din("iden", [128, 128], BF16)

    ydev = nc.dram_tensor("y", [D, Q], F32, kind="ExternalOutput").ap()
    sck = "ExternalOutput" if expose else "Internal"
    modsc = nc.dram_tensor("modsc", [6 * D], F32, kind=sck).ap()
    avhs = nc.dram_tensor("avhs", [D, Q], BF16, kind="Internal").ap()
    avls = nc.dram_tensor("avls", [D, Q], BF16, kind="Internal").ap()
    avsc = nc.dram_tensor("avsc", [D, Q], F32, kind=sck).ap()
    x1sc = nc.dram_tensor("x1sc", [D, Q], F32, kind=sck).ap()
    o0sc = nc.dram_tensor("o0sc", [D, Q], F32, kind=sck).ap()
    dmp = {}
    if expose:
        for nm, shp in (("modT", [128, 96]), ("der", [128, 64]),
                        ("K", [128, 4 * S]), ("V", [128, 16 * 512]),
                        ("H", [128, DC * Q]), ("H2", [128, DC * Q])):
            dmp[nm] = nc.dram_tensor("dump_" + nm, shp, F32,
                                     kind="ExternalOutput").ap()

    O_ROWS = 4096
    O_BCAST = 16384
    BIG = 22528
    ARENA = BIG + 180224 + 8192
    XTRA = BIG + 180224
    arena = es.enter_context(nc.sbuf_tensor("arena", [128, ARENA], U8))

    def buf(off, n, dt=F32):
        sz = mybir.dt.size(dt)
        assert off % 4 == 0 and off + n * sz <= ARENA, (off, n)
        return arena[:, off:off + n * sz].bitcast(dt)

    ones = buf(0, 128)
    modT = buf(512, 96)
    der = buf(896, 64)
    siluc = buf(1152, 16)
    cfs = buf(1216, 16)
    qg, kg = buf(1280, 1), buf(1288, 1)
    b0a_s, b1a_s = buf(1296, FC), buf(1424, FC)
    b0b_s, b1b_s = buf(1552, DC), buf(1616, DC)
    wg0, wg1, wgd = buf(1680, 16), buf(1744, 16), buf(1808, 16)
    tmpa, tmpb = buf(1872, 16), buf(1936, 16)
    ones_bf = buf(2192, 64, BF16)
    iden_bf = buf(2320, 128, BF16)

    def row(o, n):
        assert o + n <= 3072
        return buf(O_ROWS + o * 4, n)[0:1, :]
    bc = buf(O_BCAST, 1536)

    Kbuf = buf(BIG + 0, 4 * S)

    def Khl(hl, kvc, c):
        base = BIG + (kvc * 4 + c // 4) * 2048 + hl * 1024 + (c % 4) * 256
        return buf(base, 128, BF16)
    Vhi = buf(BIG + 32768, 16 * 512, BF16)
    Vlo = buf(BIG + 49152, 16 * 512, BF16)

    def Hap(hl, c, qt):
        return buf(BIG + 65536 + c * 4096 + qt * 2048 + hl * 1024, 512, BF16)
    TR = BIG + 131072
    wmod_s = [buf(TR + 16384 + i * 2048, 512) for i in range(4)]
    xh = [buf(TR, DC * TT), buf(TR + 16384, DC * TT)]
    x2s = [buf(TR + 32768 + i * 1024, TT) for i in range(2)]
    wk_s = [buf(TR + 34816 + i * 2048, 512) for i in range(2)]
    wv_s = [buf(TR + 38912 + i * 2048, 512) for i in range(2)]
    k2buf = buf(TR + 43008, 1024)
    t1c = [bc[:, 512 + i * 256:768 + i * 256] for i in range(2)]
    masks = buf(TR, 16 * 512, BF16)
    wq_s = [buf(TR + 16384, DC * 128)]
    wq_hi = buf(XTRA, DC * 128, BF16)
    wq_lo = buf(XTRA + 4096, DC * 128, BF16)
    qh_hi = [buf(TR + 24576 + i * 4096, 1024, BF16) for i in range(2)]
    qh_lo = [buf(TR + 26624 + i * 4096, 1024, BF16) for i in range(2)]
    x2q = [buf(O_ROWS + i * 2048, 512) for i in range(2)]
    pt_s = [buf(TR + 34816 + i * 2048, 512) for i in range(2)]
    pt_hi = [buf(TR + 38912 + i * 1024, 512, BF16) for i in range(3)]
    pt_lo = [buf(TR + 41984 + i * 1024, 512, BF16) for i in range(3)]
    avn_s = [buf(TR + 45056, 512)]
    avn_hi = [buf(TR + 47104, 512, BF16)]
    avn_lo = [buf(TR + 48128, 512, BF16)]
    xq_s = [buf(TR + i * 2048, 512) for i in range(2)]
    x1st_s = [buf(TR + 4096 + i * 2048, 512) for i in range(2)]
    x2w_s = [buf(TR + 8192 + i * 2048, 512) for i in range(2)]
    avqh = buf(TR + 16384, 16 * 512, BF16)
    avql_ = buf(TR + 32768, 16 * 512, BF16)
    wo_s = [buf(BIG + i * 8192, DC * 128) for i in range(2)]
    wo_hi = [buf(BIG + 16384 + i * 8192, DC * 128, BF16) for i in range(2)]
    wo_lo = [buf(BIG + 20480 + i * 8192, DC * 128, BF16) for i in range(2)]

    def H2ap(hl, c, qt):
        return buf(BIG + 65536 + c * 4096 + qt * 2048 + hl * 1024, 512, BF16)
    x1rd3_s = [buf(BIG + 32768 + i * 2048, 512) for i in range(2)]
    hid_hi = buf(BIG + 0, FC * 512, BF16)
    hid_lo = buf(BIG + 32768, FC * 512, BF16)
    t1c2 = [bc[:, 1024:1536], bc[:, 1024:1536]]
    w0a_f = [buf(TR + 0, DC * 128), buf(XTRA, DC * 128)]
    w0a_hi = [buf(TR + 8192 + i * 8192, DC * 128, BF16) for i in range(2)]
    w0a_lo = [buf(TR + 12288 + i * 8192, DC * 128, BF16) for i in range(2)]
    x1rd_s = [buf(TR + i * 2048, 512) for i in range(2)]
    o0st_s = [buf(TR + 4096 + i * 2048, 512) for i in range(2)]
    o0rd_s = [buf(TR + 8192 + i * 2048, 512) for i in range(2)]
    w0b_f = buf(TR + 24576, 16 * 128)
    w0b_hi = [buf(TR + 32768 + i * 8192, 16 * 128, BF16) for i in range(2)]
    w0b_lo = [buf(TR + 36864 + i * 8192, 16 * 128, BF16) for i in range(2)]
    wgd_hi = buf(2000, 16, BF16)
    wgd_lo = buf(2032, 16, BF16)
    nb0b = buf(2064, DC)
    nb1b = buf(2128, DC)

    ps = [es.enter_context(nc.psum_tensor(f"ps{i}", [128, 512], F32))
          for i in range(8)]

    P = Plan(nc, es)
    sT, sV, sC = P.sem("sT"), P.sem("sV"), P.sem("sC")
    sG = P.sem("sG")
    sst = P.sem("sst")
    s_sp = P.sem("ssp")
    r_wmod = Ring(P, "rwm", 4, "sync")
    r_xt = Ring(P, "rxt", 2, "gpsimd")
    r_wk = Ring(P, "rwk", 2, "sync")
    r_wv = Ring(P, "rwv", 2, "sync")
    r_wq = Ring(P, "rwq", 1, "sync")
    r_wqhi = Ring(P, "rwqhi", 1, "gpsimd")
    r_wo = Ring(P, "rwo", 2, "sync")
    r_wohi = Ring(P, "rwohi", 2, "gpsimd")
    avql = [P.sem(f"avq{i}") for i in range(16)]
    r_xq = Ring(P, "rxq", 2, "gpsimd")
    r_x1 = Ring(P, "rx1", 2, "gpsimd")
    r_w0a = Ring(P, "rw0a", 2, "sync")
    r_w0b = Ring(P, "rw0b", 2, "sync")
    r_wahi = Ring(P, "rwahi", 2, "gpsimd")
    r_wbhi = Ring(P, "rwbhi", 2, "gpsimd")
    r_o0 = Ring(P, "ro0", 2, "gpsimd")
    st_avn = [P.sem("stav0"), P.sem("stav1")]
    st_av32 = P.sem("stav32")
    st_x1 = [P.sem("stx10"), P.sem("stx11")]
    st_o0 = [P.sem("sto00"), P.sem("sto01")]

    def mm(out, lhsT, rhs, start, stop, wait=(), inc=()):
        P.op("tensor", "matmul", out, lhsT, rhs, wait=wait, inc=inc,
             start=start, stop=stop)

    ones_col = ones[:, 0:1]
    ones_row = ones[0:1, :]

    for dst, src in ((cfs, cfd), (qg, qgd), (kg, kgd), (iden_bf, idend),
                     (b0a_s, b0afd),
                     (b1a_s, b1afd), (b0b_s, b0bfd), (b1b_s, b1bfd)):
        P.op("sync", "dma_start", out=dst, in_=src, inc=[(sG, 16)])
    P.op("sync", "dma_start", out=wg0,
         in_=wgr[0:1, :].rearrange("o (a p) -> (o p) a", p=128),
         allow_slow_non_contiguous=True, inc=[(sG, 16)])
    P.op("sync", "dma_start", out=wg1,
         in_=wgr[1:2, :].rearrange("o (a p) -> (o p) a", p=128),
         allow_slow_non_contiguous=True, inc=[(sG, 16)])
    g0 = P.tok(sG)
    P.op("vector", "memset", ones, 1.0)
    P.op("vector", "memset", ones_bf, 1.0, inc=[(sV, 1)])
    P.op("vector", "tensor_sub", wgd, wg0, wg1, wait=[g0], inc=[(sV, 1)])
    P.op("scalar", "activation", siluc, cfs, AF.Silu, wait=[g0], inc=[(sC, 1)])
    silu_tok = P.tok(sC)
    ones_tok = P.tok(sV)

    P.label = "mod"
    n_ft = 6 * D // 512
    mrow = [row(0, 512), row(512, 512)]
    prev_evac = None
    def _wmod_load(ft, c):
        return r_wmod.load(wmod_s[(ft * DC + c) % 4],
                           wmodd[c * 128:(c + 1) * 128, ft * 512:(ft + 1) * 512])
    toks = {c: _wmod_load(0, c) for c in range(4)}
    for ft in range(n_ft):
        for c in range(DC):
            w = [toks.pop(c)]
            if c == 0:
                w += [silu_tok, prev_evac]
            mm(ps[0][0:1, :], siluc[:, c:c + 1], wmod_s[(ft * DC + c) % 4],
               start=(c == 0), stop=(c == DC - 1), wait=w, inc=[(sT, 1)])
            r_wmod.set_free(ft * DC + c, P.tok(sT))
            nc_, nft = c + 4, ft
            if nc_ >= DC:
                nc_, nft = nc_ - DC, ft + 1
            if nft < n_ft:
                toks[nc_] = _wmod_load(nft, nc_)
        slot = mrow[ft % 2]
        w = [P.tok(sT)]
        if ft >= 2:
            w.append((s_sp, 16 * ft))
        P.op("scalar", "copy", slot, ps[0][0:1, :], wait=w, inc=[(sC, 1)])
        prev_evac = P.tok(sC)
        P.op("sync", "dma_start", out=modsc[ft * 512:(ft + 1) * 512], in_=slot,
             wait=[prev_evac], inc=[(s_sp, 16)])
    P.op("sync", "dma_start", out=modT,
         in_=modsc.rearrange("(a p) -> p a", p=128),
         allow_slow_non_contiguous=True, wait=[(s_sp, 16 * n_ft)], inc=[(sG, 16)])
    gm = P.tok(sG)
    ca1, gac, cb1, gbc = der[:, 0:16], der[:, 16:32], der[:, 32:48], der[:, 48:64]
    sa_v, ca_v, ga_v = modT[:, 0:16], modT[:, 16:32], modT[:, 32:48]
    sb_v, cb_v, gb_v = modT[:, 48:64], modT[:, 64:80], modT[:, 80:96]
    P.op("vector", "tensor_scalar_add", ca1, ca_v, 1.0, wait=[gm])
    P.op("vector", "tensor_scalar_add", cb1, cb_v, 1.0)
    for g_in, g_out in ((ga_v, gac), (gb_v, gbc)):
        P.op("vector", "tensor_single_scalar", tmpa, g_in, 0.0, ALU.is_ge)
        P.op("vector", "tensor_scalar", tmpa, tmpa, 2.0, 1.0, ALU.mult, ALU.subtract)
        P.op("vector", "tensor_mul", tmpb, g_in, tmpa)
        P.op("vector", "tensor_scalar_max", tmpb, tmpb, 0.25)
        P.op("vector", "tensor_mul", g_out, tmpb, tmpa)
    P.op("vector", "tensor_copy", wgd_hi, wgd)
    P.op("vector", "tensor_tensor", wgd_lo, wgd, wgd_hi, ALU.subtract)
    P.op("vector", "tensor_scalar", nb0b, b0b_s, -1.0, None, ALU.mult)
    P.op("vector", "tensor_scalar", nb1b, b1b_s, -1.0, None, ALU.mult)
    P.op("vector", "engine_nop", inc=[(sV, 1)])
    mod_tok = P.tok(sV)
    wmod_alldone = P.tok(sT)

    if expose:
        P.op("sync", "dma_start", out=dmp["modT"], in_=modT, wait=[mod_tok],
             inc=[(sst, 16)])
        P.op("sync", "dma_start", out=dmp["der"], in_=der, wait=[mod_tok],
             inc=[(sst, 16)])
    if stage == "mod":
        return _finish(P, nc, es, sst)

    st2 = {"x2free": [None, None], "bc_copy": None, "kevac": None,
           "vevac": None, "bc6_free": None}
    P.label = "ph2"
    NT_ALL = NTILE + Q // TT
    r_xt.free_tok[1] = wmod_alldone
    pend_kv = [None]
    ln2_eps = buf(2576, 1)[0:1, :]
    rms_eps = buf(2584, 1)[0:1, :]
    P.op("vector", "memset", ln2_eps, EPS_LN)
    P.op("vector", "memset", rms_eps, EPS_RMS, inc=[(sV, 1)])
    eps_tok = P.tok(sV)

    def emit_kv(t, x_ap, htok, htoks):
        t0 = t * TT
        wk_t = {c: r_wk.load(wk_s[c % 2], Wkd[c * 128:(c + 1) * 128, :])
                for c in range(2)}
        wv_t = {c: r_wv.load(wv_s[c % 2], Wvd[c * 128:(c + 1) * 128, :])
                for c in range(2)}
        for c in range(DC):
            hc = x_ap[:, c * TT:(c + 1) * TT]
            for kvc in range(4):
                w = []
                if kvc == 0:
                    w = [wk_t[c], htoks[c]]
                    if c == 0:
                        w += [st2["kevac"]]
                mm(ps[kvc][:, 0:TT], wk_s[c % 2][:, kvc * 128:(kvc + 1) * 128],
                   hc, start=(c == 0), stop=(c == DC - 1), wait=w,
                   inc=[(sT, 1)] if kvc == 3 else ())
            r_wk.set_free(t * DC + c, P.tok(sT))
            for tc in range(2):
                w = []
                if tc == 0:
                    w = [wv_t[c]]
                    if c == 0:
                        w += [st2["vevac"], st2["bc_copy"]]
                mm(ps[6 + tc][:, :], hc[:, tc * 128:(tc + 1) * 128], wv_s[c % 2],
                   start=(c == 0), stop=(c == DC - 1), wait=w,
                   inc=[(sT, 1)] if tc == 1 else ())
            r_wv.set_free(t * DC + c, P.tok(sT))
            if c + 2 < DC:
                wk_t[c + 2] = r_wk.load(wk_s[c % 2],
                                        Wkd[(c + 2) * 128:(c + 3) * 128, :])
                wv_t[c + 2] = r_wv.load(wv_s[c % 2],
                                        Wvd[(c + 2) * 128:(c + 3) * 128, :])
        kvtok = P.tok(sT)
        for kvc in range(4):
            P.op("scalar", "copy", Kbuf[:, kvc * S + t0:kvc * S + t0 + TT],
                 ps[kvc][:, 0:TT], wait=[kvtok] if kvc == 0 else [],
                 inc=[(sC, 1)])
        st2["kevac"] = P.tok(sC)
        for tc in range(2):
            kc = t0 // 128 + tc
            P.op("scalar", "copy", Vhi[:, kc * 512:(kc + 1) * 512],
                 ps[6 + tc][:, :], inc=[(sC, 1)])
            P.op("vector", "tensor_tensor", Vlo[:, kc * 512:(kc + 1) * 512],
                 ps[6 + tc][:, :], Vhi[:, kc * 512:(kc + 1) * 512],
                 ALU.subtract, wait=[P.tok(sC)], inc=[(sV, 1)])
        st2["vevac"] = P.tok(sV)
        return kvtok

    TILE_ORDER = [0, 1, 2, 3, NTILE, 4, NTILE + 1, 5, NTILE + 2, 6,
                  NTILE + 3, 7]
    for ti, t in enumerate(TILE_ORDER):
        own = t >= NTILE
        t0 = (t - NTILE) * TT if own else t * TT
        x_ap = xh[ti % 2]
        src_d = xq if own else xfeat
        xtok = r_xt.load(
            x_ap.rearrange("p (c t) -> p c t", t=TT),
            src_d.rearrange("(c p) t -> p c t", p=128)[:, :, t0:t0 + TT])
        for c in range(DC):
            xc = x_ap[:, c * TT:(c + 1) * TT]
            x2 = x2s[c % 2][:, 0:TT]
            P.op("scalar", "activation", x2, xc, AF.Square,
                 wait=[xtok, st2["x2free"][c % 2]], inc=[(sC, 1)])
            x2tok = P.tok(sC)
            w_mean = [xtok, ones_tok] if c == 0 else []
            mm(ps[4][0:1, 0:TT], ones_col, xc, start=(c == 0),
               stop=(c == DC - 1), wait=w_mean)
            mm(ps[5][0:1, 0:TT], ones_col, x2, start=(c == 0),
               stop=(c == DC - 1), wait=[x2tok], inc=[(sT, 1)])
            st2["x2free"][c % 2] = P.tok(sT)
        stats_tok = P.tok(sT)
        if pend_kv[0] is not None:
            pend_kv[0]()
            pend_kv[0] = None
        r_m, r_t = row(1024, TT), row(1280, TT)
        r_rstd, r_shift = row(1536, TT), row(1536 + TT, TT)
        P.op("vector", "tensor_scalar", r_m, ps[4][0:1, 0:TT], 1.0 / D, None,
             ALU.mult, wait=[stats_tok, eps_tok])
        P.op("vector", "tensor_mul", r_t, r_m, r_m)
        P.op("vector", "scalar_tensor_tensor", r_rstd, ps[5][0:1, 0:TT],
             1.0 / D, r_t, ALU.mult, ALU.subtract, inc=[(sV, 1)])
        P.op("scalar", "activation", r_rstd, r_rstd, AF.Ln,
             bias=ln2_eps, wait=[P.tok(sV)])
        P.op("scalar", "activation", r_rstd, r_rstd, AF.Exp, scale=-0.5,
             inc=[(sC, 1)])
        P.op("vector", "scalar_tensor_tensor", r_shift, r_m, -1.0, r_rstd,
             ALU.mult, ALU.mult, wait=[P.tok(sC)], inc=[(sV, 1)])
        mm(ps[6][:, 0:2 * TT], ones_row, row(1536, 2 * TT), start=True,
           stop=True, wait=[P.tok(sV), st2["bc_copy"], st2["vevac"],
                            st2.get("h_done")], inc=[(sT, 1)])
        P.op("scalar", "copy", bc[:, 0:2 * TT], ps[6][:, 0:2 * TT],
             wait=[P.tok(sT)], inc=[(sC, 1)])
        st2["bc_copy"] = P.tok(sC)
        bctok = st2["bc_copy"]
        RSTD, SHIFT = bc[:, 0:TT], bc[:, TT:2 * TT]
        htoks = []
        for c in range(DC):
            xc = x_ap[:, c * TT:(c + 1) * TT]
            hd = xc
            w = [bctok]
            if t == 0 and c == 0:
                w.append(mod_tok)
            P.op("scalar", "activation", t1c[c % 2], SHIFT, AF.Identity,
                 scale=ca1[:, c:c + 1], bias=sa_v[:, c:c + 1],
                 wait=[bctok] + ([htoks[c - 2]] if c >= 2 else []),
                 inc=[(sC, 1)])
            c2tok = P.tok(sC)
            P.op("vector", "scalar_tensor_tensor", hd, xc, ca1[:, c:c + 1],
                 RSTD, ALU.mult, ALU.mult, wait=w)
            P.op("vector", "tensor_add", hd, hd, t1c[c % 2], wait=[c2tok],
                 inc=[(sV, 1)])
            if own:
                qt_half = (t0 // 512)
                off = t0 % 512
                hhi = buf(BIG + 65536 + c * 4096 + qt_half * 2048, 512, BF16)[:, off:off + TT]
                hlo = buf(BIG + 65536 + c * 4096 + qt_half * 2048 + 1024, 512, BF16)[:, off:off + TT]
                P.op("vector", "tensor_copy", hhi, xc)
                P.op("vector", "tensor_tensor", hlo, xc, hhi, ALU.subtract,
                     inc=[(sV, 1)])
            htoks.append(P.tok(sV))
        st2["h_done"] = P.tok(sV)
        st2["htoks"] = htoks
        htok = P.tok(sV)
        if own:
            r_xt.set_free(ti, htok)
        else:
            def mk(t=t, ti=ti, x_ap=x_ap, htok=htok,
                   htoks=tuple(st2["htoks"])):
                def go():
                    kvtok = emit_kv(t, x_ap, htok, htoks)
                    r_xt.set_free(ti, kvtok)
                return go
            pend_kv[0] = mk()
    if pend_kv[0] is not None:
        pend_kv[0]()
    hown_tok = P.tok(sV)

    kv_all = st2["vevac"]
    units = [(kvc, half) for kvc in range(4) for half in range(2)]
    prev_stt = hown_tok
    ktmp = wk_s[0]
    for i, (kvc, half) in enumerate(units):
        k2 = k2buf
        P.op("scalar", "activation", k2, Kbuf[:, kvc * S + half * 1024:
             kvc * S + (half + 1) * 1024], AF.Square,
             wait=[kv_all] + ([P.tok(sV)] if i > 0 else [st2["h_done"]]),
             inc=[(sC, 1)])
        k2tok = P.tok(sC)
        for blk in range(2):
            b = half * 2 + blk
            sl = slice(blk * 512, (blk + 1) * 512)
            mm(ps[6][0:1, :], ones_col, k2[:, sl], start=True, stop=True,
               wait=[k2tok, P.tok(sC)], inc=[(sT, 1)])
            r_r = row(1024, 512)
            P.op("scalar", "activation", r_r, ps[6][0:1, :], AF.Ln,
                 scale=1.0 / DH, bias=rms_eps, wait=[P.tok(sT)])
            P.op("scalar", "activation", r_r, r_r, AF.Exp, scale=-0.5,
                 inc=[(sC, 1)])
            mm(ps[7][:, :], ones_row, r_r, start=True, stop=True,
               wait=[P.tok(sC), prev_stt], inc=[(sT, 1)])
            P.op("scalar", "copy", bc[:, 1024:1536], ps[7][:, :],
                 wait=[P.tok(sT)], inc=[(sC, 1)])
            prev_bc = P.tok(sC)
            base = kvc * S + half * 1024 + blk * 512
            kc = Kbuf[:, base:base + 512]
            P.op("vector", "scalar_tensor_tensor", ktmp, kc, kg,
                 bc[:, 1024:1536], ALU.mult, ALU.mult, wait=[prev_bc])
            khi = buf(BIG + (kvc * 4 + b) * 2048, 512, BF16)
            klo = buf(BIG + (kvc * 4 + b) * 2048 + 1024, 512, BF16)
            P.op("vector", "tensor_copy", khi, ktmp)
            P.op("vector", "tensor_tensor", klo, ktmp, khi, ALU.subtract,
                 inc=[(sV, 1)])
            prev_stt = P.tok(sV)
    krms_tok = P.tok(sV)

    if expose:
        P.op("sync", "dma_start", out=dmp["H"],
             in_=buf(BIG + 65536, DC * Q), wait=[hown_tok], inc=[(sst, 16)])
    if stage == "ph2":
        return _finish(P, nc, es, sst)

    P.label = "attn"
    P.op("sync", "dma_start", out=masks.rearrange("p (i n) -> p i n", n=512),
         in_=maskd.rearrange("i p n -> p i n"), wait=[krms_tok], inc=[(sG, 16)])
    mask_tok = P.tok(sG)

    st3 = {
        "qfree": [hown_tok, hown_tok],
        "wq_pair_free": krms_tok,
        "avfree": krms_tok,
        "sbank_free": [krms_tok, krms_tok],
        "ptf_free": [krms_tok, krms_tok],
        "pthl_free": [krms_tok, krms_tok, krms_tok],
        "avslot_free": None,
        "bc_user": [krms_tok, krms_tok],
        "ps6_free": krms_tok,
        "qh_toks": [None, None],
        "x2q_free": [krms_tok, krms_tok],
    }

    def emit_stage_a(h):
        wsrc_q = Wqd.rearrange("(c p) e -> p c e", p=128)[:, :, h * 128:(h + 1) * 128]
        wq_tok = r_wq.load(
            wq_s[0].rearrange("p (c e) -> p c e", e=128), wsrc_q,
            extra_wait=[krms_tok] if h < 1 else ())
        wqhi_tok = r_wqhi.load(
            wq_hi[:].rearrange("p (c e) -> p c e", e=128), wsrc_q,
            extra_wait=[krms_tok] if h < 1 else ())
        P.op("vector", "tensor_tensor", wq_lo, wq_s[0], wq_hi, ALU.subtract,
             wait=[wq_tok, wqhi_tok, st3["wq_pair_free"]], inc=[(sV, 1)])
        wq_split = P.tok(sV)
        r_wq.set_free(h, wq_split)
        qmm_toks = []
        for qt in range(2):
            for c in range(DC):
                w = []
                if c == 0:
                    w = [wq_split, st3["qfree"][qt]]
                    if h == 0:
                        w.append(hown_tok)
                whi = wq_hi[:, c * 128:(c + 1) * 128]
                wlo = wq_lo[:, c * 128:(c + 1) * 128]
                mm(ps[4 + qt][:, :], whi, Hap(0, c, qt),
                   start=(c == 0), stop=False, wait=w)
                mm(ps[4 + qt][:, :], whi, Hap(1, c, qt), start=False,
                   stop=False)
                mm(ps[4 + qt][:, :], wlo, Hap(0, c, qt), start=False,
                   stop=(c == DC - 1),
                   inc=[(sT, 1)] if c == DC - 1 else ())
            qmm_toks.append(P.tok(sT))
        r_wqhi.set_free(h, qmm_toks[1])
        st3["wq_pair_free"] = qmm_toks[1]
        sq_toks = []
        for qt in range(2):
            P.op("scalar", "activation", x2q[qt], ps[4 + qt][:, :], AF.Square,
                 wait=[qmm_toks[qt], st3["x2q_free"][qt]], inc=[(sC, 1)])
            sq_toks.append(P.tok(sC))
        return qmm_toks, sq_toks

    def mk_rms_steps(h, qmm_toks, sq_toks):
        state = {}

        def msq(qt):
            def go():
                w = [sq_toks[qt], st3["ps6_free"]]
                mm(ps[6][0:1, :], ones_col, x2q[qt], start=True, stop=True,
                   wait=w, inc=[(sT, 1)])
                state["msq%d" % qt] = P.tok(sT)
                st3["x2q_free"][qt] = state["msq%d" % qt]
                r_r = row(1024 + qt * 512, 512)
                P.op("scalar", "activation", r_r, ps[6][0:1, :], AF.Ln,
                     scale=1.0 / DH, bias=rms_eps, wait=[state["msq%d" % qt]])
                P.op("scalar", "activation", r_r, r_r, AF.Exp, scale=-0.5,
                     inc=[(sC, 1)])
                state["ln%d" % qt] = P.tok(sC)
            return go

        def bccp(qt):
            def go():
                mm(ps[6][:, :], ones_row, row(1024 + qt * 512, 512),
                   start=True, stop=True, wait=[state["ln%d" % qt]],
                   inc=[(sT, 1)])
                P.op("scalar", "copy", bc[:, qt * 512:qt * 512 + 512],
                     ps[6][:, :], wait=[P.tok(sT), st3["bc_user"][qt]],
                     inc=[(sC, 1)])
                state["cp%d" % qt] = P.tok(sC)
                st3["ps6_free"] = state["cp%d" % qt]
            return go

        def qhat():
            for qt in range(2):
                qsl = slice(qt * 512, qt * 512 + 512)
                P.op("vector", "scalar_tensor_tensor",
                     ps[4 + qt][:, :], ps[4 + qt][:, :], qg,
                     bc[:, qt * 512:qt * 512 + 512], ALU.mult, ALU.mult,
                     wait=[state["cp%d" % qt]])
                P.op("vector", "tensor_copy", qh_hi[h % 2][:, qsl],
                     ps[4 + qt][:, :])
                P.op("vector", "tensor_tensor", qh_lo[h % 2][:, qsl],
                     ps[4 + qt][:, :], qh_hi[h % 2][:, qsl], ALU.subtract,
                     inc=[(sV, 1)])
                st3["qh_next"][qt] = P.tok(sV)
                st3["bc_user"][qt] = st3["qh_next"][qt]
                st3["qfree"][qt] = st3["qh_next"][qt]
        return [msq(0), bccp(0), msq(1), bccp(1)], qhat

    def emit_chunks(h, qt, inject):
        g = h // 4
        ncH = QT_CHUNKS[qt]
        qsl = slice(qt * 512, qt * 512 + 512)
        qhh = qh_hi[h % 2][:, qsl]
        qhl = qh_lo[h % 2][:, qsl]
        qtok = st3["qh_toks"][qt]
        exp_toks = {}
        plo_toks = {}

        def emit_hi(j):
            w_av = [exp_toks[j]]
            if j == 0:
                w_av.append(st3["avfree"])
            vs = slice(j * 512 + g * 128, j * 512 + (g + 1) * 128)
            mm(ps[2][:, :], Vhi[:, vs], pt_hi[j % 3],
               start=(j == 0), stop=False, wait=w_av)
            mm(ps[2][:, :], Vlo[:, vs], pt_hi[j % 3],
               start=False, stop=False)
            mm(ps[3][0:1, :], ones_bf[:, 0:1], pt_hi[j % 3],
               start=(j == 0), stop=False, inc=[(sT, 1)])

        def emit_lo(j):
            vs = slice(j * 512 + g * 128, j * 512 + (g + 1) * 128)
            mm(ps[2][:, :], Vhi[:, vs], pt_lo[j % 3],
               start=False, stop=(j == ncH - 1), wait=[plo_toks[j]])
            mm(ps[3][0:1, :], ones_bf[:, 0:1], pt_lo[j % 3],
               start=False, stop=(j == ncH - 1), inc=[(sT, 1)])
            st3["pthl_free"][j % 3] = P.tok(sT)

        for c in range(ncH):
            if c >= 2:
                emit_hi(c - 2)
            if c >= 3:
                emit_lo(c - 3)
            if inject and c in inject:
                inject[c]()
            sb = ps[c % 2]
            masked = (qt == 0) or (c >= 8)
            w = [qtok, st3["sbank_free"][c % 2]]
            if h == 0 and qt == 0 and c < 2:
                w.append(mask_tok)
            if masked:
                mm(sb[:, :], iden_bf, masks[:, c * 512:(c + 1) * 512],
                   start=True, stop=False,
                   wait=[st3["sbank_free"][c % 2]] +
                   ([mask_tok, ones_tok] if h == 0 and qt == 0 and c < 2 else []))
            mm(sb[:, :], Khl(0, g, c), qhh, start=not masked, stop=False,
               wait=w)
            mm(sb[:, :], Khl(0, g, c), qhl, start=False, stop=False)
            mm(sb[:, :], Khl(1, g, c), qhh, start=False, stop=True,
               inc=[(sT, 1)])
            smm_tok = P.tok(sT)
            P.op("scalar", "activation", pt_s[c % 2], sb[:, :], AF.Exp,
                 scale=SCALE, wait=[smm_tok, st3["ptf_free"][c % 2]])
            P.op("scalar", "activation", pt_hi[c % 3], sb[:, :], AF.Exp,
                 scale=SCALE, wait=[st3["pthl_free"][c % 3]], inc=[(sC, 1)])
            exp_toks[c] = P.tok(sC)
            st3["sbank_free"][c % 2] = exp_toks[c]
            P.op("vector", "tensor_tensor", pt_lo[c % 3], pt_s[c % 2],
                 pt_hi[c % 3], ALU.subtract, wait=[exp_toks[c]],
                 inc=[(sV, 1)])
            plo_toks[c] = P.tok(sV)
            st3["ptf_free"][c % 2] = plo_toks[c]
        emit_hi(ncH - 2)
        emit_lo(ncH - 3)
        emit_hi(ncH - 1)
        emit_lo(ncH - 2)
        emit_lo(ncH - 1)
        den_tok = P.tok(sT)
        r_d = row(1536, 512)
        P.op("vector", "reciprocal", r_d, ps[3][0:1, :], wait=[den_tok],
             inc=[(sV, 1)])
        mm(ps[7][:, :], ones_row, r_d, start=True, stop=True,
           wait=[P.tok(sV), st3["rdb_free"]], inc=[(sT, 1)])
        P.op("scalar", "copy", bc[:, 1024:1536], ps[7][:, :],
             wait=[P.tok(sT)], inc=[(sC, 1)])
        rdb_tok = P.tok(sC)
        P.op("vector", "tensor_tensor", avn_s[0], ps[2][:, :],
             bc[:, 1024:1536], ALU.mult, wait=[rdb_tok] +
             ([(st_av32, st_av32["n"])] if expose else []))
        P.op("vector", "tensor_copy", avn_hi[0], avn_s[0],
             wait=[st3["avslot_free"]])
        P.op("vector", "tensor_tensor", avn_lo[0], avn_s[0], avn_hi[0],
             ALU.subtract, inc=[(sV, 1)])
        st3["avfree"] = P.tok(sV)
        st3["rdb_free"] = st3["avfree"]
        P.op("sync", "dma_start",
             out=avhs[h * 128:(h + 1) * 128, qt * 512:qt * 512 + 512],
             in_=avn_hi[0], wait=[st3["avfree"]], inc=[(st_avn[0], 16)])
        P.op("sync", "dma_start",
             out=avls[h * 128:(h + 1) * 128, qt * 512:qt * 512 + 512],
             in_=avn_lo[0], wait=[st3["avfree"]], inc=[(st_avn[0], 16)])
        if expose:
            P.op("sync", "dma_start",
                 out=avsc[h * 128:(h + 1) * 128, qt * 512:qt * 512 + 512],
                 in_=avn_s[0], wait=[st3["avfree"]], inc=[(st_av32, 16)])
        st3["avslot_free"] = (st_avn[0], st_avn[0]["n"])

    st3["rdb_free"] = krms_tok
    st3["qh_next"] = [None, None]
    qmm0, sq0 = emit_stage_a(0)
    steps0, qhat0 = mk_rms_steps(0, qmm0, sq0)
    for s in steps0:
        s()
    qhat0()
    st3["qh_toks"] = list(st3["qh_next"])
    for h in range(NH):
        nxt = None
        if h + 1 < NH:
            qmms, sqs = emit_stage_a(h + 1)
            steps, qhat_n = mk_rms_steps(h + 1, qmms, sqs)
            nxt = {4: steps[0], 8: steps[1], 11: steps[2], 14: steps[3]}
        emit_chunks(h, 0, None)
        emit_chunks(h, 1, nxt)
        if h + 1 < NH:
            qhat_n()
            st3["qh_toks"] = list(st3["qh_next"])
    avfree = st3["avfree"]
    qfree = st3["qfree"]
    msq_free = [st3["ps6_free"], st3["rdb_free"]]

    if stage == "attn":
        return _finish(P, nc, es, sst)


    P.label = "wo"
    av_total = [(st_avn[0], st_avn[0]["n"]), (st_avn[1], st_avn[1]["n"])]
    wofree = [avfree, avfree]
    wo_split_prev = [avfree, avfree]
    wo_pair_free = [avfree, avfree]
    x1slot_free = [(), ()]
    stats_user = [qfree[0], qfree[1]]
    ps6_user = msq_free[0]
    ps7_user = msq_free[1]
    h2_toks = [None, None]
    ln2rows_tok = [None, None]
    x1q_tokens = [None, None]
    x1_uses = [0]
    h2_last_v = [None]
    h2_cp = [None, None]

    def h2_prologue(qt, bc_gate):
        r_mean = row(1024 + qt * 512, 512)
        r_rstd = row(2048 + qt * 512, 512)
        s0 = row(0, 512)
        P.op("vector", "tensor_scalar", s0, r_mean, -1.0, None, ALU.mult,
             wait=[ln2rows_tok[qt]])
        P.op("vector", "tensor_mul", s0, s0, r_rstd, inc=[(sV, 1)])
        sh_tok = P.tok(sV)
        mm(ps[6][:, :], ones_row, r_rstd, start=True, stop=True,
           wait=[ln2rows_tok[qt], bc_gate[0]], inc=[(sT, 1)])
        mm(ps[7][:, :], ones_row, s0, start=True, stop=True,
           wait=[sh_tok, bc_gate[1]], inc=[(sT, 1)])
        P.op("scalar", "copy", bc[:, 0:512], ps[6][:, :],
             wait=[(sT, P.tok(sT)[1] - 1)])
        P.op("scalar", "copy", bc[:, 512:1024], ps[7][:, :],
             wait=[P.tok(sT)], inc=[(sC, 1)])
        return P.tok(sC)

    def h2_chunk(qt, c, bc2tok):
        x1tok = r_x1.load(
            x1rd3_s[c % 2],
            x1sc[c * 128:(c + 1) * 128, qt * 512:qt * 512 + 512],
            extra_wait=x1q_tokens[qt] if c < 2 else ())
        pt = ps[6 + c % 2]
        P.op("vector", "scalar_tensor_tensor", pt[:, :], x1rd3_s[c % 2],
             cb1[:, c:c + 1], bc[:, 0:512], ALU.mult, ALU.mult,
             wait=[x1tok, bc2tok, h2_cp[c % 2]])
        P.op("scalar", "activation", t1c2[0], bc[:, 512:1024], AF.Identity,
             scale=cb1[:, c:c + 1], bias=sb_v[:, c:c + 1],
             wait=[bc2tok] + ([h2_last_v[0]] if h2_last_v[0] else []),
             inc=[(sC, 1)])
        P.op("vector", "tensor_add", pt[:, :], pt[:, :], t1c2[0],
             wait=[P.tok(sC)], inc=[(sV, 1)])
        P.op("scalar", "copy", H2ap(0, c, qt), pt[:, :],
             wait=[P.tok(sV)], inc=[(sC, 1)])
        h2_cp[c % 2] = P.tok(sC)
        P.op("vector", "tensor_tensor", H2ap(1, c, qt), pt[:, :],
             H2ap(0, c, qt), ALU.subtract, wait=[P.tok(sC)], inc=[(sV, 1)])
        h2_last_v[0] = P.tok(sV)
        x1_uses[0] += 1
        r_x1.set_free(x1_uses[0] - 1, h2_last_v[0])

    h2q0 = [None]
    wo_prev_qt = None
    for qt in range(2):
        avq_toks = []
        for c2 in range(16):
            w = list(av_total) if (qt == 0 and c2 == 0) else []
            if qt == 1 and c2 == 0:
                w = [wo_prev_qt]
            P.op("gpsimd", "dma_start",
                 out=avqh[:, c2 * 512:(c2 + 1) * 512],
                 in_=avhs[c2 * 128:(c2 + 1) * 128, qt * 512:qt * 512 + 512],
                 wait=w, inc=[(avql[c2], 16)])
            P.op("gpsimd", "dma_start",
                 out=avql_[:, c2 * 512:(c2 + 1) * 512],
                 in_=avls[c2 * 128:(c2 + 1) * 128, qt * 512:qt * 512 + 512],
                 inc=[(avql[c2], 16)])
            avq_toks.append((avql[c2], 32 * (qt + 1)))
        last_mm = None
        for do in range(DC):
            wsrc_o = Wod.rearrange("(c p) e -> p c e", p=128)[:, :, do * 128:(do + 1) * 128]
            wo_tok = r_wo.load(
                wo_s[do % 2].rearrange("p (c e) -> p c e", e=128), wsrc_o,
                extra_wait=[wo_split_prev[do % 2]])
            wohi_tok = r_wohi.load(
                wo_hi[do % 2][:].rearrange("p (c e) -> p c e", e=128), wsrc_o,
                extra_wait=[wo_pair_free[do % 2]])
            P.op("vector", "tensor_tensor", wo_lo[do % 2], wo_s[do % 2],
                 wo_hi[do % 2], ALU.subtract, wait=[wo_tok, wohi_tok],
                 inc=[(sV, 1)])
            wo_split_prev[do % 2] = P.tok(sV)
            for c in range(DC):
                w = [avq_toks[c]] if do == 0 else []
                if c == 0:
                    w += [wo_split_prev[do % 2], wofree[do % 2]]
                whi = wo_hi[do % 2][:, c * 128:(c + 1) * 128]
                wlo = wo_lo[do % 2][:, c * 128:(c + 1) * 128]
                ah = avqh[:, c * 512:(c + 1) * 512]
                al = avql_[:, c * 512:(c + 1) * 512]
                mm(ps[do % 2][:, :], whi, ah, start=(c == 0), stop=False,
                   wait=w)
                mm(ps[do % 2][:, :], whi, al, start=False, stop=False)
                mm(ps[do % 2][:, :], wlo, ah, start=False, stop=(c == DC - 1),
                   inc=[(sT, 1)] if c == DC - 1 else ())
            womm_tok = P.tok(sT)
            last_mm = womm_tok
            r_wo.set_free(qt * DC + do, womm_tok)
            r_wohi.set_free(qt * DC + do, womm_tok)
            wo_pair_free[do % 2] = womm_tok
            xq_tok = r_xq.load(
                xq_s[do % 2],
                xq[do * 128:(do + 1) * 128, qt * 512:qt * 512 + 512])
            sl = do % 2
            x1sl = x1st_s[sl]
            P.op("vector", "tensor_tensor", x1sl, ps[do % 2][:, :],
                 Hap(0, do, qt), ALU.add,
                 wait=[womm_tok] + list(x1slot_free[sl]))
            P.op("vector", "tensor_tensor", x1sl, x1sl, Hap(1, do, qt),
                 ALU.add)
            P.op("vector", "scalar_tensor_tensor", x1sl, x1sl,
                 gac[:, do:do + 1], xq_s[sl], ALU.mult, ALU.add,
                 wait=[xq_tok], inc=[(sV, 1)])
            x1tok = P.tok(sV)
            r_xq.set_free(qt * DC + do, x1tok)
            wofree[do % 2] = x1tok
            P.op("scalar", "activation", x2w_s[sl], x1sl, AF.Square,
                 wait=[x1tok], inc=[(sC, 1)])
            x2wtok = P.tok(sC)
            w_m = [x1tok]
            if do == 0:
                w_m += stats_user
            mm(ps[4][0:1, :], ones_col, x1sl, start=(do == 0),
               stop=(do == DC - 1), wait=w_m)
            mm(ps[5][0:1, :], ones_col, x2w_s[sl], start=(do == 0),
               stop=(do == DC - 1), wait=[x2wtok], inc=[(sT, 1)])
            stats_tok = P.tok(sT)
            P.op("scalar", "dma_start",
                 out=x1sc[do * 128:(do + 1) * 128, qt * 512:qt * 512 + 512],
                 in_=x1sl, wait=[x1tok], inc=[(st_x1[sl], 16)])
            x1slot_free[sl] = ((st_x1[sl], st_x1[sl]["n"]), stats_tok)
            if qt == 1 and h2q0[0] is not None:
                h2_chunk(0, do, h2q0[0])
        wo_prev_qt = last_mm
        x1q_tokens[qt] = [(st_x1[0], st_x1[0]["n"]), (st_x1[1], st_x1[1]["n"])]
        r_mean = row(1024 + qt * 512, 512)
        r_rstd = row(2048 + qt * 512, 512)
        s1 = row(512, 512)
        P.op("vector", "tensor_scalar", r_mean, ps[4][0:1, :], 1.0 / D, None,
             ALU.mult, wait=[P.tok(sT)])
        P.op("vector", "tensor_mul", s1, r_mean, r_mean)
        P.op("vector", "scalar_tensor_tensor", r_rstd, ps[5][0:1, :], 1.0 / D,
             s1, ALU.mult, ALU.subtract, inc=[(sV, 1)])
        P.op("scalar", "activation", r_rstd, r_rstd, AF.Ln, bias=ln2_eps,
             wait=[P.tok(sV)])
        P.op("scalar", "activation", r_rstd, r_rstd, AF.Exp, scale=-0.5,
             inc=[(sC, 1)])
        P.op("vector", "engine_nop", wait=[P.tok(sC)], inc=[(sV, 1)])
        ln2rows_tok[qt] = P.tok(sV)
        stats_user = [P.tok(sV)]
        if qt == 0:
            h2q0[0] = h2_prologue(0, (ps6_user, ps7_user))
    t_3b_end = P.tok(sT)
    if stage == "wo":
        return _finish(P, nc, es, sst)

    P.label = "h2c"
    x1_total = [(st_x1[0], st_x1[0]["n"]), (st_x1[1], st_x1[1]["n"])]
    h2_toks[0] = h2_last_v[0]
    bcg = (h2_last_v[0], h2_last_v[0])
    bc2tok = h2_prologue(1, bcg)
    for c in range(DC):
        h2_chunk(1, c, bc2tok)
    h2_toks[1] = h2_last_v[0]
    h2_tok = h2_last_v[0]
    if expose:
        P.op("sync", "dma_start", out=dmp["H2"],
             in_=buf(BIG + 65536, DC * Q), wait=[h2_tok], inc=[(sst, 16)])
    if stage == "h2":
        return _finish(P, nc, es, sst)

    P.label = "ffn"
    gate_user = h2_tok
    hidfree = [h2_tok, h2_tok]
    outfree2 = [h2_tok, h2_tok, h2_tok, h2_tok]
    o0slot_free = [None, None]
    hidden_gate = h2_tok
    regA_gate = x1_total + [t_3b_end]
    regB_gate = [t_3b_end]
    x1_uses = 2 * DC
    wa_split_prev = h2_tok
    wa_hilo_free = [h2_tok, h2_tok]
    ps67_free = [h2_tok, h2_tok]
    wb_split_prev = h2_tok
    wb_hilo_free = [h2_tok, h2_tok]
    for qt in range(2):
        for c in range(DC):
            w = []
            if c == 0:
                w = [h2_toks[qt], gate_user, mod_tok]
            h2h = H2ap(0, c, qt)
            h2l = H2ap(1, c, qt)
            mm(ps[4][0:1, :], wgd_hi[:, c:c + 1], h2h,
               start=(c == 0), stop=False, wait=w)
            mm(ps[4][0:1, :], wgd_hi[:, c:c + 1], h2l, start=False, stop=False)
            mm(ps[4][0:1, :], wgd_lo[:, c:c + 1], h2h, start=False,
               stop=(c == DC - 1), inc=[(sT, 1)] if c == DC - 1 else ())
        r_g = row(0, 512)
        P.op("vector", "tensor_single_scalar", r_g, ps[4][0:1, :], 0.0,
             ALU.is_ge, wait=[P.tok(sT)], inc=[(sV, 1)])
        gate_user = P.tok(sV)
        mm(ps[5][:, :], ones_row, r_g, start=True, stop=True,
           wait=[P.tok(sV)] + (list(outfree2) if qt == 1 else []),
           inc=[(sT, 1)])
        mb_tok = P.tok(sT)
        Mb = ps[5]
        for e in range(2):
            WaT = W0aTd if e == 0 else W1aTd
            WbT = W0bTd if e == 0 else W1bTd
            ba = b0a_s if e == 0 else b1a_s
            bb_ = b0b_s if e == 0 else b1b_s
            actf = AF.Gelu if e == 0 else AF.Relu
            hid_toks = []
            hmm_toks = {}
            split_toks = {}
            for f in range(FC):
                ew = list(regA_gate) if f < 2 else []
                sl2 = f % 2
                wsrc = WaT.rearrange("(c p) f -> p c f", p=128)[:, :, f * 128:(f + 1) * 128]
                wa_tok = r_w0a.load(
                    w0a_f[sl2].rearrange("p (c e) -> p c e", e=128), wsrc,
                    extra_wait=ew)
                hi_tok = r_wahi.load(
                    w0a_hi[sl2].rearrange("p (c e) -> p c e", e=128), wsrc,
                    extra_wait=(ew if f < 2 else []) + [wa_hilo_free[sl2]])
                P.op("vector", "tensor_tensor", w0a_lo[sl2], w0a_f[sl2],
                     w0a_hi[sl2], ALU.subtract,
                     wait=[wa_tok, hi_tok] + (list(regA_gate) if f < 2 else []),
                     inc=[(sV, 1)])
                split_toks[f] = P.tok(sV)
                split_tok = split_toks[f]
                r_w0a.set_free((qt * 2 + e) * FC + f, split_tok)
                for c in range(DC):
                    w = []
                    if c == 0:
                        w = [split_tok, hidfree[sl2]]
                        if f == 0:
                            w.append(hidden_gate)
                    h2h = H2ap(0, c, qt)
                    h2l = H2ap(1, c, qt)
                    whi = w0a_hi[sl2][:, c * 128:(c + 1) * 128]
                    wlo = w0a_lo[sl2][:, c * 128:(c + 1) * 128]
                    mm(ps[sl2][:, :], whi, h2h, start=(c == 0), stop=False,
                       wait=w)
                    mm(ps[sl2][:, :], whi, h2l, start=False, stop=False)
                    mm(ps[sl2][:, :], wlo, h2h, start=False,
                       stop=(c == DC - 1),
                       inc=[(sT, 1)] if c == DC - 1 else ())
                hmm_tok = P.tok(sT)
                hmm_toks[f] = hmm_tok
                wa_hilo_free[sl2] = hmm_tok
                P.op("scalar", "activation", hid_hi[:, f * 512:(f + 1) * 512],
                     ps[sl2][:, :], actf, bias=ba[:, f:f + 1],
                     wait=[hmm_tok, ps67_free[sl2]])
                P.op("scalar", "activation", ps[6 + sl2][:, :], ps[sl2][:, :],
                     actf, bias=ba[:, f:f + 1],
                     wait=[outfree2[2 + sl2]] if f < 2 else (), inc=[(sC, 1)])
                P.op("vector", "tensor_tensor", hid_lo[:, f * 512:(f + 1) * 512],
                     ps[6 + sl2][:, :], hid_hi[:, f * 512:(f + 1) * 512],
                     ALU.subtract, wait=[P.tok(sC)], inc=[(sV, 1)])
                hid_toks.append(P.tok(sV))
                ps67_free[sl2] = hid_toks[-1]
                hidfree[sl2] = hid_toks[-1]
            hid_all = hid_toks[-1]
            outfree2[2] = ps67_free[0]
            outfree2[3] = ps67_free[1]
            gA_f32 = split_toks[FC - 2]
            gA_hilo = hmm_toks[FC - 2]
            for do in range(DC):
                ob = [2, 3, 6, 7][do % 4]
                omm_tok = None
                for fh in range(2):
                    ew = list(regB_gate) if (qt == 0 and e == 0 and do == 0) else []
                    wsrc = (WbT.rearrange("(c p) e -> p c e", p=128)
                            [:, fh * 16:(fh + 1) * 16, do * 128:(do + 1) * 128])
                    wb_tok = r_w0b.load(
                        w0b_f.rearrange("p (c e) -> p c e", e=128), wsrc,
                        extra_wait=ew + [wb_split_prev])
                    hi_tok = r_wbhi.load(
                        w0b_hi[fh][:].rearrange("p (c e) -> p c e", e=128), wsrc,
                        extra_wait=ew + [wb_hilo_free[fh]])
                    P.op("vector", "tensor_tensor", w0b_lo[fh], w0b_f,
                         w0b_hi[fh], ALU.subtract, wait=[wb_tok, hi_tok] + ew,
                         inc=[(sV, 1)])
                    wb_split_prev = P.tok(sV)
                    sb_tok = wb_split_prev
                    for c in range(16):
                        fidx = fh * 16 + c
                        w = []
                        if c == 0:
                            w = [sb_tok, hid_all]
                            if fh == 0:
                                w.append(outfree2[do % 4])
                        whi = w0b_hi[fh][:, c * 128:(c + 1) * 128]
                        wlo = w0b_lo[fh][:, c * 128:(c + 1) * 128]
                        hh = hid_hi[:, fidx * 512:(fidx + 1) * 512]
                        hl = hid_lo[:, fidx * 512:(fidx + 1) * 512]
                        mm(ps[ob][:, :], whi, hh,
                           start=(fidx == 0), stop=False, wait=w)
                        mm(ps[ob][:, :], whi, hl, start=False,
                           stop=False)
                        mm(ps[ob][:, :], wlo, hh, start=False,
                           stop=(fidx == FC - 1),
                           inc=[(sT, 1)] if c == 15 else ())
                    wb_hilo_free[fh] = P.tok(sT)
                    r_w0b.set_free((((qt * 2 + e) * DC) + do) * 2 + fh, P.tok(sT))
                omm_tok = P.tok(sT)
                sl = do % 2
                if e == 0:
                    P.op("vector", "tensor_scalar", o0st_s[sl], ps[ob][:, :],
                         bb_[:, do:do + 1], None, ALU.add,
                         wait=[omm_tok, o0slot_free[sl], gA_f32],
                         inc=[(sV, 1)])
                    ctok = P.tok(sV)
                    outfree2[do % 4] = ctok
                    P.op("scalar", "dma_start",
                         out=o0sc[do * 128:(do + 1) * 128, qt * 512:qt * 512 + 512],
                         in_=o0st_s[sl], wait=[ctok], inc=[(st_o0[sl], 16)])
                    o0slot_free[sl] = (st_o0[sl], st_o0[sl]["n"])
                else:
                    o0_tok = r_o0.load(
                        o0rd_s[sl],
                        o0sc[do * 128:(do + 1) * 128, qt * 512:qt * 512 + 512],
                        extra_wait=([(st_o0[0], st_o0[0]["n"]),
                                     (st_o0[1], st_o0[1]["n"]), gA_hilo]
                                    if do < 2 else ()))
                    x1tok = r_x1.load(
                        x1rd_s[sl],
                        x1sc[do * 128:(do + 1) * 128, qt * 512:qt * 512 + 512],
                        extra_wait=[gA_f32] if do < 2 else ())
                    x1_uses += 1
                    t = o0rd_s[sl]
                    nb = nb0b if e == 10 else nb1b
                    P.op("scalar", "activation", t, t, AF.Identity,
                         bias=nb[:, do:do + 1],
                         wait=[o0_tok, omm_tok, mb_tok], inc=[(sC, 1)])
                    P.op("vector", "tensor_tensor", t, t, ps[ob][:, :],
                         ALU.subtract, wait=[P.tok(sC)])
                    P.op("vector", "tensor_mul", t, t, Mb[:, :])
                    P.op("vector", "tensor_scalar", t, t, bb_[:, do:do + 1],
                         None, ALU.add)
                    P.op("vector", "tensor_tensor", t, t, ps[ob][:, :],
                         ALU.add)
                    P.op("vector", "scalar_tensor_tensor", o0st_s[sl], t,
                         gbc[:, do:do + 1], x1rd_s[sl], ALU.mult, ALU.add,
                         wait=[x1tok, o0slot_free[sl]] +
                         ([gA_f32] if do < 2 else []), inc=[(sV, 1)])
                    ytok = P.tok(sV)
                    outfree2[do % 4] = ytok
                    r_o0.set_free(qt * DC + do, ytok)
                    r_x1.set_free(x1_uses - 1, ytok)
                    P.op("scalar", "dma_start",
                         out=ydev[do * 128:(do + 1) * 128, qt * 512:qt * 512 + 512],
                         in_=o0st_s[sl], wait=[ytok], inc=[(sst, 16)])
                    o0slot_free[sl] = (sst, sst["n"])
            hidden_gate = P.tok(sT)
            regA_gate = [outfree2[0], outfree2[1],
                         (st_o0[0], st_o0[0]["n"]), (st_o0[1], st_o0[1]["n"]),
                         (sst, sst["n"])]

    return _finish(P, nc, es, sst)




def _finish(P, nc, es, sst):
    final = (sst, sst["n"])
    with nc.Block() as blk:
        @blk.sync
        def _(E):
            for fn in P.q["sync"]:
                fn(E)
            if final[1]:
                E.wait_ge(final[0]["h"], final[1])

        @blk.gpsimd
        def _(E):
            for fn in P.q["gpsimd"]:
                fn(E)

        @blk.vector
        def _(E):
            for fn in P.q["vector"]:
                fn(E)

        @blk.scalar
        def _(E):
            for fn in P.q["scalar"]:
                fn(E)

        @blk.tensor
        def _(E):
            for fn in P.q["tensor"]:
                fn(E)
    es.close()
    return nc



def _own_rows(cls):
    if cls == 0:
        return np.r_[0:512, 1536:2048]
    return np.r_[512:1536]


def _mk_masks(cls):
    q_bases = (0, 1536) if cls == 0 else (512, 1024)
    kp = np.arange(128).reshape(128, 1)
    qf = np.arange(512).reshape(1, 512)
    m = np.zeros((16, 128, 512), np.float32)
    for c in range(16):
        qb = q_bases[0] if c < 8 else q_bases[1]
        m[c] = np.where(128 * c + kp <= qb + qf, 0.0, NEG)
    return np.ascontiguousarray(m.astype(ml_dtypes.bfloat16))


def kernel(L0_attn_x, L0_attn_Wq, L0_attn_Wk, L0_attn_Wv, L0_attn_Wo,
           L0_attn_q_gamma, L0_attn_k_gamma, L0_ffn_Wg, L0_ffn_W0a, L0_ffn_b0a,
           L0_ffn_W0b, L0_ffn_b0b, L0_ffn_W1a, L0_ffn_b1a, L0_ffn_W1b,
           L0_ffn_b1b, L0_c, L0_wmod):
    x = np.ascontiguousarray(np.asarray(L0_attn_x, dtype=np.float32))
    shared = dict(
        Wq=np.ascontiguousarray(L0_attn_Wq, np.float32),
        Wk=np.ascontiguousarray(L0_attn_Wk, np.float32),
        Wv=np.ascontiguousarray(L0_attn_Wv, np.float32),
        Wo=np.ascontiguousarray(L0_attn_Wo, np.float32),
        qg=np.ascontiguousarray(np.asarray(L0_attn_q_gamma, np.float32).reshape(128, 1)),
        kg=np.ascontiguousarray(np.asarray(L0_attn_k_gamma, np.float32).reshape(128, 1)),
        wg=np.ascontiguousarray(L0_ffn_Wg, np.float32),
        W0aT=np.ascontiguousarray(np.asarray(L0_ffn_W0a, np.float32).T),
        W0bT=np.ascontiguousarray(np.asarray(L0_ffn_W0b, np.float32).T),
        W1aT=np.ascontiguousarray(np.asarray(L0_ffn_W1a, np.float32).T),
        W1bT=np.ascontiguousarray(np.asarray(L0_ffn_W1b, np.float32).T),
        b0af=np.ascontiguousarray(np.asarray(L0_ffn_b0a, np.float32).reshape(FC, 128).T),
        b1af=np.ascontiguousarray(np.asarray(L0_ffn_b1a, np.float32).reshape(FC, 128).T),
        b0bf=np.ascontiguousarray(np.asarray(L0_ffn_b0b, np.float32).reshape(DC, 128).T),
        b1bf=np.ascontiguousarray(np.asarray(L0_ffn_b1b, np.float32).reshape(DC, 128).T),
        wmod=np.ascontiguousarray(L0_wmod, np.float32),
        iden=np.ascontiguousarray(np.eye(128).astype(ml_dtypes.bfloat16)),
    )
    cin = np.asarray(L0_c, np.float32)
    masks_by_cls = [_mk_masks(0), _mk_masks(1)]
    in_maps = []
    for core in range(8):
        b, cls = core // 2, core % 2
        xf = np.ascontiguousarray(x[b].T)
        m = dict(shared)
        m["xfeat"] = xf
        m["xq"] = np.ascontiguousarray(xf[:, _own_rows(cls)])
        m["cf"] = np.ascontiguousarray(cin[b].reshape(16, 128).T)
        m["maskt"] = masks_by_cls[cls]
        in_maps.append(m)

    nc = build_program("full")
    res = run_bass_kernel_spmd(nc, in_maps, core_ids=list(range(8)))
    out = np.empty((4, 2048, 2048), np.float32)
    for core in range(8):
        b, cls = core // 2, core % 2
        out[b][_own_rows(cls)] = res.results[core]["y"].T
    return out


# version 15
# speedup vs baseline: 1.1157x; 1.0136x over previous
import numpy as np
import ml_dtypes
import concourse.bass as bass
import concourse.mybir as mybir
from concourse.bass_utils import run_bass_kernel_spmd

F32 = mybir.dt.float32
BF16 = mybir.dt.bfloat16
U8 = mybir.dt.uint8
AF = mybir.ActivationFunctionType
ALU = mybir.AluOpType

D, DC, NH, KVH, DH = 2048, 16, 16, 4, 128
S, Q, F, FC = 2048, 1024, 4096, 32
TT = 256
NTILE = S // TT
SCALE = float(DH) ** -0.5
NEG = -1.0e30
EPS_LN, EPS_RMS = 1.0e-5, 1.0e-8
QT_CHUNKS = (8, 16)


class Plan:
    ENGINES = ("sync", "gpsimd", "vector", "scalar", "tensor")

    def __init__(self, nc, es):
        self.nc, self.es = nc, es
        self.q = {e: [] for e in self.ENGINES}
        self.label = ""
        self.ttags = []

    def sem(self, name):
        h = self.es.enter_context(self.nc.semaphore(name))
        return {"h": h, "n": 0, "name": name}

    def op(self, eng, method, *args, wait=(), inc=(), **kw):
        wait = tuple(w for w in wait if w is not None)
        for sm, v in wait:
            assert v <= sm["n"], (f"wait {sm['name']}>={v} but planned {sm['n']} "
                                  f"({eng}.{method})")
        for sm, k in inc:
            sm["n"] += k
        if eng == "tensor":
            is32 = False
            try:
                is32 = (args[1].dtype == mybir.dt.float32 and
                        args[1].shape[0] > 1)
            except Exception:
                pass
            self.ttags.append((self.label, is32))
        vdrain = (eng == "vector" and method != "engine_nop")

        def fn(E, wait=wait, inc=tuple(inc), args=args, kw=kw, method=method,
               vdrain=vdrain):
            for sm, v in wait:
                E.wait_ge(sm["h"], v)
            ins = getattr(E, method)(*args, **kw)
            if vdrain:
                ins = E.drain()
            for sm, k in inc:
                ins.then_inc(sm["h"], k)
        self.q[eng].append(fn)

    def tok(self, sm):
        return (sm, sm["n"])


class Ring:
    def __init__(self, plan, name, nslots, queue):
        self.p, self.queue = plan, queue
        self.sems = [plan.sem(f"{name}{i}") for i in range(nslots)]
        self.nuse = [0] * nslots
        self.free_tok = [None] * nslots
        self.k = 0

    def load(self, out_ap, in_ap, extra_wait=(), **kw):
        s = self.k % len(self.sems)
        w = list(extra_wait)
        if self.free_tok[s] is not None:
            w.append(self.free_tok[s])
        self.p.op(self.queue, "dma_start", wait=w, inc=[(self.sems[s], 16)],
                  out=out_ap, in_=in_ap, **kw)
        self.nuse[s] += 1
        self.k += 1
        return (self.sems[s], 16 * self.nuse[s])

    def set_free(self, use_k, tok):
        self.free_tok[use_k % len(self.sems)] = tok


def build_program(stage="full", expose=False):
    nc = bass.Bass("TRN2", target_bir_lowering=False)
    from contextlib import ExitStack
    es = ExitStack()

    def din(name, shape, dt=F32):
        return nc.dram_tensor(name, shape, dt, kind="ExternalInput").ap()

    xfeat = din("xfeat", [D, S])
    xq = din("xq", [D, Q])
    Wqd, Wkd = din("Wq", [D, D]), din("Wk", [D, 512])
    Wvd, Wod = din("Wv", [D, 512]), din("Wo", [D, D])
    qgd, kgd = din("qg", [128, 1]), din("kg", [128, 1])
    wgr = din("wg", [2, D])
    W0aTd, W0bTd = din("W0aT", [D, F]), din("W0bT", [F, D])
    W1aTd, W1bTd = din("W1aT", [D, F]), din("W1bT", [F, D])
    b0afd, b1afd = din("b0af", [128, FC]), din("b1af", [128, FC])
    b0bfd, b1bfd = din("b0bf", [128, DC]), din("b1bf", [128, DC])
    cfd = din("cf", [128, 16])
    wmodd = din("wmod", [D, 6 * D])
    maskd = din("maskt", [16, 128, 512], BF16)
    idend = din("iden", [128, 128], BF16)

    ydev = nc.dram_tensor("y", [D, Q], F32, kind="ExternalOutput").ap()
    sck = "ExternalOutput" if expose else "Internal"
    modsc = nc.dram_tensor("modsc", [6 * D], F32, kind=sck).ap()
    avhs = nc.dram_tensor("avhs", [D, Q], BF16, kind="Internal").ap()
    avls = nc.dram_tensor("avls", [D, Q], BF16, kind="Internal").ap()
    avsc = nc.dram_tensor("avsc", [D, Q], F32, kind=sck).ap()
    x1sc = nc.dram_tensor("x1sc", [D, Q], F32, kind=sck).ap()
    o0sc = nc.dram_tensor("o0sc", [D, Q], F32, kind=sck).ap()
    dmp = {}
    if expose:
        for nm, shp in (("modT", [128, 96]), ("der", [128, 64]),
                        ("K", [128, 4 * S]), ("V", [128, 16 * 512]),
                        ("H", [128, DC * Q]), ("H2", [128, DC * Q])):
            dmp[nm] = nc.dram_tensor("dump_" + nm, shp, F32,
                                     kind="ExternalOutput").ap()

    O_ROWS = 4096
    O_BCAST = 16384
    BIG = 22528
    ARENA = BIG + 180224 + 8192
    XTRA = BIG + 180224
    arena = es.enter_context(nc.sbuf_tensor("arena", [128, ARENA], U8))

    def buf(off, n, dt=F32):
        sz = mybir.dt.size(dt)
        assert off % 4 == 0 and off + n * sz <= ARENA, (off, n)
        return arena[:, off:off + n * sz].bitcast(dt)

    ones = buf(0, 128)
    modT = buf(512, 96)
    der = buf(896, 64)
    siluc = buf(1152, 16)
    cfs = buf(1216, 16)
    qg, kg = buf(1280, 1), buf(1288, 1)
    b0a_s, b1a_s = buf(1296, FC), buf(1424, FC)
    b0b_s, b1b_s = buf(1552, DC), buf(1616, DC)
    wg0, wg1, wgd = buf(1680, 16), buf(1744, 16), buf(1808, 16)
    tmpa, tmpb = buf(1872, 16), buf(1936, 16)
    ones_bf = buf(2192, 64, BF16)
    iden_bf = buf(2320, 128, BF16)

    def row(o, n):
        assert o + n <= 3072
        return buf(O_ROWS + o * 4, n)[0:1, :]
    bc = buf(O_BCAST, 1536)

    Kbuf = buf(BIG + 0, 4 * S)

    def Khl(hl, kvc, c):
        base = BIG + (kvc * 4 + c // 4) * 2048 + hl * 1024 + (c % 4) * 256
        return buf(base, 128, BF16)
    Vhi = buf(BIG + 32768, 16 * 512, BF16)
    Vlo = buf(BIG + 49152, 16 * 512, BF16)

    def Hap(hl, c, qt):
        return buf(BIG + 65536 + c * 4096 + qt * 2048 + hl * 1024, 512, BF16)
    TR = BIG + 131072
    wmod_s = [buf(TR + 16384 + i * 2048, 512) for i in range(4)]
    xh = [buf(TR, DC * TT), buf(TR + 16384, DC * TT)]
    x2s = [buf(TR + 32768 + i * 1024, TT) for i in range(2)]
    wk_s = [buf(TR + 34816 + i * 2048, 512) for i in range(2)]
    wv_s = [buf(TR + 38912 + i * 2048, 512) for i in range(2)]
    k2buf = buf(TR + 43008, 1024)
    t1c = [bc[:, 512 + i * 256:768 + i * 256] for i in range(2)]
    masks = buf(TR, 16 * 512, BF16)
    wq_s = [buf(TR + 16384, DC * 128)]
    wq_hi = buf(XTRA, DC * 128, BF16)
    wq_lo = buf(XTRA + 4096, DC * 128, BF16)
    qh_hi = [buf(TR + 24576 + i * 4096, 1024, BF16) for i in range(2)]
    qh_lo = [buf(TR + 26624 + i * 4096, 1024, BF16) for i in range(2)]
    x2q = [buf(O_ROWS + i * 2048, 512) for i in range(2)]
    pt_s = [buf(TR + 34816 + i * 2048, 512) for i in range(2)]
    pt_hi = [buf(TR + 38912 + i * 1024, 512, BF16) for i in range(3)]
    pt_lo = [buf(TR + 41984 + i * 1024, 512, BF16) for i in range(3)]
    avn_s = [buf(TR + 45056, 512)]
    avn_hi = [buf(TR + 47104, 512, BF16)]
    avn_lo = [buf(TR + 48128, 512, BF16)]
    xq_s = [buf(TR + i * 2048, 512) for i in range(2)]
    x1st_s = [buf(TR + 4096 + i * 2048, 512) for i in range(2)]
    x2w_s = [buf(TR + 8192 + i * 2048, 512) for i in range(2)]
    avqh = buf(TR + 16384, 16 * 512, BF16)
    avql_ = buf(TR + 32768, 16 * 512, BF16)
    wo_s = [buf(BIG + i * 8192, DC * 128) for i in range(2)]
    wo_hi = [buf(BIG + 16384 + i * 8192, DC * 128, BF16) for i in range(2)]
    wo_lo = [buf(BIG + 20480 + i * 8192, DC * 128, BF16) for i in range(2)]

    def H2ap(hl, c, qt):
        return buf(BIG + 65536 + c * 4096 + qt * 2048 + hl * 1024, 512, BF16)
    x1rd3_s = [buf(BIG + 32768 + i * 2048, 512) for i in range(2)]
    hid_hi = buf(BIG + 0, FC * 512, BF16)
    hid_lo = buf(BIG + 32768, FC * 512, BF16)
    t1c2 = [bc[:, 1024:1536], bc[:, 1024:1536]]
    w0a_f = [buf(TR + 0, DC * 128), buf(XTRA, DC * 128)]
    w0a_hi = [buf(TR + 8192 + i * 8192, DC * 128, BF16) for i in range(2)]
    w0a_lo = [buf(TR + 12288 + i * 8192, DC * 128, BF16) for i in range(2)]
    x1rd_s = [buf(TR + i * 2048, 512) for i in range(2)]
    o0st_s = [buf(TR + 4096 + i * 2048, 512) for i in range(2)]
    o0rd_s = [buf(TR + 8192 + i * 2048, 512) for i in range(2)]
    w0b_f = buf(TR + 24576, 16 * 128)
    w0b_hi = [buf(TR + 32768 + i * 8192, 16 * 128, BF16) for i in range(2)]
    w0b_lo = [buf(TR + 36864 + i * 8192, 16 * 128, BF16) for i in range(2)]
    wgd_hi = buf(2000, 16, BF16)
    wgd_lo = buf(2032, 16, BF16)
    nb0b = buf(2064, DC)
    nb1b = buf(2128, DC)

    ps = [es.enter_context(nc.psum_tensor(f"ps{i}", [128, 512], F32))
          for i in range(8)]

    P = Plan(nc, es)
    sT, sV, sC = P.sem("sT"), P.sem("sV"), P.sem("sC")
    sG = P.sem("sG")
    sst = P.sem("sst")
    s_sp = P.sem("ssp")
    r_wmod = Ring(P, "rwm", 4, "sync")
    r_xt = Ring(P, "rxt", 2, "gpsimd")
    r_wk = Ring(P, "rwk", 2, "sync")
    r_wv = Ring(P, "rwv", 2, "sync")
    r_wq = Ring(P, "rwq", 1, "sync")
    r_wqhi = Ring(P, "rwqhi", 1, "gpsimd")
    r_wo = Ring(P, "rwo", 2, "sync")
    r_wohi = Ring(P, "rwohi", 2, "gpsimd")
    avql = [P.sem(f"avq{i}") for i in range(16)]
    r_xq = Ring(P, "rxq", 2, "gpsimd")
    r_x1 = Ring(P, "rx1", 2, "gpsimd")
    r_w0a = Ring(P, "rw0a", 2, "sync")
    r_w0b = Ring(P, "rw0b", 2, "sync")
    r_wahi = Ring(P, "rwahi", 2, "gpsimd")
    r_wbhi = Ring(P, "rwbhi", 2, "gpsimd")
    r_o0 = Ring(P, "ro0", 2, "gpsimd")
    st_avn = [P.sem("stav0"), P.sem("stav1")]
    st_av32 = P.sem("stav32")
    st_x1 = [P.sem("stx10"), P.sem("stx11")]
    st_o0 = [P.sem("sto00"), P.sem("sto01")]
    yst = [P.sem("yst0"), P.sem("yst1")]

    def mm(out, lhsT, rhs, start, stop, wait=(), inc=()):
        P.op("tensor", "matmul", out, lhsT, rhs, wait=wait, inc=inc,
             start=start, stop=stop)

    ones_col = ones[:, 0:1]
    ones_row = ones[0:1, :]

    for dst, src in ((cfs, cfd), (qg, qgd), (kg, kgd), (iden_bf, idend),
                     (b0a_s, b0afd),
                     (b1a_s, b1afd), (b0b_s, b0bfd), (b1b_s, b1bfd)):
        P.op("sync", "dma_start", out=dst, in_=src, inc=[(sG, 16)])
    P.op("sync", "dma_start", out=wg0,
         in_=wgr[0:1, :].rearrange("o (a p) -> (o p) a", p=128),
         allow_slow_non_contiguous=True, inc=[(sG, 16)])
    P.op("sync", "dma_start", out=wg1,
         in_=wgr[1:2, :].rearrange("o (a p) -> (o p) a", p=128),
         allow_slow_non_contiguous=True, inc=[(sG, 16)])
    g0 = P.tok(sG)
    P.op("vector", "memset", ones, 1.0)
    P.op("vector", "memset", ones_bf, 1.0, inc=[(sV, 1)])
    P.op("vector", "tensor_sub", wgd, wg0, wg1, wait=[g0], inc=[(sV, 1)])
    P.op("scalar", "activation", siluc, cfs, AF.Silu, wait=[g0], inc=[(sC, 1)])
    silu_tok = P.tok(sC)
    ones_tok = P.tok(sV)

    P.label = "mod"
    n_ft = 6 * D // 512
    mrow = [row(0, 512), row(512, 512)]
    prev_evac = None
    def _wmod_load(ft, c):
        return r_wmod.load(wmod_s[(ft * DC + c) % 4],
                           wmodd[c * 128:(c + 1) * 128, ft * 512:(ft + 1) * 512])
    toks = {c: _wmod_load(0, c) for c in range(4)}
    for ft in range(n_ft):
        for c in range(DC):
            w = [toks.pop(c)]
            if c == 0:
                w += [silu_tok, prev_evac]
            mm(ps[0][0:1, :], siluc[:, c:c + 1], wmod_s[(ft * DC + c) % 4],
               start=(c == 0), stop=(c == DC - 1), wait=w, inc=[(sT, 1)])
            r_wmod.set_free(ft * DC + c, P.tok(sT))
            nc_, nft = c + 4, ft
            if nc_ >= DC:
                nc_, nft = nc_ - DC, ft + 1
            if nft < n_ft:
                toks[nc_] = _wmod_load(nft, nc_)
        slot = mrow[ft % 2]
        w = [P.tok(sT)]
        if ft >= 2:
            w.append((s_sp, 16 * ft))
        P.op("scalar", "copy", slot, ps[0][0:1, :], wait=w, inc=[(sC, 1)])
        prev_evac = P.tok(sC)
        P.op("sync", "dma_start", out=modsc[ft * 512:(ft + 1) * 512], in_=slot,
             wait=[prev_evac], inc=[(s_sp, 16)])
    P.op("sync", "dma_start", out=modT,
         in_=modsc.rearrange("(a p) -> p a", p=128),
         allow_slow_non_contiguous=True, wait=[(s_sp, 16 * n_ft)], inc=[(sG, 16)])
    gm = P.tok(sG)
    ca1, gac, cb1, gbc = der[:, 0:16], der[:, 16:32], der[:, 32:48], der[:, 48:64]
    sa_v, ca_v, ga_v = modT[:, 0:16], modT[:, 16:32], modT[:, 32:48]
    sb_v, cb_v, gb_v = modT[:, 48:64], modT[:, 64:80], modT[:, 80:96]
    P.op("vector", "tensor_scalar_add", ca1, ca_v, 1.0, wait=[gm])
    P.op("vector", "tensor_scalar_add", cb1, cb_v, 1.0)
    for g_in, g_out in ((ga_v, gac), (gb_v, gbc)):
        P.op("vector", "tensor_single_scalar", tmpa, g_in, 0.0, ALU.is_ge)
        P.op("vector", "tensor_scalar", tmpa, tmpa, 2.0, 1.0, ALU.mult, ALU.subtract)
        P.op("vector", "tensor_mul", tmpb, g_in, tmpa)
        P.op("vector", "tensor_scalar_max", tmpb, tmpb, 0.25)
        P.op("vector", "tensor_mul", g_out, tmpb, tmpa)
    P.op("vector", "tensor_copy", wgd_hi, wgd)
    P.op("vector", "tensor_tensor", wgd_lo, wgd, wgd_hi, ALU.subtract)
    P.op("vector", "tensor_scalar", nb0b, b0b_s, -1.0, None, ALU.mult)
    P.op("vector", "tensor_scalar", nb1b, b1b_s, -1.0, None, ALU.mult)
    P.op("vector", "engine_nop", inc=[(sV, 1)])
    mod_tok = P.tok(sV)
    wmod_alldone = P.tok(sT)

    if expose:
        P.op("sync", "dma_start", out=dmp["modT"], in_=modT, wait=[mod_tok],
             inc=[(sst, 16)])
        P.op("sync", "dma_start", out=dmp["der"], in_=der, wait=[mod_tok],
             inc=[(sst, 16)])
    if stage == "mod":
        return _finish(P, nc, es, sst)

    st2 = {"x2free": [None, None], "bc_copy": None, "kevac": None,
           "vevac": None, "bc6_free": None}
    P.label = "ph2"
    NT_ALL = NTILE + Q // TT
    r_xt.free_tok[1] = wmod_alldone
    pend_kv = [None]
    ln2_eps = buf(2576, 1)[0:1, :]
    rms_eps = buf(2584, 1)[0:1, :]
    P.op("vector", "memset", ln2_eps, EPS_LN)
    P.op("vector", "memset", rms_eps, EPS_RMS, inc=[(sV, 1)])
    eps_tok = P.tok(sV)

    def emit_kv(t, x_ap, htok, htoks):
        t0 = t * TT
        wk_t = {c: r_wk.load(wk_s[c % 2], Wkd[c * 128:(c + 1) * 128, :])
                for c in range(2)}
        wv_t = {c: r_wv.load(wv_s[c % 2], Wvd[c * 128:(c + 1) * 128, :])
                for c in range(2)}
        for c in range(DC):
            hc = x_ap[:, c * TT:(c + 1) * TT]
            for kvc in range(4):
                w = []
                if kvc == 0:
                    w = [wk_t[c], htoks[c]]
                    if c == 0:
                        w += [st2["kevac"]]
                mm(ps[kvc][:, 0:TT], wk_s[c % 2][:, kvc * 128:(kvc + 1) * 128],
                   hc, start=(c == 0), stop=(c == DC - 1), wait=w,
                   inc=[(sT, 1)] if kvc == 3 else ())
            r_wk.set_free(t * DC + c, P.tok(sT))
            for tc in range(2):
                w = []
                if tc == 0:
                    w = [wv_t[c]]
                    if c == 0:
                        w += [st2["vevac"], st2["bc_copy"]]
                mm(ps[6 + tc][:, :], hc[:, tc * 128:(tc + 1) * 128], wv_s[c % 2],
                   start=(c == 0), stop=(c == DC - 1), wait=w,
                   inc=[(sT, 1)] if tc == 1 else ())
            r_wv.set_free(t * DC + c, P.tok(sT))
            if c + 2 < DC:
                wk_t[c + 2] = r_wk.load(wk_s[c % 2],
                                        Wkd[(c + 2) * 128:(c + 3) * 128, :])
                wv_t[c + 2] = r_wv.load(wv_s[c % 2],
                                        Wvd[(c + 2) * 128:(c + 3) * 128, :])
        kvtok = P.tok(sT)
        for kvc in range(4):
            P.op("scalar", "copy", Kbuf[:, kvc * S + t0:kvc * S + t0 + TT],
                 ps[kvc][:, 0:TT], wait=[kvtok] if kvc == 0 else [],
                 inc=[(sC, 1)])
        st2["kevac"] = P.tok(sC)
        for tc in range(2):
            kc = t0 // 128 + tc
            P.op("scalar", "copy", Vhi[:, kc * 512:(kc + 1) * 512],
                 ps[6 + tc][:, :], inc=[(sC, 1)])
            P.op("vector", "tensor_tensor", Vlo[:, kc * 512:(kc + 1) * 512],
                 ps[6 + tc][:, :], Vhi[:, kc * 512:(kc + 1) * 512],
                 ALU.subtract, wait=[P.tok(sC)], inc=[(sV, 1)])
        st2["vevac"] = P.tok(sV)
        return kvtok

    TILE_ORDER = [0, 1, 2, 3, NTILE, 4, NTILE + 1, 5, NTILE + 2, 6,
                  NTILE + 3, 7]
    for ti, t in enumerate(TILE_ORDER):
        own = t >= NTILE
        t0 = (t - NTILE) * TT if own else t * TT
        x_ap = xh[ti % 2]
        src_d = xq if own else xfeat
        xtok = r_xt.load(
            x_ap.rearrange("p (c t) -> p c t", t=TT),
            src_d.rearrange("(c p) t -> p c t", p=128)[:, :, t0:t0 + TT])
        for c in range(DC):
            xc = x_ap[:, c * TT:(c + 1) * TT]
            x2 = x2s[c % 2][:, 0:TT]
            P.op("scalar", "activation", x2, xc, AF.Square,
                 wait=[xtok, st2["x2free"][c % 2]], inc=[(sC, 1)])
            x2tok = P.tok(sC)
            w_mean = [xtok, ones_tok] if c == 0 else []
            mm(ps[4][0:1, 0:TT], ones_col, xc, start=(c == 0),
               stop=(c == DC - 1), wait=w_mean)
            mm(ps[5][0:1, 0:TT], ones_col, x2, start=(c == 0),
               stop=(c == DC - 1), wait=[x2tok], inc=[(sT, 1)])
            st2["x2free"][c % 2] = P.tok(sT)
        stats_tok = P.tok(sT)
        if pend_kv[0] is not None:
            pend_kv[0]()
            pend_kv[0] = None
        r_m, r_t = row(1024, TT), row(1280, TT)
        r_rstd, r_shift = row(1536, TT), row(1536 + TT, TT)
        P.op("vector", "tensor_scalar", r_m, ps[4][0:1, 0:TT], 1.0 / D, None,
             ALU.mult, wait=[stats_tok, eps_tok])
        P.op("vector", "tensor_mul", r_t, r_m, r_m)
        P.op("vector", "scalar_tensor_tensor", r_rstd, ps[5][0:1, 0:TT],
             1.0 / D, r_t, ALU.mult, ALU.subtract, inc=[(sV, 1)])
        P.op("scalar", "activation", r_rstd, r_rstd, AF.Ln,
             bias=ln2_eps, wait=[P.tok(sV)])
        P.op("scalar", "drain")
        P.op("scalar", "activation", r_rstd, r_rstd, AF.Exp, scale=-0.5,
             inc=[(sC, 1)])
        P.op("vector", "scalar_tensor_tensor", r_shift, r_m, -1.0, r_rstd,
             ALU.mult, ALU.mult, wait=[P.tok(sC)], inc=[(sV, 1)])
        mm(ps[6][:, 0:2 * TT], ones_row, row(1536, 2 * TT), start=True,
           stop=True, wait=[P.tok(sV), st2["bc_copy"], st2["vevac"],
                            st2.get("h_done")], inc=[(sT, 1)])
        P.op("scalar", "copy", bc[:, 0:2 * TT], ps[6][:, 0:2 * TT],
             wait=[P.tok(sT)], inc=[(sC, 1)])
        st2["bc_copy"] = P.tok(sC)
        bctok = st2["bc_copy"]
        RSTD, SHIFT = bc[:, 0:TT], bc[:, TT:2 * TT]
        htoks = []
        for c in range(DC):
            xc = x_ap[:, c * TT:(c + 1) * TT]
            hd = xc
            w = [bctok]
            if t == 0 and c == 0:
                w.append(mod_tok)
            P.op("scalar", "activation", t1c[c % 2], SHIFT, AF.Identity,
                 scale=ca1[:, c:c + 1], bias=sa_v[:, c:c + 1],
                 wait=[bctok] + ([htoks[c - 2]] if c >= 2 else []),
                 inc=[(sC, 1)])
            c2tok = P.tok(sC)
            P.op("vector", "scalar_tensor_tensor", hd, xc, ca1[:, c:c + 1],
                 RSTD, ALU.mult, ALU.mult, wait=w)
            P.op("vector", "tensor_add", hd, hd, t1c[c % 2], wait=[c2tok],
                 inc=[(sV, 1)])
            if own:
                qt_half = (t0 // 512)
                off = t0 % 512
                hhi = buf(BIG + 65536 + c * 4096 + qt_half * 2048, 512, BF16)[:, off:off + TT]
                hlo = buf(BIG + 65536 + c * 4096 + qt_half * 2048 + 1024, 512, BF16)[:, off:off + TT]
                P.op("vector", "tensor_copy", hhi, xc)
                P.op("vector", "tensor_tensor", hlo, xc, hhi, ALU.subtract,
                     inc=[(sV, 1)])
            htoks.append(P.tok(sV))
        st2["h_done"] = P.tok(sV)
        st2["htoks"] = htoks
        htok = P.tok(sV)
        if own:
            r_xt.set_free(ti, htok)
        else:
            def mk(t=t, ti=ti, x_ap=x_ap, htok=htok,
                   htoks=tuple(st2["htoks"])):
                def go():
                    kvtok = emit_kv(t, x_ap, htok, htoks)
                    r_xt.set_free(ti, kvtok)
                return go
            pend_kv[0] = mk()
    if pend_kv[0] is not None:
        pend_kv[0]()
    hown_tok = P.tok(sV)

    kv_all = st2["vevac"]
    units = [(kvc, half) for kvc in range(4) for half in range(2)]
    prev_stt = hown_tok
    ktmp = wk_s[0]
    for i, (kvc, half) in enumerate(units):
        k2 = k2buf
        P.op("scalar", "activation", k2, Kbuf[:, kvc * S + half * 1024:
             kvc * S + (half + 1) * 1024], AF.Square,
             wait=[kv_all] + ([P.tok(sV)] if i > 0 else [st2["h_done"]]),
             inc=[(sC, 1)])
        k2tok = P.tok(sC)
        for blk in range(2):
            b = half * 2 + blk
            sl = slice(blk * 512, (blk + 1) * 512)
            mm(ps[6][0:1, :], ones_col, k2[:, sl], start=True, stop=True,
               wait=[k2tok, P.tok(sC)], inc=[(sT, 1)])
            r_r = row(1024, 512)
            P.op("scalar", "activation", r_r, ps[6][0:1, :], AF.Ln,
                 scale=1.0 / DH, bias=rms_eps, wait=[P.tok(sT)])
            P.op("scalar", "drain")
            P.op("scalar", "activation", r_r, r_r, AF.Exp, scale=-0.5,
                 inc=[(sC, 1)])
            mm(ps[7][:, :], ones_row, r_r, start=True, stop=True,
               wait=[P.tok(sC), prev_stt], inc=[(sT, 1)])
            P.op("scalar", "copy", bc[:, 1024:1536], ps[7][:, :],
                 wait=[P.tok(sT)], inc=[(sC, 1)])
            prev_bc = P.tok(sC)
            base = kvc * S + half * 1024 + blk * 512
            kc = Kbuf[:, base:base + 512]
            P.op("vector", "scalar_tensor_tensor", ktmp, kc, kg,
                 bc[:, 1024:1536], ALU.mult, ALU.mult, wait=[prev_bc])
            khi = buf(BIG + (kvc * 4 + b) * 2048, 512, BF16)
            klo = buf(BIG + (kvc * 4 + b) * 2048 + 1024, 512, BF16)
            P.op("vector", "tensor_copy", khi, ktmp)
            P.op("vector", "tensor_tensor", klo, ktmp, khi, ALU.subtract,
                 inc=[(sV, 1)])
            prev_stt = P.tok(sV)
    krms_tok = P.tok(sV)

    if expose:
        P.op("sync", "dma_start", out=dmp["H"],
             in_=buf(BIG + 65536, DC * Q), wait=[hown_tok], inc=[(sst, 16)])
    if stage == "ph2":
        return _finish(P, nc, es, sst)

    P.label = "attn"
    P.op("sync", "dma_start", out=masks.rearrange("p (i n) -> p i n", n=512),
         in_=maskd.rearrange("i p n -> p i n"), wait=[krms_tok], inc=[(sG, 16)])
    mask_tok = P.tok(sG)

    st3 = {
        "qfree": [hown_tok, hown_tok],
        "wq_pair_free": krms_tok,
        "avfree": krms_tok,
        "sbank_free": [krms_tok, krms_tok],
        "ptf_free": [krms_tok, krms_tok],
        "pthl_free": [krms_tok, krms_tok, krms_tok],
        "avslot_free": None,
        "avslot_free2": None,
        "bc_user": [krms_tok, krms_tok],
        "ps6_free": krms_tok,
        "qh_toks": [None, None],
        "x2q_free": [krms_tok, krms_tok],
    }

    def emit_stage_a(h):
        wsrc_q = Wqd.rearrange("(c p) e -> p c e", p=128)[:, :, h * 128:(h + 1) * 128]
        wq_tok = r_wq.load(
            wq_s[0].rearrange("p (c e) -> p c e", e=128), wsrc_q,
            extra_wait=[krms_tok] if h < 1 else ())
        wqhi_tok = r_wqhi.load(
            wq_hi[:].rearrange("p (c e) -> p c e", e=128), wsrc_q,
            extra_wait=[krms_tok] if h < 1 else ())
        P.op("vector", "tensor_tensor", wq_lo, wq_s[0], wq_hi, ALU.subtract,
             wait=[wq_tok, wqhi_tok, st3["wq_pair_free"]], inc=[(sV, 1)])
        wq_split = P.tok(sV)
        r_wq.set_free(h, wq_split)
        qmm_toks = []
        for qt in range(2):
            for c in range(DC):
                w = []
                if c == 0:
                    w = [wq_split, st3["qfree"][qt]]
                    if h == 0:
                        w.append(hown_tok)
                whi = wq_hi[:, c * 128:(c + 1) * 128]
                wlo = wq_lo[:, c * 128:(c + 1) * 128]
                mm(ps[4 + qt][:, :], whi, Hap(0, c, qt),
                   start=(c == 0), stop=False, wait=w)
                mm(ps[4 + qt][:, :], whi, Hap(1, c, qt), start=False,
                   stop=False)
                mm(ps[4 + qt][:, :], wlo, Hap(0, c, qt), start=False,
                   stop=(c == DC - 1),
                   inc=[(sT, 1)] if c == DC - 1 else ())
            qmm_toks.append(P.tok(sT))
        r_wqhi.set_free(h, qmm_toks[1])
        st3["wq_pair_free"] = qmm_toks[1]
        sq_toks = []
        for qt in range(2):
            P.op("scalar", "activation", x2q[qt], ps[4 + qt][:, :], AF.Square,
                 wait=[qmm_toks[qt], st3["x2q_free"][qt]], inc=[(sC, 1)])
            sq_toks.append(P.tok(sC))
        return qmm_toks, sq_toks

    def mk_rms_steps(h, qmm_toks, sq_toks):
        state = {}

        def msq(qt):
            def go():
                w = [sq_toks[qt], st3["ps6_free"]]
                mm(ps[6][0:1, :], ones_col, x2q[qt], start=True, stop=True,
                   wait=w, inc=[(sT, 1)])
                state["msq%d" % qt] = P.tok(sT)
                st3["x2q_free"][qt] = state["msq%d" % qt]
                r_r = row(1024 + qt * 512, 512)
                P.op("scalar", "activation", r_r, ps[6][0:1, :], AF.Ln,
                     scale=1.0 / DH, bias=rms_eps, wait=[state["msq%d" % qt]])
                P.op("scalar", "drain")
                P.op("scalar", "activation", r_r, r_r, AF.Exp, scale=-0.5,
                     inc=[(sC, 1)])
                state["ln%d" % qt] = P.tok(sC)
            return go

        def bccp(qt):
            def go():
                mm(ps[6][:, :], ones_row, row(1024 + qt * 512, 512),
                   start=True, stop=True, wait=[state["ln%d" % qt]],
                   inc=[(sT, 1)])
                P.op("scalar", "copy", bc[:, qt * 512:qt * 512 + 512],
                     ps[6][:, :], wait=[P.tok(sT), st3["bc_user"][qt]],
                     inc=[(sC, 1)])
                state["cp%d" % qt] = P.tok(sC)
                st3["ps6_free"] = state["cp%d" % qt]
            return go

        def qhat():
            for qt in range(2):
                qsl = slice(qt * 512, qt * 512 + 512)
                P.op("vector", "scalar_tensor_tensor",
                     ps[4 + qt][:, :], ps[4 + qt][:, :], qg,
                     bc[:, qt * 512:qt * 512 + 512], ALU.mult, ALU.mult,
                     wait=[state["cp%d" % qt]])
                P.op("vector", "tensor_copy", qh_hi[h % 2][:, qsl],
                     ps[4 + qt][:, :])
                P.op("vector", "tensor_tensor", qh_lo[h % 2][:, qsl],
                     ps[4 + qt][:, :], qh_hi[h % 2][:, qsl], ALU.subtract,
                     inc=[(sV, 1)])
                st3["qh_next"][qt] = P.tok(sV)
                st3["bc_user"][qt] = st3["qh_next"][qt]
                st3["qfree"][qt] = st3["qh_next"][qt]
        return [msq(0), bccp(0), msq(1), bccp(1)], qhat

    def emit_chunks(h, qt, inject):
        g = h // 4
        ncH = QT_CHUNKS[qt]
        qsl = slice(qt * 512, qt * 512 + 512)
        qhh = qh_hi[h % 2][:, qsl]
        qhl = qh_lo[h % 2][:, qsl]
        qtok = st3["qh_toks"][qt]
        exp_toks = {}
        plo_toks = {}

        def emit_hi(j):
            w_av = [exp_toks[j]]
            if j == 0:
                w_av.append(st3["avfree"])
            vs = slice(j * 512 + g * 128, j * 512 + (g + 1) * 128)
            mm(ps[2][:, :], Vhi[:, vs], pt_hi[j % 3],
               start=(j == 0), stop=False, wait=w_av)
            mm(ps[2][:, :], Vlo[:, vs], pt_hi[j % 3],
               start=False, stop=False)
            mm(ps[3][0:1, :], ones_bf[:, 0:1], pt_hi[j % 3],
               start=(j == 0), stop=False, inc=[(sT, 1)])

        def emit_lo(j):
            vs = slice(j * 512 + g * 128, j * 512 + (g + 1) * 128)
            mm(ps[2][:, :], Vhi[:, vs], pt_lo[j % 3],
               start=False, stop=(j == ncH - 1), wait=[plo_toks[j]])
            mm(ps[3][0:1, :], ones_bf[:, 0:1], pt_lo[j % 3],
               start=False, stop=(j == ncH - 1), inc=[(sT, 1)])
            st3["pthl_free"][j % 3] = P.tok(sT)

        for c in range(ncH):
            if c >= 2:
                emit_hi(c - 2)
            if c >= 3:
                emit_lo(c - 3)
            if inject and c in inject:
                inject[c]()
            sb = ps[c % 2]
            masked = (qt == 0) or (c >= 8)
            w = [qtok, st3["sbank_free"][c % 2]]
            if h == 0 and qt == 0 and c < 2:
                w.append(mask_tok)
            if masked:
                mm(sb[:, :], iden_bf, masks[:, c * 512:(c + 1) * 512],
                   start=True, stop=False,
                   wait=[st3["sbank_free"][c % 2]] +
                   ([mask_tok, ones_tok] if h == 0 and qt == 0 and c < 2 else []))
            mm(sb[:, :], Khl(0, g, c), qhh, start=not masked, stop=False,
               wait=w)
            mm(sb[:, :], Khl(0, g, c), qhl, start=False, stop=False)
            mm(sb[:, :], Khl(1, g, c), qhh, start=False, stop=True,
               inc=[(sT, 1)])
            smm_tok = P.tok(sT)
            P.op("scalar", "activation", pt_s[c % 2], sb[:, :], AF.Exp,
                 scale=SCALE, wait=[smm_tok, st3["ptf_free"][c % 2]])
            P.op("scalar", "activation", pt_hi[c % 3], sb[:, :], AF.Exp,
                 scale=SCALE, wait=[st3["pthl_free"][c % 3]], inc=[(sC, 1)])
            exp_toks[c] = P.tok(sC)
            st3["sbank_free"][c % 2] = exp_toks[c]
            P.op("vector", "tensor_tensor", pt_lo[c % 3], pt_s[c % 2],
                 pt_hi[c % 3], ALU.subtract, wait=[exp_toks[c]],
                 inc=[(sV, 1)])
            plo_toks[c] = P.tok(sV)
            st3["ptf_free"][c % 2] = plo_toks[c]
        emit_hi(ncH - 2)
        emit_lo(ncH - 3)
        emit_hi(ncH - 1)
        emit_lo(ncH - 2)
        emit_lo(ncH - 1)
        den_tok = P.tok(sT)
        r_d = row(1536, 512)
        P.op("vector", "reciprocal", r_d, ps[3][0:1, :], wait=[den_tok],
             inc=[(sV, 1)])
        mm(ps[7][:, :], ones_row, r_d, start=True, stop=True,
           wait=[P.tok(sV), st3["rdb_free"]], inc=[(sT, 1)])
        P.op("scalar", "copy", bc[:, 1024:1536], ps[7][:, :],
             wait=[P.tok(sT)], inc=[(sC, 1)])
        rdb_tok = P.tok(sC)
        P.op("vector", "tensor_tensor", avn_s[0], ps[2][:, :],
             bc[:, 1024:1536], ALU.mult, wait=[rdb_tok] +
             ([(st_av32, st_av32["n"])] if expose else []))
        P.op("vector", "tensor_copy", avn_hi[0], avn_s[0],
             wait=[st3["avslot_free"], st3.get("avslot_free2")])
        P.op("vector", "tensor_tensor", avn_lo[0], avn_s[0], avn_hi[0],
             ALU.subtract, inc=[(sV, 1)])
        st3["avfree"] = P.tok(sV)
        st3["rdb_free"] = st3["avfree"]
        P.op("sync", "dma_start",
             out=avhs[h * 128:(h + 1) * 128, qt * 512:qt * 512 + 512],
             in_=avn_hi[0], wait=[st3["avfree"]], inc=[(st_avn[0], 16)])
        P.op("sync", "dma_start",
             out=avls[h * 128:(h + 1) * 128, qt * 512:qt * 512 + 512],
             in_=avn_lo[0], wait=[st3["avfree"]], inc=[(st_avn[1], 16)])
        if expose:
            P.op("sync", "dma_start",
                 out=avsc[h * 128:(h + 1) * 128, qt * 512:qt * 512 + 512],
                 in_=avn_s[0], wait=[st3["avfree"]], inc=[(st_av32, 16)])
        st3["avslot_free"] = (st_avn[0], st_avn[0]["n"])
        st3["avslot_free2"] = (st_avn[1], st_avn[1]["n"])

    st3["rdb_free"] = krms_tok
    st3["qh_next"] = [None, None]
    qmm0, sq0 = emit_stage_a(0)
    steps0, qhat0 = mk_rms_steps(0, qmm0, sq0)
    for s in steps0:
        s()
    qhat0()
    st3["qh_toks"] = list(st3["qh_next"])
    for h in range(NH):
        nxt = None
        if h + 1 < NH:
            qmms, sqs = emit_stage_a(h + 1)
            steps, qhat_n = mk_rms_steps(h + 1, qmms, sqs)
            nxt = {4: steps[0], 8: steps[1], 11: steps[2], 14: steps[3]}
        emit_chunks(h, 0, None)
        emit_chunks(h, 1, nxt)
        if h + 1 < NH:
            qhat_n()
            st3["qh_toks"] = list(st3["qh_next"])
    avfree = st3["avfree"]
    qfree = st3["qfree"]
    msq_free = [st3["ps6_free"], st3["rdb_free"]]

    if stage == "attn":
        return _finish(P, nc, es, sst)


    P.label = "wo"
    av_total = [(st_avn[0], st_avn[0]["n"]), (st_avn[1], st_avn[1]["n"])]
    wofree = [avfree, avfree]
    wo_split_prev = [avfree, avfree]
    wo_pair_free = [avfree, avfree]
    x1slot_free = [(), ()]
    stats_user = [qfree[0], qfree[1]]
    ps6_user = msq_free[0]
    ps7_user = msq_free[1]
    h2_toks = [None, None]
    ln2rows_tok = [None, None]
    x1q_tokens = [None, None]
    x1_uses = [0]
    h2_last_v = [None]
    h2_cp = [None, None]

    def h2_prologue(qt, bc_gate):
        r_mean = row(1024 + qt * 512, 512)
        r_rstd = row(2048 + qt * 512, 512)
        s0 = row(0, 512)
        P.op("vector", "tensor_scalar", s0, r_mean, -1.0, None, ALU.mult,
             wait=[ln2rows_tok[qt]])
        P.op("vector", "tensor_mul", s0, s0, r_rstd, inc=[(sV, 1)])
        sh_tok = P.tok(sV)
        mm(ps[6][:, :], ones_row, r_rstd, start=True, stop=True,
           wait=[ln2rows_tok[qt], bc_gate[0]], inc=[(sT, 1)])
        mm(ps[7][:, :], ones_row, s0, start=True, stop=True,
           wait=[sh_tok, bc_gate[1]], inc=[(sT, 1)])
        P.op("scalar", "copy", bc[:, 0:512], ps[6][:, :],
             wait=[(sT, P.tok(sT)[1] - 1)])
        P.op("scalar", "copy", bc[:, 512:1024], ps[7][:, :],
             wait=[P.tok(sT)], inc=[(sC, 1)])
        return P.tok(sC)

    def h2_chunk(qt, c, bc2tok):
        x1tok = r_x1.load(
            x1rd3_s[c % 2],
            x1sc[c * 128:(c + 1) * 128, qt * 512:qt * 512 + 512],
            extra_wait=x1q_tokens[qt] if c < 2 else ())
        pt = ps[6 + c % 2]
        P.op("vector", "scalar_tensor_tensor", pt[:, :], x1rd3_s[c % 2],
             cb1[:, c:c + 1], bc[:, 0:512], ALU.mult, ALU.mult,
             wait=[x1tok, bc2tok, h2_cp[c % 2]])
        P.op("scalar", "activation", t1c2[0], bc[:, 512:1024], AF.Identity,
             scale=cb1[:, c:c + 1], bias=sb_v[:, c:c + 1],
             wait=[bc2tok] + ([h2_last_v[0]] if h2_last_v[0] else []),
             inc=[(sC, 1)])
        P.op("vector", "tensor_add", pt[:, :], pt[:, :], t1c2[0],
             wait=[P.tok(sC)], inc=[(sV, 1)])
        P.op("scalar", "copy", H2ap(0, c, qt), pt[:, :],
             wait=[P.tok(sV)], inc=[(sC, 1)])
        h2_cp[c % 2] = P.tok(sC)
        P.op("vector", "tensor_tensor", H2ap(1, c, qt), pt[:, :],
             H2ap(0, c, qt), ALU.subtract, wait=[P.tok(sC)], inc=[(sV, 1)])
        h2_last_v[0] = P.tok(sV)
        x1_uses[0] += 1
        r_x1.set_free(x1_uses[0] - 1, h2_last_v[0])

    h2q0 = [None]
    wo_prev_qt = None
    for qt in range(2):
        avq_toks = []
        for c2 in range(16):
            w = list(av_total) if (qt == 0 and c2 == 0) else []
            if qt == 1 and c2 == 0:
                w = [wo_prev_qt]
            P.op("gpsimd", "dma_start",
                 out=avqh[:, c2 * 512:(c2 + 1) * 512],
                 in_=avhs[c2 * 128:(c2 + 1) * 128, qt * 512:qt * 512 + 512],
                 wait=w, inc=[(avql[c2], 16)])
            P.op("gpsimd", "dma_start",
                 out=avql_[:, c2 * 512:(c2 + 1) * 512],
                 in_=avls[c2 * 128:(c2 + 1) * 128, qt * 512:qt * 512 + 512],
                 inc=[(avql[c2], 16)])
            avq_toks.append((avql[c2], 32 * (qt + 1)))
        last_mm = None
        for do in range(DC):
            wsrc_o = Wod.rearrange("(c p) e -> p c e", p=128)[:, :, do * 128:(do + 1) * 128]
            wo_tok = r_wo.load(
                wo_s[do % 2].rearrange("p (c e) -> p c e", e=128), wsrc_o,
                extra_wait=[wo_split_prev[do % 2]])
            wohi_tok = r_wohi.load(
                wo_hi[do % 2][:].rearrange("p (c e) -> p c e", e=128), wsrc_o,
                extra_wait=[wo_pair_free[do % 2]])
            P.op("vector", "tensor_tensor", wo_lo[do % 2], wo_s[do % 2],
                 wo_hi[do % 2], ALU.subtract, wait=[wo_tok, wohi_tok],
                 inc=[(sV, 1)])
            wo_split_prev[do % 2] = P.tok(sV)
            for c in range(DC):
                w = [avq_toks[c]] if do == 0 else []
                if c == 0:
                    w += [wo_split_prev[do % 2], wofree[do % 2]]
                whi = wo_hi[do % 2][:, c * 128:(c + 1) * 128]
                wlo = wo_lo[do % 2][:, c * 128:(c + 1) * 128]
                ah = avqh[:, c * 512:(c + 1) * 512]
                al = avql_[:, c * 512:(c + 1) * 512]
                mm(ps[do % 2][:, :], whi, ah, start=(c == 0), stop=False,
                   wait=w)
                mm(ps[do % 2][:, :], whi, al, start=False, stop=False)
                mm(ps[do % 2][:, :], wlo, ah, start=False, stop=(c == DC - 1),
                   inc=[(sT, 1)] if c == DC - 1 else ())
            womm_tok = P.tok(sT)
            last_mm = womm_tok
            r_wo.set_free(qt * DC + do, womm_tok)
            r_wohi.set_free(qt * DC + do, womm_tok)
            wo_pair_free[do % 2] = womm_tok
            xq_tok = r_xq.load(
                xq_s[do % 2],
                xq[do * 128:(do + 1) * 128, qt * 512:qt * 512 + 512])
            sl = do % 2
            x1sl = x1st_s[sl]
            P.op("vector", "tensor_tensor", x1sl, ps[do % 2][:, :],
                 Hap(0, do, qt), ALU.add,
                 wait=[womm_tok] + list(x1slot_free[sl]))
            P.op("vector", "tensor_tensor", x1sl, x1sl, Hap(1, do, qt),
                 ALU.add)
            P.op("vector", "scalar_tensor_tensor", x1sl, x1sl,
                 gac[:, do:do + 1], xq_s[sl], ALU.mult, ALU.add,
                 wait=[xq_tok], inc=[(sV, 1)])
            x1tok = P.tok(sV)
            r_xq.set_free(qt * DC + do, x1tok)
            wofree[do % 2] = x1tok
            P.op("scalar", "activation", x2w_s[sl], x1sl, AF.Square,
                 wait=[x1tok], inc=[(sC, 1)])
            x2wtok = P.tok(sC)
            w_m = [x1tok]
            if do == 0:
                w_m += stats_user
            mm(ps[4][0:1, :], ones_col, x1sl, start=(do == 0),
               stop=(do == DC - 1), wait=w_m)
            mm(ps[5][0:1, :], ones_col, x2w_s[sl], start=(do == 0),
               stop=(do == DC - 1), wait=[x2wtok], inc=[(sT, 1)])
            stats_tok = P.tok(sT)
            P.op("scalar", "dma_start",
                 out=x1sc[do * 128:(do + 1) * 128, qt * 512:qt * 512 + 512],
                 in_=x1sl, wait=[x1tok], inc=[(st_x1[sl], 16)])
            x1slot_free[sl] = ((st_x1[sl], st_x1[sl]["n"]), stats_tok)
            if qt == 1 and h2q0[0] is not None:
                h2_chunk(0, do, h2q0[0])
        wo_prev_qt = last_mm
        x1q_tokens[qt] = [(st_x1[0], st_x1[0]["n"]), (st_x1[1], st_x1[1]["n"])]
        r_mean = row(1024 + qt * 512, 512)
        r_rstd = row(2048 + qt * 512, 512)
        s1 = row(512, 512)
        P.op("vector", "tensor_scalar", r_mean, ps[4][0:1, :], 1.0 / D, None,
             ALU.mult, wait=[P.tok(sT)])
        P.op("vector", "tensor_mul", s1, r_mean, r_mean)
        P.op("vector", "scalar_tensor_tensor", r_rstd, ps[5][0:1, :], 1.0 / D,
             s1, ALU.mult, ALU.subtract, inc=[(sV, 1)])
        P.op("scalar", "activation", r_rstd, r_rstd, AF.Ln, bias=ln2_eps,
             wait=[P.tok(sV)])
        P.op("scalar", "drain")
        P.op("scalar", "activation", r_rstd, r_rstd, AF.Exp, scale=-0.5,
             inc=[(sC, 1)])
        P.op("vector", "engine_nop", wait=[P.tok(sC)], inc=[(sV, 1)])
        ln2rows_tok[qt] = P.tok(sV)
        stats_user = [P.tok(sV)]
        if qt == 0:
            h2q0[0] = h2_prologue(0, (ps6_user, ps7_user))
    t_3b_end = P.tok(sT)
    if stage == "wo":
        return _finish(P, nc, es, sst)

    P.label = "h2c"
    x1_total = [(st_x1[0], st_x1[0]["n"]), (st_x1[1], st_x1[1]["n"])]
    h2_toks[0] = h2_last_v[0]
    bcg = (h2_last_v[0], h2_last_v[0])
    bc2tok = h2_prologue(1, bcg)
    for c in range(DC):
        h2_chunk(1, c, bc2tok)
    h2_toks[1] = h2_last_v[0]
    h2_tok = h2_last_v[0]
    if expose:
        P.op("sync", "dma_start", out=dmp["H2"],
             in_=buf(BIG + 65536, DC * Q), wait=[h2_tok], inc=[(sst, 16)])
    if stage == "h2":
        return _finish(P, nc, es, sst)

    P.label = "ffn"
    gate_user = h2_tok
    hidfree = [h2_tok, h2_tok]
    outfree2 = [h2_tok, h2_tok, h2_tok, h2_tok]
    o0slot_free = [None, None]
    hidden_gate = h2_tok
    regA_gate = x1_total + [t_3b_end]
    regB_gate = [t_3b_end]
    x1_uses = 2 * DC
    wa_split_prev = h2_tok
    wa_hilo_free = [h2_tok, h2_tok]
    ps67_free = [h2_tok, h2_tok]
    wb_split_prev = h2_tok
    wb_hilo_free = [h2_tok, h2_tok]
    for qt in range(2):
        for c in range(DC):
            w = []
            if c == 0:
                w = [h2_toks[qt], gate_user, mod_tok]
            h2h = H2ap(0, c, qt)
            h2l = H2ap(1, c, qt)
            mm(ps[4][0:1, :], wgd_hi[:, c:c + 1], h2h,
               start=(c == 0), stop=False, wait=w)
            mm(ps[4][0:1, :], wgd_hi[:, c:c + 1], h2l, start=False, stop=False)
            mm(ps[4][0:1, :], wgd_lo[:, c:c + 1], h2h, start=False,
               stop=(c == DC - 1), inc=[(sT, 1)] if c == DC - 1 else ())
        r_g = row(0, 512)
        P.op("vector", "tensor_single_scalar", r_g, ps[4][0:1, :], 0.0,
             ALU.is_ge, wait=[P.tok(sT)], inc=[(sV, 1)])
        gate_user = P.tok(sV)
        mm(ps[5][:, :], ones_row, r_g, start=True, stop=True,
           wait=[P.tok(sV)] + (list(outfree2) if qt == 1 else []),
           inc=[(sT, 1)])
        mb_tok = P.tok(sT)
        Mb = ps[5]
        for e in range(2):
            WaT = W0aTd if e == 0 else W1aTd
            WbT = W0bTd if e == 0 else W1bTd
            ba = b0a_s if e == 0 else b1a_s
            bb_ = b0b_s if e == 0 else b1b_s
            actf = AF.Gelu if e == 0 else AF.Relu
            hid_toks = []
            hmm_toks = {}
            split_toks = {}
            for f in range(FC):
                ew = list(regA_gate) if f < 2 else []
                sl2 = f % 2
                wsrc = WaT.rearrange("(c p) f -> p c f", p=128)[:, :, f * 128:(f + 1) * 128]
                wa_tok = r_w0a.load(
                    w0a_f[sl2].rearrange("p (c e) -> p c e", e=128), wsrc,
                    extra_wait=ew)
                hi_tok = r_wahi.load(
                    w0a_hi[sl2].rearrange("p (c e) -> p c e", e=128), wsrc,
                    extra_wait=(ew if f < 2 else []) + [wa_hilo_free[sl2]])
                P.op("vector", "tensor_tensor", w0a_lo[sl2], w0a_f[sl2],
                     w0a_hi[sl2], ALU.subtract,
                     wait=[wa_tok, hi_tok] + (list(regA_gate) if f < 2 else []),
                     inc=[(sV, 1)])
                split_toks[f] = P.tok(sV)
                split_tok = split_toks[f]
                r_w0a.set_free((qt * 2 + e) * FC + f, split_tok)
                for c in range(DC):
                    w = []
                    if c == 0:
                        w = [split_tok, hidfree[sl2]]
                        if f == 0:
                            w.append(hidden_gate)
                    h2h = H2ap(0, c, qt)
                    h2l = H2ap(1, c, qt)
                    whi = w0a_hi[sl2][:, c * 128:(c + 1) * 128]
                    wlo = w0a_lo[sl2][:, c * 128:(c + 1) * 128]
                    mm(ps[sl2][:, :], whi, h2h, start=(c == 0), stop=False,
                       wait=w)
                    mm(ps[sl2][:, :], whi, h2l, start=False, stop=False)
                    mm(ps[sl2][:, :], wlo, h2h, start=False,
                       stop=(c == DC - 1),
                       inc=[(sT, 1)] if c == DC - 1 else ())
                hmm_tok = P.tok(sT)
                hmm_toks[f] = hmm_tok
                wa_hilo_free[sl2] = hmm_tok
                P.op("scalar", "activation", hid_hi[:, f * 512:(f + 1) * 512],
                     ps[sl2][:, :], actf, bias=ba[:, f:f + 1],
                     wait=[hmm_tok, ps67_free[sl2]])
                P.op("scalar", "activation", ps[6 + sl2][:, :], ps[sl2][:, :],
                     actf, bias=ba[:, f:f + 1],
                     wait=[outfree2[2 + sl2]] if f < 2 else (), inc=[(sC, 1)])
                P.op("vector", "tensor_tensor", hid_lo[:, f * 512:(f + 1) * 512],
                     ps[6 + sl2][:, :], hid_hi[:, f * 512:(f + 1) * 512],
                     ALU.subtract, wait=[P.tok(sC)], inc=[(sV, 1)])
                hid_toks.append(P.tok(sV))
                ps67_free[sl2] = hid_toks[-1]
                hidfree[sl2] = hid_toks[-1]
            hid_all = hid_toks[-1]
            outfree2[2] = ps67_free[0]
            outfree2[3] = ps67_free[1]
            gA_f32 = split_toks[FC - 2]
            gA_hilo = hmm_toks[FC - 2]
            for do in range(DC):
                ob = [2, 3, 6, 7][do % 4]
                omm_tok = None
                for fh in range(2):
                    ew = list(regB_gate) if (qt == 0 and e == 0 and do == 0) else []
                    wsrc = (WbT.rearrange("(c p) e -> p c e", p=128)
                            [:, fh * 16:(fh + 1) * 16, do * 128:(do + 1) * 128])
                    wb_tok = r_w0b.load(
                        w0b_f.rearrange("p (c e) -> p c e", e=128), wsrc,
                        extra_wait=ew + [wb_split_prev])
                    hi_tok = r_wbhi.load(
                        w0b_hi[fh][:].rearrange("p (c e) -> p c e", e=128), wsrc,
                        extra_wait=ew + [wb_hilo_free[fh]])
                    P.op("vector", "tensor_tensor", w0b_lo[fh], w0b_f,
                         w0b_hi[fh], ALU.subtract, wait=[wb_tok, hi_tok] + ew,
                         inc=[(sV, 1)])
                    wb_split_prev = P.tok(sV)
                    sb_tok = wb_split_prev
                    for c in range(16):
                        fidx = fh * 16 + c
                        w = []
                        if c == 0:
                            w = [sb_tok, hid_all]
                            if fh == 0:
                                w.append(outfree2[do % 4])
                        whi = w0b_hi[fh][:, c * 128:(c + 1) * 128]
                        wlo = w0b_lo[fh][:, c * 128:(c + 1) * 128]
                        hh = hid_hi[:, fidx * 512:(fidx + 1) * 512]
                        hl = hid_lo[:, fidx * 512:(fidx + 1) * 512]
                        mm(ps[ob][:, :], whi, hh,
                           start=(fidx == 0), stop=False, wait=w)
                        mm(ps[ob][:, :], whi, hl, start=False,
                           stop=False)
                        mm(ps[ob][:, :], wlo, hh, start=False,
                           stop=(fidx == FC - 1),
                           inc=[(sT, 1)] if c == 15 else ())
                    wb_hilo_free[fh] = P.tok(sT)
                    r_w0b.set_free((((qt * 2 + e) * DC) + do) * 2 + fh, P.tok(sT))
                omm_tok = P.tok(sT)
                sl = do % 2
                if e == 0:
                    P.op("vector", "tensor_scalar", o0st_s[sl], ps[ob][:, :],
                         bb_[:, do:do + 1], None, ALU.add,
                         wait=[omm_tok, o0slot_free[sl], gA_f32],
                         inc=[(sV, 1)])
                    ctok = P.tok(sV)
                    outfree2[do % 4] = ctok
                    P.op("scalar", "dma_start",
                         out=o0sc[do * 128:(do + 1) * 128, qt * 512:qt * 512 + 512],
                         in_=o0st_s[sl], wait=[ctok], inc=[(st_o0[sl], 16)])
                    o0slot_free[sl] = (st_o0[sl], st_o0[sl]["n"])
                else:
                    o0_tok = r_o0.load(
                        o0rd_s[sl],
                        o0sc[do * 128:(do + 1) * 128, qt * 512:qt * 512 + 512],
                        extra_wait=([(st_o0[0], st_o0[0]["n"]),
                                     (st_o0[1], st_o0[1]["n"]), gA_hilo]
                                    if do < 2 else ()))
                    x1tok = r_x1.load(
                        x1rd_s[sl],
                        x1sc[do * 128:(do + 1) * 128, qt * 512:qt * 512 + 512],
                        extra_wait=[gA_f32] if do < 2 else ())
                    x1_uses += 1
                    t = o0rd_s[sl]
                    nb = nb0b if e == 10 else nb1b
                    P.op("scalar", "activation", t, t, AF.Identity,
                         bias=nb[:, do:do + 1],
                         wait=[o0_tok, omm_tok, mb_tok], inc=[(sC, 1)])
                    P.op("vector", "tensor_tensor", t, t, ps[ob][:, :],
                         ALU.subtract, wait=[P.tok(sC)])
                    P.op("vector", "tensor_mul", t, t, Mb[:, :])
                    P.op("vector", "tensor_scalar", t, t, bb_[:, do:do + 1],
                         None, ALU.add)
                    P.op("vector", "tensor_tensor", t, t, ps[ob][:, :],
                         ALU.add)
                    P.op("vector", "scalar_tensor_tensor", o0st_s[sl], t,
                         gbc[:, do:do + 1], x1rd_s[sl], ALU.mult, ALU.add,
                         wait=[x1tok, o0slot_free[sl]] +
                         ([gA_f32] if do < 2 else []), inc=[(sV, 1)])
                    ytok = P.tok(sV)
                    outfree2[do % 4] = ytok
                    r_o0.set_free(qt * DC + do, ytok)
                    r_x1.set_free(x1_uses - 1, ytok)
                    P.op("scalar", "dma_start",
                         out=ydev[do * 128:(do + 1) * 128, qt * 512:qt * 512 + 512],
                         in_=o0st_s[sl], wait=[ytok], inc=[(yst[sl], 16)])
                    o0slot_free[sl] = (yst[sl], yst[sl]["n"])
            hidden_gate = P.tok(sT)
            regA_gate = [outfree2[0], outfree2[1],
                         (st_o0[0], st_o0[0]["n"]), (st_o0[1], st_o0[1]["n"]),
                         (yst[0], yst[0]["n"]), (yst[1], yst[1]["n"])]

    return _finish(P, nc, es, sst, extra=yst)




def _finish(P, nc, es, sst, extra=()):
    finals = [(sst, sst["n"])] + [(s, s["n"]) for s in extra]
    with nc.Block() as blk:
        @blk.sync
        def _(E):
            for fn in P.q["sync"]:
                fn(E)
            for s, v in finals:
                if v:
                    E.wait_ge(s["h"], v)

        @blk.gpsimd
        def _(E):
            for fn in P.q["gpsimd"]:
                fn(E)

        @blk.vector
        def _(E):
            for fn in P.q["vector"]:
                fn(E)

        @blk.scalar
        def _(E):
            for fn in P.q["scalar"]:
                fn(E)

        @blk.tensor
        def _(E):
            for fn in P.q["tensor"]:
                fn(E)
    es.close()
    return nc



def _own_rows(cls):
    if cls == 0:
        return np.r_[0:512, 1536:2048]
    return np.r_[512:1536]


def _mk_masks(cls):
    q_bases = (0, 1536) if cls == 0 else (512, 1024)
    kp = np.arange(128).reshape(128, 1)
    qf = np.arange(512).reshape(1, 512)
    m = np.zeros((16, 128, 512), np.float32)
    for c in range(16):
        qb = q_bases[0] if c < 8 else q_bases[1]
        m[c] = np.where(128 * c + kp <= qb + qf, 0.0, NEG)
    return np.ascontiguousarray(m.astype(ml_dtypes.bfloat16))


def kernel(L0_attn_x, L0_attn_Wq, L0_attn_Wk, L0_attn_Wv, L0_attn_Wo,
           L0_attn_q_gamma, L0_attn_k_gamma, L0_ffn_Wg, L0_ffn_W0a, L0_ffn_b0a,
           L0_ffn_W0b, L0_ffn_b0b, L0_ffn_W1a, L0_ffn_b1a, L0_ffn_W1b,
           L0_ffn_b1b, L0_c, L0_wmod):
    x = np.ascontiguousarray(np.asarray(L0_attn_x, dtype=np.float32))
    shared = dict(
        Wq=np.ascontiguousarray(L0_attn_Wq, np.float32),
        Wk=np.ascontiguousarray(L0_attn_Wk, np.float32),
        Wv=np.ascontiguousarray(L0_attn_Wv, np.float32),
        Wo=np.ascontiguousarray(L0_attn_Wo, np.float32),
        qg=np.ascontiguousarray(np.asarray(L0_attn_q_gamma, np.float32).reshape(128, 1)),
        kg=np.ascontiguousarray(np.asarray(L0_attn_k_gamma, np.float32).reshape(128, 1)),
        wg=np.ascontiguousarray(L0_ffn_Wg, np.float32),
        W0aT=np.ascontiguousarray(np.asarray(L0_ffn_W0a, np.float32).T),
        W0bT=np.ascontiguousarray(np.asarray(L0_ffn_W0b, np.float32).T),
        W1aT=np.ascontiguousarray(np.asarray(L0_ffn_W1a, np.float32).T),
        W1bT=np.ascontiguousarray(np.asarray(L0_ffn_W1b, np.float32).T),
        b0af=np.ascontiguousarray(np.asarray(L0_ffn_b0a, np.float32).reshape(FC, 128).T),
        b1af=np.ascontiguousarray(np.asarray(L0_ffn_b1a, np.float32).reshape(FC, 128).T),
        b0bf=np.ascontiguousarray(np.asarray(L0_ffn_b0b, np.float32).reshape(DC, 128).T),
        b1bf=np.ascontiguousarray(np.asarray(L0_ffn_b1b, np.float32).reshape(DC, 128).T),
        wmod=np.ascontiguousarray(L0_wmod, np.float32),
        iden=np.ascontiguousarray(np.eye(128).astype(ml_dtypes.bfloat16)),
    )
    cin = np.asarray(L0_c, np.float32)
    masks_by_cls = [_mk_masks(0), _mk_masks(1)]
    in_maps = []
    for core in range(8):
        b, cls = core // 2, core % 2
        xf = np.ascontiguousarray(x[b].T)
        m = dict(shared)
        m["xfeat"] = xf
        m["xq"] = np.ascontiguousarray(xf[:, _own_rows(cls)])
        m["cf"] = np.ascontiguousarray(cin[b].reshape(16, 128).T)
        m["maskt"] = masks_by_cls[cls]
        in_maps.append(m)

    nc = build_program("full")
    res = run_bass_kernel_spmd(nc, in_maps, core_ids=list(range(8)))
    out = np.empty((4, 2048, 2048), np.float32)
    for core in range(8):
        b, cls = core // 2, core % 2
        out[b][_own_rows(cls)] = res.results[core]["y"].T
    return out
